# version 01
# baseline (speedup 1.0000x reference)
import numpy as np
import ml_dtypes
import concourse.bass as bass
import concourse.mybir as mybir
from concourse.bass_utils import run_bass_kernel_spmd

F32 = mybir.dt.float32
BF16 = mybir.dt.bfloat16
OP = mybir.AluOpType
AF = mybir.ActivationFunctionType
AX = mybir.AxisListType

B, L, D = 2, 8192, 1024
DIN = 2048
NST = 16
DTR = 64
FFI = 4096
NE = 8
EI = 2688
TOK = 2048
CH = 512
EPS = 1e-5
GROUPS = [[0, 1, 2, 3], [4, 5, 6, 7]]

DEBUG = False


class Chan:

    def __init__(self, nc, name):
        self.h = nc.alloc_semaphore(name)
        self.n = 0

    def tick(self, instr, amt=1):
        instr.then_inc(self.h, amt)
        self.n += amt
        return self.n

    def wait(self, eng, n=None):
        eng.wait_ge(self.h, self.n if n is None else n)


def build_nc(debug=DEBUG):
    nc = bass.Bass("TRN2", target_bir_lowering=False)

    def din(name, shape, dt=F32):
        return nc.dram_tensor(name, shape, dt, kind="ExternalInput").ap()

    def dout(name, shape, dt=F32):
        return nc.dram_tensor(name, shape, dt, kind="ExternalOutput").ap()

    x_sl = din("x_sl", [D, TOK])
    ln1g = din("ln1g", [128, 8])
    ln1b = din("ln1b", [128, 8])
    ln2g = din("ln2g", [128, 8])
    ln2b = din("ln2b", [128, 8])
    w_in_xc = din("w_in_xc", [D, CH])
    w_in_z = din("w_in_z", [D, CH])
    conv_w = din("conv_w", [128, 16])
    conv_b = din("conv_b", [128, 4])
    w_x_sl = din("w_x_sl", [CH, 96])
    w_dt_sl = din("w_dt_sl", [DTR, CH])
    b_dt = din("b_dt", [128, 4])
    a_log = din("a_log", [128, 64])
    d_skip = din("d_skip", [128, 4])
    w_out_sl = din("w_out_sl", [CH, D])
    w_ff1 = din("w_ff1", [D, 2 * FFI])
    b_ff1 = din("b_ff1", [128, 64])
    w_ff2 = din("w_ff2", [FFI, D])
    b_ff2 = din("b_ff2", [128, 8])
    rtw1 = din("rtw1", [D, 58])
    b1row = din("b1row", [1, 50])
    aw2 = din("aw2", [50, 6])
    b2row = din("b2row", [1, 6])
    gate_wT = din("gate_wT", [NE, D, EI])
    up_wT = din("up_wT", [NE, D, EI])
    down_wT = din("down_wT", [NE, EI, D])
    ident = din("ident", [128, 128])
    iota6 = din("iota6", [1, 6])
    rev8 = din("rev8", [1, 8])
    ones_r = din("ones_r", [1, 128])
    ones_c = din("ones_c", [128, 1])

    out_fm = dout("out_fm", [D, TOK])

    dbg = {}
    if debug:
        dbg["xn"] = dout("dbg_xn", [D, TOK])
        dbg["h"] = dout("dbg_h", [D, TOK], BF16)
        dbg["probs"] = dout("dbg_probs", [TOK, 8])
        dbg["kidx"] = dout("dbg_kidx", [TOK, 1])
        dbg["scores"] = dout("dbg_scores", [TOK, 8])
        dbg["hfull"] = dout("dbg_hfull", [4 * D, TOK], BF16)
        dbg["rtraw"] = dout("dbg_rtraw", [TOK, 58])
        dbg["ahT"] = dout("dbg_ahT", [50, 128])
        dbg["praw"] = dout("dbg_praw", [TOK, 6])
        dbg["st1"] = dout("dbg_st1", [128, 6])
        dbg["st2"] = dout("dbg_st2", [128, 8])
        dbg["st3"] = dout("dbg_st3", [128, 8])
        dbg["st4"] = dout("dbg_st4", [128, 1])
        dbg["xcs"] = dout("dbg_xcs", [CH, L], BF16)
        dbg["sz"] = dout("dbg_sz", [CH, L], BF16)
        dbg["dbc"] = dout("dbg_dbc", [96, L])
        dbg["dt"] = dout("dbg_dt", [CH, L], BF16)
        dbg["ys"] = dout("dbg_ys", [CH, L])
        dbg["mamba"] = dout("dbg_mamba", [D, TOK])
        dbg["attn"] = dout("dbg_attn", [D, TOK])
        dbg["moe"] = dout("dbg_moe", [D, TOK])
        dbg["alog2"] = dout("dbg_alog2", [TOK, 6])

    ag_in = nc.dram_tensor("ag_in", [D, TOK], BF16).ap()
    ag_out = nc.dram_tensor("ag_out", [4 * D, TOK], BF16).ap()
    ar_in = nc.dram_tensor("ar_in", [96, L], F32).ap()
    ar_out = nc.dram_tensor("ar_out", [96, L], F32).ap()
    rs_in = nc.dram_tensor("rs_in", [4 * D, TOK], F32).ap()
    rs_out = nc.dram_tensor("rs_out", [D, TOK], F32).ap()
    xn_dram = nc.dram_tensor("xn_dram", [D, TOK], F32).ap()
    xcs_dram = nc.dram_tensor("xcs_dram", [CH, L], BF16).ap()
    sz_dram = nc.dram_tensor("sz_dram", [CH, L], BF16).ap()
    sc_dram = nc.dram_tensor("sc_dram", [NE, TOK], F32).ap()
    moe_dram = nc.dram_tensor("moe_dram", [D, TOK], F32).ap()
    ysd_dram = nc.dram_tensor("ysd_dram", [CH, L], F32).ap()
    dtd_dram = nc.dram_tensor("dtd_dram", [CH, L], BF16).ap()

    sb = nc.alloc_sbuf_tensor

    c_ln1g = sb("c_ln1g", [128, 8], F32)
    c_ln1b = sb("c_ln1b", [128, 8], F32)
    c_ln2g = sb("c_ln2g", [128, 8], F32)
    c_ln2b = sb("c_ln2b", [128, 8], F32)
    c_convw = sb("c_convw", [128, 16], F32)
    c_convb = sb("c_convb", [128, 4], F32)
    c_bdt = sb("c_bdt", [128, 4], F32)
    c_alog = sb("c_alog", [128, 64], F32)
    c_dskip = sb("c_dskip", [128, 4], F32)
    c_ident = sb("c_ident", [128, 128], F32)
    c_iota6 = sb("c_iota6", [128, 6], F32)
    c_rev8 = sb("c_rev8", [128, 8], F32)
    c_ones = sb("c_ones", [1, 128], F32)
    c_onec = sb("c_onec", [128, 1], F32)
    c_b1row = sb("c_b1row", [1, 50], F32)
    c_b2row = sb("c_b2row", [1, 6], F32)
    c_bff1 = sb("c_bff1", [128, 64], F32)
    c_bff2 = sb("c_bff2", [128, 8], F32)
    c_acol = sb("c_acol", [128, 64], F32)
    c_zero = sb("c_zero", [128, 512], F32)

    w_rtw1 = sb("w_rtw1", [128, 8, 58], F32)
    w_aw2 = sb("w_aw2", [50, 6], F32)
    wb_inxc = sb("wb_inxc", [128, 8, CH], BF16)
    wb_inz = sb("wb_inz", [128, 8, CH], BF16)
    wb_x = sb("wb_x", [128, 4, 96], BF16)
    wb_dt = sb("wb_dt", [DTR, CH], BF16)
    wb_out = sb("wb_out", [128, 4, D], BF16)

    t_xnb = sb("t_xnb", [128, 8, TOK], BF16)
    r_mu = sb("r_mu", [1, 512], F32)
    r_m2 = sb("r_m2", [1, 512], F32)
    r_t = sb("r_t", [1, 512], F32)
    r_sd = sb("r_sd", [1, 512], F32)
    rt_probs = sb("rt_probs", [128, 8], F32)
    rt_ah = sb("rt_ah", [128, 50], F32)
    rt_ahT = sb("rt_ahT", [50, 128], F32)
    rt_lg = sb("rt_lg", [128, 8], F32)
    rt_col = sb("rt_col", [128, 1], F32)
    rt_col2 = sb("rt_col2", [128, 1], F32)
    rt_eq6 = sb("rt_eq6", [128, 6], F32)
    rt_srt = sb("rt_srt", [128, 8], F32)
    rt_sc = sb("rt_sc", [128, 16, 8], F32)
    rt_scrow = sb("rt_scrow", [8, 128], F32)
    rt_dbg6 = sb("rt_dbg6", [128, 6], F32)
    dbg_s1t = sb("dbg_s1t", [128, 6], F32)
    dbg_s2t = sb("dbg_s2t", [128, 8], F32)
    dbg_s3t = sb("dbg_s3t", [128, 8], F32)
    dbg_s4t = sb("dbg_s4t", [128, 1], F32)

    _rem = nc.sbuf_bytes_remaining - 256
    base_ov, top_ov = nc.bump_sbuf(_rem)
    OVSZ = top_ov - base_ov
    assert OVSZ >= 131 * 1024, OVSZ

    def ov(name, shape, dt, off):
        h = nc.alloc_sbuf_tensor_at(name, shape, dt, offset=base_ov + off)
        return h

    t_xn = ov("t_xn", [128, 8, TOK], F32, 0)
    t_x = ov("t_x", [128, 8, 512], F32, 64 * 1024)
    t_sq = ov("t_sq", [128, 8, 512], F32, 80 * 1024)
    t_hb = ov("t_hb", [128, 8, TOK], BF16, 96 * 1024)
    t_tmp = ov("t_tmp", [128, 512], F32, 128 * 1024)
    p2_h = ov("p2_h", [128, 8, 2048], BF16, 0)
    p2_xc = ov("p2_xc", [128, 4, 2048], F32, 32 * 1024)
    p2_xcs = ov("p2_xcs", [128, 4, 2048], F32, 64 * 1024)
    p2_xcsb = ov("p2_xcsb", [128, 4, 2048], BF16, 96 * 1024)
    p2_tail = ov("p2_tail", [128, 4, 4], F32, 112 * 1024)
    p2_szb = ov("p2_szb", [128, 2, 512], BF16, 113 * 1024)
    p2_dbst = ov("p2_dbst", [96, 2, 512], F32, 116 * 1024)
    p3_lo = ov("p3_lo", [64, L], BF16, 0)
    p3_dt = ov("p3_dt", [128, 2048], BF16, 16 * 1024)
    p3_dtx = ov("p3_dtx", [128, 2048], BF16, 20 * 1024)
    p3_xcs = ov("p3_xcs", [128, 2048], BF16, 24 * 1024)
    p3_a = ov("p3_a", [128, 2048], F32, 28 * 1024)
    p3_u = ov("p3_u", [128, 2048], BF16, 36 * 1024)
    p3_hh = ov("p3_hh", [128, 2048], BF16, 40 * 1024)
    p3_y = ov("p3_y", [128, 2048], F32, 44 * 1024)
    p3_Bbc = ov("p3_Bbc", [128, 2, 2048], BF16, 52 * 1024)
    p3_Cbc = ov("p3_Cbc", [128, 2, 2048], BF16, 60 * 1024)
    p3_sz = ov("p3_sz", [128, 2048], BF16, 68 * 1024)
    p3_yf = ov("p3_yf", [128, 4, 2048], BF16, 72 * 1024)
    p3_wo = ov("p3_wo", [128, 2, 512], F32, 88 * 1024)
    p3_t1 = ov("p3_t1", [128, 512], F32, 92 * 1024)
    p3_t2 = ov("p3_t2", [128, 512], F32, 94 * 1024)
    p3_carry = ov("p3_carry", [128, 64], F32, 96 * 1024)
    p5_xm = ov("p5_xm", [128, 8, 512], F32, 0)
    p5_h2 = ov("p5_h2", [128, 8, 512], BF16, 16 * 1024)
    p5_sq = ov("p5_sq", [128, 8, 512], F32, 24 * 1024)
    p5_ag = ov("p5_ag", [128, 32, 512], BF16, 24 * 1024)
    p5_w1 = ov("p5_w1", [128, 8, 1024], BF16, 56 * 1024)
    p5_w2 = ov("p5_w2", [128, 2, 512], BF16, 72 * 1024)
    p5_g = ov("p5_g", [128, 512], BF16, 74 * 1024)
    p5_xt = ov("p5_xt", [128, 512], F32, 75 * 1024)
    p5_mt = ov("p5_mt", [128, 512], F32, 77 * 1024)
    p4_dw = ov("p4_dw", [128, 21, 1024], BF16, 0)
    p4_gw = ov("p4_gw", [128, 8, 1408], BF16, 42 * 1024)
    p4_uw = ov("p4_uw", [128, 8, 1408], BF16, 64 * 1024)
    p4_prod = ov("p4_prod", [128, 11, 512], BF16, 86 * 1024)
    p4_sb = ov("p4_sb", [128, 2, 512], BF16, 97 * 1024)
    p4_ds = ov("p4_ds", [128, 2, 512], F32, 99 * 1024)
    p4_scb = ov("p4_scb", [128, TOK], F32, 103 * 1024)

    pb = [nc.alloc_psum_tensor(f"pb{i}", [128, 512], F32) for i in range(8)]

    ch_ld = Chan(nc, "ld")
    ch_g = Chan(nc, "chg")
    ch_t = Chan(nc, "cht")
    ch_v = Chan(nc, "chv")
    ch_s = Chan(nc, "chs")
    ch_st = Chan(nc, "chst")
    ch_cc = Chan(nc, "chcc")

    ch_rh = Chan(nc, "chrh0")

    def vfence(v):
        ch_rh.tick(v.engine_nop())
        ch_rh.wait(v)

    with nc.Block() as block:
        @block.sync
        def _(sync):
            pairs = [
                (c_ln1g[:], ln1g), (c_ln1b[:], ln1b), (c_ln2g[:], ln2g),
                (c_ln2b[:], ln2b), (c_convw[:], conv_w), (c_convb[:], conv_b),
                (c_bdt[:], b_dt), (c_alog[:], a_log), (c_dskip[:], d_skip),
                (c_ident[:], ident),
                (c_iota6[:], iota6.partition_broadcast(128)),
                (c_rev8[:], rev8.partition_broadcast(128)),
                (c_ones[:], ones_r), (c_onec[:], ones_c),
                (c_b1row[:], b1row), (c_b2row[:], b2row),
                (c_bff1[:], b_ff1), (c_bff2[:], b_ff2),
                (w_rtw1[:], rtw1.rearrange("(k p) m -> p k m", p=128)),
                (w_aw2[:], aw2),
            ]
            for o_, i_ in pairs:
                ch_ld.tick(sync.dma_start(out=o_, in_=i_), 16)

        @block.gpsimd
        def _(g):
            ch_g.tick(g.dma_start(out=wb_inxc[:], in_=w_in_xc.rearrange("(k p) m -> p k m", p=128)), 16)
            ch_g.tick(g.dma_start(out=wb_inz[:], in_=w_in_z.rearrange("(k p) m -> p k m", p=128)), 16)
            ch_g.tick(g.dma_start(out=wb_x[:], in_=w_x_sl.rearrange("(k p) m -> p k m", p=128)), 16)
            ch_g.tick(g.dma_start(out=wb_dt[:], in_=w_dt_sl), 16)
            ch_g.tick(g.dma_start(out=wb_out[:], in_=w_out_sl.rearrange("(k p) m -> p k m", p=128)), 16)

        @block.vector
        def _(v):
            ch_v.tick(v.memset(c_zero[:], 0.0))

        @block.sync
        def _(sync):
            ch_v.wait(sync)
            for m in range(8):
                for j in range(4):
                    ch_st.tick(sync.dma_start(
                        out=moe_dram[m * 128:(m + 1) * 128, j * 512:(j + 1) * 512],
                        in_=c_zero[:]), 16)

        @block.scalar
        def _(s):
            ch_ld.wait(s)
            ch_s.tick(s.activation(out=c_acol[:], in_=c_alog[:], func=AF.Exp))

        @block.vector
        def _(v):
            ch_s.wait(v)
            ch_v.tick(v.tensor_scalar(out=c_acol[:], in0=c_acol[:], scalar1=-1.0,
                                      scalar2=None, op0=OP.mult))

        def emit_ln(srcf, dst_write, pre=None):
            for tt in range(TOK // 512):
                sl = slice(tt * 512, (tt + 1) * 512)
                if pre is not None:
                    pre(tt)

                @block.scalar
                def _(s):
                    ch_st.wait(s)
                    for k in range(8):
                        ins = s.activation(out=t_sq[:, k, :], in_=srcf(k, tt),
                                           func=AF.Square)
                    ch_s.tick(ins)

                @block.tensor
                def _(t):
                    ch_s.wait(t)
                    ch_v.wait(t)
                    for k in range(8):
                        t.matmul(pb[6][0:1, :], c_onec[:, 0:1], srcf(k, tt),
                                 start=(k == 0), stop=(k == 7))
                    for k in range(8):
                        ins = t.matmul(pb[7][0:1, :], c_onec[:, 0:1], t_sq[:, k, :],
                                       start=(k == 0), stop=(k == 7))
                    ch_t.tick(ins)

                @block.vector
                def _(v):
                    ch_t.wait(v)
                    v.tensor_scalar(out=r_mu[:], in0=pb[6][0:1, :], scalar1=1.0 / D,
                                    scalar2=None, op0=OP.mult)
                    v.tensor_scalar(out=r_m2[:], in0=pb[7][0:1, :], scalar1=1.0 / D,
                                    scalar2=EPS, op0=OP.mult, op1=OP.add)
                    v.tensor_mul(r_t[:], r_mu[:], r_mu[:])
                    ch_v.tick(v.tensor_sub(r_m2[:], r_m2[:], r_t[:]))

                @block.scalar
                def _(s):
                    ch_v.wait(s)
                    ch_s.tick(s.activation(out=r_sd[:], in_=r_m2[:], func=AF.Sqrt))

                @block.vector
                def _(v):
                    ch_s.wait(v)
                    ch_v.tick(v.reciprocal(r_sd[:], r_sd[:]))

                @block.tensor
                def _(t):
                    ch_v.wait(t)
                    t.matmul(pb[4][:, :], c_ones[:, :], r_mu[:, :],
                             start=True, stop=True)
                    ins = t.matmul(pb[5][:, :], c_ones[:, :], r_sd[:, :],
                                   start=True, stop=True)
                    ch_t.tick(ins)

                @block.vector
                def _(v):
                    ch_t.wait(v)
                    for k in range(8):
                        ins = dst_write(v, k, sl, tt)
                    ch_v.tick(ins)

        def x_pre(tt):
            @block.sync
            def _(sync):
                if tt > 0:
                    ch_v.wait(sync)
                ins = sync.dma_start(
                    out=t_x[:],
                    in_=x_sl[:, tt * 512:(tt + 1) * 512]
                    .rearrange("(k p) t -> p k t", p=128))
                ch_st.tick(ins, 16)

        def xn_write(v, k, sl, tt):
            v.tensor_sub(t_xn[:, k, sl], t_x[:, k, :], pb[4][:, :])
            return v.tensor_mul(t_xn[:, k, sl], t_xn[:, k, sl], pb[5][:, :])

        emit_ln(lambda k, tt: t_x[:, k, :], xn_write, pre=x_pre)

        @block.sync
        def _(sync):
            ch_v.wait(sync)
            ins = sync.dma_start(out=xn_dram.rearrange("(k p) t -> p k t", p=128),
                                 in_=t_xn[:])
            ch_st.tick(ins, 16)

        @block.vector
        def _(v):
            for k in range(8):
                ins = v.tensor_copy(out=t_xnb[:, k, :], in_=t_xn[:, k, :])
            ch_v.tick(ins)

        def h_write(v, k, sl, tt):
            v.tensor_sub(t_tmp[:, :], t_xn[:, k, sl], pb[4][:, :])
            v.tensor_mul(t_tmp[:, :], t_tmp[:, :], pb[5][:, :])
            return v.tensor_scalar(out=t_hb[:, k, sl], in0=t_tmp[:, :],
                                   scalar1=c_ln1g[:, k:k + 1],
                                   scalar2=c_ln1b[:, k:k + 1],
                                   op0=OP.mult, op1=OP.add)

        emit_ln(lambda k, tt: t_xn[:, k, tt * 512:(tt + 1) * 512], h_write)

        @block.sync
        def _(sync):
            ch_v.wait(sync)
            ins = sync.dma_start(out=ag_in.rearrange("(k p) t -> p k t", p=128),
                                 in_=t_hb[:])
            ch_st.tick(ins, 16)

        @block.gpsimd
        def _(g):
            ch_st.wait(g)
            ins = g.collective_compute(
                "AllGather", OP.bypass, replica_groups=GROUPS,
                ins=[ag_in], outs=[ag_out])
            ch_cc.tick(ins)

        for tt in range(16):
            tsl = slice(tt * 128, (tt + 1) * 128)
            bank = pb[0] if tt % 2 == 0 else pb[1]

            @block.tensor
            def _(t):
                if tt >= 2:
                    ch_s.wait(t)
                    ch_v.wait(t)
                for k in range(8):
                    t.matmul(bank[:, 0:58], t_xn[:, k, tsl], w_rtw1[:, k, :],
                             start=(k == 0), stop=False)
                ins = t.matmul(bank[:, 8:58], c_ones[:, :], c_b1row[:, :],
                               start=False, stop=True)
                ch_t.tick(ins)

            if debug:
                @block.vector
                def _(v):
                    ch_t.wait(v)
                    ch_v.tick(v.tensor_copy(out=rt_ah[:, 0:58] if False else rt_sc[:, 0, :] if False else rt_ahT[0:128, 0:58] if False else rt_ah[:, :], in_=bank[:, 8:58]))

                @block.sync
                def _(sync):
                    ch_v.wait(sync)
                    ch_st.tick(sync.dma_start(out=dbg["rtraw"][tsl, 8:58],
                                              in_=rt_ah[:, :]), 16)

                @block.vector
                def _(v):
                    ch_st.wait(v)
                    ch_v.tick(v.tensor_copy(out=rt_lg[:, :], in_=bank[:, 0:8]))

                @block.sync
                def _(sync):
                    ch_v.wait(sync)
                    ch_st.tick(sync.dma_start(out=dbg["rtraw"][tsl, 0:8],
                                              in_=rt_lg[:, :]), 16)

            @block.vector
            def _(v):
                ch_t.wait(v)
                ch_s.wait(v)
                if debug:
                    ch_st.wait(v)
                v.reduce_max(out=rt_col[:], in_=bank[:, 0:8], axis=AX.X)
                vfence(v)
                ins = v.tensor_scalar(out=rt_col[:], in0=rt_col[:], scalar1=-1.0,
                                      scalar2=None, op0=OP.mult)
                ch_rh.tick(ins)

            @block.scalar
            def _(s):
                ch_rh.wait(s)
                if debug:
                    ch_st.wait(s)
                s.activation(out=rt_probs[:], in_=bank[:, 0:8], func=AF.Exp,
                             bias=rt_col[:, 0:1])
                ins = s.activation(out=rt_ah[:], in_=bank[:, 8:58],
                                   func=AF.Gelu_apprx_tanh)
                ch_s.tick(ins)

            @block.vector
            def _(v):
                ch_s.wait(v)
                v.tensor_reduce(out=rt_col[:], in_=rt_probs[:],
                                axis=AX.X, op=OP.add)
                vfence(v)
                v.reciprocal(rt_col[:], rt_col[:])
                vfence(v)
                ins = v.tensor_scalar(out=rt_probs[:], in0=rt_probs[:],
                                      scalar1=rt_col[:, 0:1], scalar2=None,
                                      op0=OP.mult)
                ch_v.tick(ins)

            @block.tensor
            def _(t):
                ch_s.wait(t)
                ch_t.tick(t.transpose(pb[2][0:50, 0:128], rt_ah[:, :],
                                      c_ident[:, :]))

            @block.vector
            def _(v):
                ch_t.wait(v)
                ins = v.tensor_copy(out=rt_ahT[:, :], in_=pb[2][0:50, 0:128])
                ch_v.tick(ins)

            if debug and tt == 0:
                @block.sync
                def _(sync):
                    ch_v.wait(sync)
                    ch_st.tick(sync.dma_start(out=dbg["ahT"], in_=rt_ahT[:, :]), 16)

            @block.tensor
            def _(t):
                ch_v.wait(t)
                if debug and tt == 0:
                    ch_st.wait(t)
                t.matmul(pb[3][:, 0:6], rt_ahT[:, :], w_aw2[:, :],
                         start=True, stop=False)
                ins = t.matmul(pb[3][:, 0:6], c_ones[:, :], c_b2row[:, :],
                               start=False, stop=True)
                ch_t.tick(ins)

            if debug:
                @block.vector
                def _(v):
                    ch_t.wait(v)
                    ch_st.wait(v)
                    ch_v.tick(v.tensor_copy(out=rt_dbg6[:, :], in_=pb[3][:, 0:6]))

                @block.sync
                def _(sync):
                    ch_v.wait(sync)
                    ch_st.tick(sync.dma_start(out=dbg["praw"][tsl, :],
                                              in_=rt_dbg6[:, :]), 16)

            @block.vector
            def _(v):
                ch_t.wait(v)
                if debug:
                    ch_st.wait(v)
                v.memset(rt_lg[:, 6:8], -1e30)
                v.tensor_copy(out=rt_eq6[:, :], in_=pb[3][:, 0:6])
                vfence(v)
                v.tensor_scalar(out=rt_lg[:, 0:6], in0=rt_eq6[:, :],
                                scalar1=-30.0, scalar2=30.0,
                                op0=OP.max, op1=OP.min)
                vfence(v)
                if debug:
                    v.tensor_copy(out=rt_dbg6[:, :], in_=rt_lg[:, 0:6])
                v.reduce_max(out=rt_col[:], in_=rt_lg[:, :], axis=AX.X)
                vfence(v)
                v.tensor_scalar(out=rt_srt[:, :], in0=rt_lg[:, :],
                                scalar1=rt_col[:, 0:1], scalar2=None,
                                op0=OP.is_ge)
                vfence(v)
                v.tensor_mul(rt_srt[:, :], rt_srt[:, :], c_rev8[:, :])
                vfence(v)
                v.tensor_reduce(out=rt_col2[:], in_=rt_srt[:, :],
                                axis=AX.X, op=OP.max)
                vfence(v)
                v.tensor_scalar(out=rt_col2[:], in0=rt_col2[:],
                                scalar1=-1.0, scalar2=7.0,
                                op0=OP.mult, op1=OP.add)
                vfence(v)
                if debug and tt == 0:
                    v.tensor_copy(out=dbg_s1t[:, :], in_=rt_eq6[:, :])
                    v.tensor_copy(out=dbg_s2t[:, :], in_=rt_lg[:, :])
                    v.tensor_copy(out=dbg_s3t[:, :], in_=rt_srt[:, :])
                    ch_v.tick(v.tensor_copy(out=dbg_s4t[:, :], in_=rt_col2[:, :]))
                v.max(out=rt_srt[:, :], in_=rt_probs[:, :])
                vfence(v)
                v.tensor_scalar(out=rt_eq6[:, :], in0=c_iota6[:, :],
                                scalar1=rt_col2[:, 0:1], scalar2=None,
                                op0=OP.is_equal)
                vfence(v)
                v.tensor_mul(rt_eq6[:, :], rt_eq6[:, :], rt_srt[:, 0:6])
                vfence(v)
                v.tensor_reduce(out=rt_col[:], in_=rt_eq6[:, :], axis=AX.X,
                                op=OP.add)
                vfence(v)
                v.tensor_scalar(out=rt_sc[:, tt, :], in0=rt_probs[:, :],
                                scalar1=rt_col[:, 0:1], scalar2=None,
                                op0=OP.is_ge)
                vfence(v)
                ins = v.tensor_mul(rt_sc[:, tt, :], rt_sc[:, tt, :],
                                   rt_probs[:, :])
                ch_v.tick(ins)

            if debug:
                @block.sync
                def _(sync):
                    ch_v.wait(sync)
                    ch_st.tick(sync.dma_start(out=dbg["probs"][tsl, :],
                                              in_=rt_probs[:, :]), 16)
                    ch_st.tick(sync.dma_start(out=dbg["kidx"][tsl, :],
                                              in_=rt_col2[:, :]), 16)
                    ch_st.tick(sync.dma_start(out=dbg["scores"][tsl, :],
                                              in_=rt_sc[:, tt, :]), 16)
                    ch_st.tick(sync.dma_start(out=dbg["alog2"][tsl, :],
                                              in_=rt_dbg6[:, :]), 16)
                    if tt == 0:
                        ch_st.tick(sync.dma_start(out=dbg["st1"], in_=dbg_s1t[:, :]), 16)
                        ch_st.tick(sync.dma_start(out=dbg["st2"], in_=dbg_s2t[:, :]), 16)
                        ch_st.tick(sync.dma_start(out=dbg["st3"], in_=dbg_s3t[:, :]), 16)
                        ch_st.tick(sync.dma_start(out=dbg["st4"], in_=dbg_s4t[:, :]), 16)

        for tt in range(16):
            bank = pb[2]

            @block.tensor
            def _(t):
                if tt > 0:
                    ch_st.wait(t)
                ins = t.transpose(bank[0:8, 0:128], rt_sc[:, tt, :], c_ident[:, :])
                ch_t.tick(ins)

            @block.vector
            def _(v):
                ch_t.wait(v)
                ins = v.tensor_copy(out=rt_scrow[:, :], in_=bank[0:8, 0:128])
                ch_v.tick(ins)

            @block.sync
            def _(sync):
                ch_v.wait(sync)
                ins = sync.dma_start(out=sc_dram[:, tt * 128:(tt + 1) * 128],
                                     in_=rt_scrow[:, :])
                ch_st.tick(ins, 16)

        if debug:
            @block.sync
            def _(sync):
                ch_v.wait(sync)
                ch_cc.wait(sync)
                ch_st.tick(sync.dma_start(
                    out=dbg["xn"].rearrange("(k p) t -> p k t", p=128),
                    in_=t_xn[:]), 16)
                ch_st.tick(sync.dma_start(
                    out=dbg["h"].rearrange("(k p) t -> p k t", p=128),
                    in_=t_hb[:]), 16)
                ch_st.tick(sync.dma_start(out=dbg["hfull"], in_=ag_out), 16)

        for blk in range(4):
            @block.sync
            def _(sync):
                if blk == 0:
                    ch_cc.wait(sync)
                    ch_v.wait(sync)
                    ch_s.wait(sync)
                else:
                    ch_v.wait(sync)
                for k in range(8):
                    ch_st.tick(sync.dma_start(
                        out=p2_h[:, k, :],
                        in_=ag_out[blk * D + k * 128:blk * D + (k + 1) * 128, :]), 16)

            for m in range(8):
                bank = pb[m % 2]
                wsl = wb_inxc if m < 4 else wb_inz
                mm = m % 4

                for tt in range(4):
                    @block.tensor
                    def _(t):
                        if blk == 0 and m == 0 and tt == 0:
                            ch_g.wait(t)
                        ch_st.wait(t)
                        ch_v.wait(t)
                        ch_s.wait(t)
                        for k in range(8):
                            ins = t.matmul(
                                bank[:, :],
                                wsl[:, k, mm * 128:(mm + 1) * 128],
                                p2_h[:, k, tt * 512:(tt + 1) * 512],
                                start=(k == 0), stop=(k == 7))
                        ch_t.tick(ins)

                    if m < 4:
                        @block.vector
                        def _(v):
                            ch_t.wait(v)
                            ins = v.tensor_copy(
                                out=p2_xc[:, mm, tt * 512:(tt + 1) * 512],
                                in_=bank[:, :])
                            ch_v.tick(ins)
                    else:
                        @block.scalar
                        def _(s_):
                            ch_t.wait(s_)
                            ch_st.wait(s_)
                            ins = s_.activation(out=p2_szb[:, tt % 2, :],
                                                in_=bank[:, :], func=AF.Silu)
                            ch_s.tick(ins)

                        @block.sync
                        def _(sync):
                            ch_s.wait(sync)
                            ins = sync.dma_start(
                                out=sz_dram[mm * 128:(mm + 1) * 128,
                                            blk * 2048 + tt * 512:
                                            blk * 2048 + (tt + 1) * 512],
                                in_=p2_szb[:, tt % 2, :])
                            ch_st.tick(ins, 16)

            for d in range(4):
                @block.vector
                def _(v):
                    ch_v.wait(v)
                    v.tensor_scalar(out=p2_xcs[:, d, :], in0=p2_xc[:, d, :],
                                    scalar1=c_convw[:, d * 4 + 3:d * 4 + 4],
                                    scalar2=None, op0=OP.mult)
                    for kk in range(3):
                        sh = 3 - kk
                        v.scalar_tensor_tensor(
                            out=p2_xcs[:, d, sh:2048],
                            in0=p2_xc[:, d, 0:2048 - sh],
                            scalar=c_convw[:, d * 4 + kk:d * 4 + kk + 1],
                            in1=p2_xcs[:, d, sh:2048],
                            op0=OP.mult, op1=OP.add)
                        if blk > 0:
                            v.scalar_tensor_tensor(
                                out=p2_xcs[:, d, 0:sh],
                                in0=p2_tail[:, d, 3 - sh:3],
                                scalar=c_convw[:, d * 4 + kk:d * 4 + kk + 1],
                                in1=p2_xcs[:, d, 0:sh],
                                op0=OP.mult, op1=OP.add)
                    ins = v.tensor_copy(out=p2_tail[:, d, 0:3],
                                        in_=p2_xc[:, d, 2045:2048])
                    ch_v.tick(ins)

                @block.scalar
                def _(s_):
                    ch_v.wait(s_)
                    ins = s_.activation(out=p2_xcs[:, d, :], in_=p2_xcs[:, d, :],
                                        func=AF.Silu,
                                        bias=c_convb[:, d:d + 1])
                    ch_s.tick(ins)

                @block.vector
                def _(v):
                    ch_s.wait(v)
                    ins = v.tensor_copy(out=p2_xcsb[:, d, :], in_=p2_xcs[:, d, :])
                    ch_v.tick(ins)

                @block.sync
                def _(sync):
                    ch_v.wait(sync)
                    ins = sync.dma_start(
                        out=xcs_dram[d * 128:(d + 1) * 128,
                                     blk * 2048:(blk + 1) * 2048],
                        in_=p2_xcsb[:, d, :])
                    ch_st.tick(ins, 16)

            for tt in range(4):
                @block.tensor
                def _(t):
                    ch_v.wait(t)
                    if tt >= 2:
                        ch_st.wait(t)
                    for d in range(4):
                        ins = t.matmul(
                            pb[2][0:96, :],
                            wb_x[:, d, :],
                            p2_xcsb[:, d, tt * 512:(tt + 1) * 512],
                            start=(d == 0), stop=(d == 3))
                    ch_t.tick(ins)

                @block.vector
                def _(v):
                    ch_t.wait(v)
                    ins = v.tensor_copy(out=p2_dbst[:, tt % 2, :],
                                        in_=pb[2][0:96, :])
                    ch_v.tick(ins)

                @block.sync
                def _(sync):
                    ch_v.wait(sync)
                    ins = sync.dma_start(
                        out=ar_in[:, blk * 2048 + tt * 512:
                                  blk * 2048 + (tt + 1) * 512],
                        in_=p2_dbst[:, tt % 2, :])
                    ch_st.tick(ins, 16)

        @block.gpsimd
        def _(g):
            ch_st.wait(g)
            ins = g.collective_compute(
                "AllReduce", OP.add, replica_groups=GROUPS,
                ins=[ar_in], outs=[ar_out])
            ch_cc.tick(ins)

        if debug:
            @block.gpsimd
            def _(g):
                ch_cc.wait(g)
                ch_g.tick(g.dma_start(out=dbg["xcs"], in_=xcs_dram), 16)
                ch_g.tick(g.dma_start(out=dbg["sz"], in_=sz_dram), 16)
                ch_g.tick(g.dma_start(out=dbg["dbc"], in_=ar_out), 16)

        @block.gpsimd
        def _(g):
            ch_cc.wait(g)
            ch_g.tick(g.dma_start(out=p3_lo[:], in_=ar_out[0:64, :]), 16)

        @block.vector
        def _(v):
            ch_v.tick(v.memset(p3_carry[:], 0.0))

        bc_ready = {}
        for qt in range(4):
            for d in range(4):
                for tt in range(4):
                    csl = slice(qt * 2048 + tt * 512, qt * 2048 + (tt + 1) * 512)

                    @block.tensor
                    def _(t):
                        ch_g.wait(t)
                        ch_s.wait(t)
                        ch_v.wait(t)
                        ins = t.matmul(pb[3][:, :],
                                       wb_dt[:, d * 128:(d + 1) * 128],
                                       p3_lo[:, csl], start=True, stop=True)
                        ch_t.tick(ins)

                    @block.scalar
                    def _(s_):
                        ch_t.wait(s_)
                        ins = s_.activation(out=p3_t1[:, :], in_=pb[3][:, :],
                                            func=AF.Sigmoid,
                                            bias=c_bdt[:, d:d + 1])
                        ch_s.tick(ins)

                    @block.vector
                    def _(v):
                        ch_s.wait(v)
                        ins = v.tensor_scalar(out=p3_t2[:, :], in0=p3_t1[:, :],
                                              scalar1=-1.0, scalar2=1.0,
                                              op0=OP.mult, op1=OP.add)
                        ch_v.tick(ins)

                    @block.scalar
                    def _(s_):
                        ch_v.wait(s_)
                        ins = s_.activation(out=p3_t2[:, :], in_=p3_t2[:, :],
                                            func=AF.Ln)
                        ch_s.tick(ins)

                    @block.vector
                    def _(v):
                        ch_s.wait(v)
                        ins = v.tensor_scalar(
                            out=p3_dt[:, tt * 512:(tt + 1) * 512],
                            in0=p3_t2[:, :], scalar1=-1.0, scalar2=None,
                            op0=OP.mult)
                        ch_v.tick(ins)

                @block.sync
                def _(sync):
                    if qt > 0 or d > 0:
                        ch_v.wait(sync)
                    ins = sync.dma_start(
                        out=p3_xcs[:, :],
                        in_=xcs_dram[d * 128:(d + 1) * 128,
                                     qt * 2048:(qt + 1) * 2048])
                    ch_st.tick(ins, 16)

                @block.vector
                def _(v):
                    ch_st.wait(v)
                    ins = v.tensor_mul(p3_dtx[:, :], p3_dt[:, :], p3_xcs[:, :])
                    ch_v.tick(ins)

                if debug:
                    @block.sync
                    def _(sync):
                        ch_v.wait(sync)
                        ins = sync.dma_start(
                            out=dtd_dram[d * 128:(d + 1) * 128,
                                         qt * 2048:(qt + 1) * 2048],
                            in_=p3_dt[:, :])
                        ch_st.tick(ins, 16)

                for n in range(16):
                    if (d, n) not in bc_ready or True:
                        pass
                    if n == 0:
                        @block.gpsimd
                        def _(g):
                            if qt > 0 or d > 0:
                                ch_v.wait(g)
                            g.dma_start(out=p3_Bbc[:, 0, :], in_=ar_out[
                                64 + 0:65 + 0, qt * 2048:(qt + 1) * 2048]
                                .partition_broadcast(128)).then_inc(ch_g.h, 16)
                            ch_g.n += 16
                            ins = g.dma_start(out=p3_Cbc[:, 0, :], in_=ar_out[
                                80 + 0:81 + 0, qt * 2048:(qt + 1) * 2048]
                                .partition_broadcast(128))
                            ch_g.tick(ins, 16)
                    bc_ready[n] = ch_g.n

                    if n + 1 < 16:
                        @block.gpsimd
                        def _(g):
                            if qt > 0 or d > 0 or n >= 1:
                                ch_v.wait(g)
                            g.dma_start(out=p3_Bbc[:, (n + 1) % 2, :],
                                        in_=ar_out[64 + n + 1:65 + n + 1,
                                                   qt * 2048:(qt + 1) * 2048]
                                        .partition_broadcast(128)).then_inc(ch_g.h, 16)
                            ch_g.n += 16
                            ins = g.dma_start(out=p3_Cbc[:, (n + 1) % 2, :],
                                              in_=ar_out[80 + n + 1:81 + n + 1,
                                                         qt * 2048:(qt + 1) * 2048]
                                              .partition_broadcast(128))
                            ch_g.tick(ins, 16)

                    @block.scalar
                    def _(s_):
                        ch_v.wait(s_)
                        ins = s_.activation(out=p3_a[:, :], in_=p3_dt[:, :],
                                            func=AF.Exp,
                                            scale=c_acol[:, d * 16 + n:
                                                         d * 16 + n + 1])
                        ch_s.tick(ins)

                    @block.vector
                    def _(v):
                        ch_g.wait(v, bc_ready[n])
                        ch_s.wait(v)
                        v.tensor_mul(p3_u[:, :], p3_dtx[:, :],
                                     p3_Bbc[:, n % 2, :])
                        vfence(v)
                        ini = 0.0 if qt == 0 else p3_carry[:, d * 16 + n:
                                                           d * 16 + n + 1]
                        v.tensor_tensor_scan(out=p3_hh[:, :], data0=p3_a[:, :],
                                             data1=p3_u[:, :], initial=ini,
                                             op0=OP.mult, op1=OP.add)
                        vfence(v)
                        v.tensor_copy(out=p3_carry[:, d * 16 + n:d * 16 + n + 1],
                                      in_=p3_hh[:, 2047:2048])
                        if n == 0:
                            ins = v.tensor_mul(p3_y[:, :], p3_hh[:, :],
                                               p3_Cbc[:, n % 2, :])
                        else:
                            v.tensor_mul(p3_u[:, :], p3_hh[:, :],
                                         p3_Cbc[:, n % 2, :])
                            vfence(v)
                            ins = v.tensor_add(p3_y[:, :], p3_y[:, :], p3_u[:, :])
                        ch_v.tick(ins)

                if debug:
                    @block.sync
                    def _(sync):
                        ch_v.wait(sync)
                        ins = sync.dma_start(
                            out=ysd_dram[d * 128:(d + 1) * 128,
                                         qt * 2048:(qt + 1) * 2048],
                            in_=p3_y[:, :])
                        ch_st.tick(ins, 16)

                @block.sync
                def _(sync):
                    if qt > 0 or d > 0:
                        ch_v.wait(sync)
                    ins = sync.dma_start(
                        out=p3_sz[:, :],
                        in_=sz_dram[d * 128:(d + 1) * 128,
                                    qt * 2048:(qt + 1) * 2048])
                    ch_st.tick(ins, 16)

                @block.vector
                def _(v):
                    ch_st.wait(v)
                    v.scalar_tensor_tensor(out=p3_y[:, :], in0=p3_xcs[:, :],
                                           scalar=c_dskip[:, d:d + 1],
                                           in1=p3_y[:, :], op0=OP.mult,
                                           op1=OP.add)
                    vfence(v)
                    ins = v.tensor_mul(p3_yf[:, d, :], p3_y[:, :], p3_sz[:, :])
                    ch_v.tick(ins)

            for m in range(8):
                for tt in range(4):
                    bank = pb[3 + (m * 4 + tt) % 2]

                    @block.tensor
                    def _(t):
                        ch_v.wait(t)
                        if m * 4 + tt >= 2:
                            ch_st.wait(t)
                        for d in range(4):
                            ins = t.matmul(
                                bank[:, :],
                                wb_out[:, d, m * 128:(m + 1) * 128],
                                p3_yf[:, d, tt * 512:(tt + 1) * 512],
                                start=(d == 0), stop=(d == 3))
                        ch_t.tick(ins)

                    @block.vector
                    def _(v):
                        ch_t.wait(v)
                        ins = v.tensor_copy(out=p3_wo[:, tt % 2, :],
                                            in_=bank[:, :])
                        ch_v.tick(ins)

                    @block.sync
                    def _(sync):
                        ch_v.wait(sync)
                        ins = sync.dma_start(
                            out=rs_in[qt * D + m * 128:qt * D + (m + 1) * 128,
                                      tt * 512:(tt + 1) * 512],
                            in_=p3_wo[:, tt % 2, :])
                        ch_st.tick(ins, 16)

        @block.gpsimd
        def _(g):
            ch_st.wait(g)
            ins = g.collective_compute(
                "ReduceScatter", OP.add, replica_groups=GROUPS,
                ins=[rs_in], outs=[rs_out])
            ch_cc.tick(ins)

        if debug:
            @block.gpsimd
            def _(g):
                ch_cc.wait(g)
                ch_g.tick(g.dma_start(out=dbg["dt"], in_=dtd_dram), 16)
                ch_g.tick(g.dma_start(out=dbg["ys"], in_=ysd_dram), 16)
                ch_g.tick(g.dma_start(out=dbg["mamba"], in_=rs_out), 16)

        IH = [(0, 10), (10, 11)]
        for e in range(NE):
            @block.gpsimd
            def _(g):
                if e == 0:
                    ch_v.wait(g)
                else:
                    ch_t.wait(g)
                g.dma_start(out=p4_scb[:, :],
                            in_=sc_dram[e:e + 1, :].partition_broadcast(128)
                            ).then_inc(ch_g.h, 16)
                ch_g.n += 16
                ins = g.dma_start(
                    out=p4_dw[:, :, :],
                    in_=down_wT[e].rearrange("(c p) m -> p c m", p=128))
                ch_g.tick(ins, 16)

            for ih, (ib, ihc) in enumerate(IH):
                @block.gpsimd
                def _(g):
                    if ih > 0:
                        ch_t.wait(g)
                    g.dma_start(
                        out=p4_gw[:, :, 0:ihc * 128],
                        in_=gate_wT[e][:, ib * 128:(ib + ihc) * 128]
                        .rearrange("(k p) m -> p k m", p=128)).then_inc(ch_g.h, 16)
                    ch_g.n += 16
                    ins = g.dma_start(
                        out=p4_uw[:, :, 0:ihc * 128],
                        in_=up_wT[e][:, ib * 128:(ib + ihc) * 128]
                        .rearrange("(k p) m -> p k m", p=128))
                    ch_g.tick(ins, 16)

                for tt in range(4):
                    tsl = slice(tt * 512, (tt + 1) * 512)
                    for ic in range(ihc):
                        @block.tensor
                        def _(t):
                            ch_g.wait(t)
                            ch_v.wait(t)
                            for k in range(8):
                                t.matmul(pb[0][:, :],
                                         p4_gw[:, k, ic * 128:(ic + 1) * 128],
                                         t_xnb[:, k, tsl],
                                         start=(k == 0), stop=(k == 7))
                            for k in range(8):
                                ins = t.matmul(
                                    pb[1][:, :],
                                    p4_uw[:, k, ic * 128:(ic + 1) * 128],
                                    t_xnb[:, k, tsl],
                                    start=(k == 0), stop=(k == 7))
                            ch_t.tick(ins)

                        @block.scalar
                        def _(s_):
                            ch_t.wait(s_)
                            ins = s_.activation(out=p4_sb[:, 0, :],
                                                in_=pb[0][:, :], func=AF.Silu)
                            ch_s.tick(ins)

                        @block.vector
                        def _(v):
                            ch_s.wait(v)
                            v.tensor_copy(out=p4_sb[:, 1, :], in_=pb[1][:, :])
                            vfence(v)
                            ins = v.tensor_mul(p4_prod[:, ic, :],
                                               p4_sb[:, 0, :], p4_sb[:, 1, :])
                            ch_v.tick(ins)

                    for m in range(8):
                        @block.tensor
                        def _(t):
                            ch_v.wait(t)
                            if m >= 2:
                                ch_g.wait(t)
                            for ic in range(ihc):
                                ins = t.matmul(
                                    pb[2 + m % 2][:, :],
                                    p4_dw[:, ib + ic, m * 128:(m + 1) * 128],
                                    p4_prod[:, ic, :],
                                    start=(ic == 0), stop=(ic == ihc - 1))
                            ch_t.tick(ins)

                        @block.vector
                        def _(v):
                            ch_t.wait(v)
                            ins = v.tensor_mul(p4_ds[:, m % 2, :],
                                               pb[2 + m % 2][:, :],
                                               p4_scb[:, tsl])
                            ch_v.tick(ins)

                        @block.gpsimd
                        def _(g):
                            ch_v.wait(g)
                            ins = g.dma_start(
                                out=moe_dram[m * 128:(m + 1) * 128, tsl],
                                in_=p4_ds[:, m % 2, :],
                                accum_op=OP.add)
                            ch_g.tick(ins, 16)

        if debug:
            @block.gpsimd
            def _(g):
                ch_g.wait(g)
                ch_g.tick(g.dma_start(out=dbg["moe"], in_=moe_dram), 16)

        for tq in range(4):
            hsl = slice(tq * 512, (tq + 1) * 512)

            for k in range(8):
                @block.sync
                def _(sync):
                    if tq == 0 and k == 0:
                        ch_cc.wait(sync)
                    ch_v.wait(sync)
                    ch_st.tick(sync.dma_start(
                        out=p5_xm[:, k, :],
                        in_=xn_dram[k * 128:(k + 1) * 128, hsl]), 16)
                    ins = sync.dma_start(out=p5_xt[:, :],
                                         in_=rs_out[k * 128:(k + 1) * 128, hsl])
                    ch_st.tick(ins, 16)

                @block.vector
                def _(v):
                    ch_st.wait(v)
                    ins = v.tensor_add(p5_xm[:, k, :], p5_xm[:, k, :],
                                       p5_xt[:, :])
                    ch_v.tick(ins)

            @block.scalar
            def _(s_):
                ch_v.wait(s_)
                for k in range(8):
                    ins = s_.activation(out=p5_sq[:, k, :], in_=p5_xm[:, k, :],
                                        func=AF.Square)
                ch_s.tick(ins)

            @block.tensor
            def _(t):
                ch_s.wait(t)
                ch_v.wait(t)
                for k in range(8):
                    t.matmul(pb[6][0:1, :], c_onec[:, 0:1], p5_xm[:, k, :],
                             start=(k == 0), stop=(k == 7))
                for k in range(8):
                    ins = t.matmul(pb[7][0:1, :], c_onec[:, 0:1], p5_sq[:, k, :],
                                   start=(k == 0), stop=(k == 7))
                ch_t.tick(ins)

            @block.vector
            def _(v):
                ch_t.wait(v)
                v.tensor_scalar(out=r_mu[:], in0=pb[6][0:1, :],
                                scalar1=1.0 / D, scalar2=None, op0=OP.mult)
                v.tensor_scalar(out=r_m2[:], in0=pb[7][0:1, :],
                                scalar1=1.0 / D, scalar2=EPS,
                                op0=OP.mult, op1=OP.add)
                vfence(v)
                v.tensor_mul(r_t[:], r_mu[:], r_mu[:])
                vfence(v)
                ch_v.tick(v.tensor_sub(r_m2[:], r_m2[:], r_t[:]))

            @block.scalar
            def _(s_):
                ch_v.wait(s_)
                ch_s.tick(s_.activation(out=r_sd[:], in_=r_m2[:], func=AF.Sqrt))

            @block.vector
            def _(v):
                ch_s.wait(v)
                ch_v.tick(v.reciprocal(r_sd[:], r_sd[:]))

            @block.tensor
            def _(t):
                ch_v.wait(t)
                t.matmul(pb[4][:, :], c_ones[:, :], r_mu[:, :],
                         start=True, stop=True)
                ins = t.matmul(pb[5][:, :], c_ones[:, :], r_sd[:, :],
                               start=True, stop=True)
                ch_t.tick(ins)

            @block.vector
            def _(v):
                ch_t.wait(v)
                for k in range(8):
                    v.tensor_sub(t_tmp[:, :], p5_xm[:, k, :], pb[4][:, :])
                    v.tensor_mul(t_tmp[:, :], t_tmp[:, :], pb[5][:, :])
                    ins = v.tensor_scalar(out=p5_h2[:, k, :], in0=t_tmp[:, :],
                                          scalar1=c_ln2g[:, k:k + 1],
                                          scalar2=c_ln2b[:, k:k + 1],
                                          op0=OP.mult, op1=OP.add)
                ch_v.tick(ins)

            for mg in range(8):
                @block.gpsimd
                def _(g):
                    if tq > 0 or mg > 0:
                        ch_t.wait(g)
                    g.dma_start(
                        out=p5_w1[:, :, 0:512],
                        in_=w_ff1[:, mg * 512:(mg + 1) * 512]
                        .rearrange("(k p) m -> p k m", p=128)).then_inc(ch_g.h, 16)
                    ch_g.n += 16
                    ins = g.dma_start(
                        out=p5_w1[:, :, 512:1024],
                        in_=w_ff1[:, FFI + mg * 512:FFI + (mg + 1) * 512]
                        .rearrange("(k p) m -> p k m", p=128))
                    ch_g.tick(ins, 16)

                for sm in range(4):
                    @block.tensor
                    def _(t):
                        ch_g.wait(t)
                        ch_v.wait(t)
                        ch_s.wait(t)
                        for k in range(8):
                            t.matmul(pb[0][:, :],
                                     p5_w1[:, k, sm * 128:(sm + 1) * 128],
                                     p5_h2[:, k, :],
                                     start=(k == 0), stop=(k == 7))
                        for k in range(8):
                            ins = t.matmul(
                                pb[1][:, :],
                                p5_w1[:, k, 512 + sm * 128:512 + (sm + 1) * 128],
                                p5_h2[:, k, :],
                                start=(k == 0), stop=(k == 7))
                        ch_t.tick(ins)

                    mi = mg * 4 + sm

                    @block.scalar
                    def _(s_):
                        ch_t.wait(s_)
                        s_.activation(out=p5_ag[:, mi, :], in_=pb[0][:, :],
                                      func=AF.Identity,
                                      bias=c_bff1[:, mi:mi + 1])
                        ins = s_.activation(out=p5_g[:, :], in_=pb[1][:, :],
                                            func=AF.Gelu_apprx_tanh,
                                            bias=c_bff1[:, 32 + mi:33 + mi])
                        ch_s.tick(ins)

                    @block.vector
                    def _(v):
                        ch_s.wait(v)
                        ins = v.tensor_mul(p5_ag[:, mi, :], p5_ag[:, mi, :],
                                           p5_g[:, :])
                        ch_v.tick(ins)

            for mp in range(2):
                for kc in range(32):
                    @block.gpsimd
                    def _(g):
                        ch_t.wait(g)
                        ins = g.dma_start(
                            out=p5_w2[:, kc % 2, :],
                            in_=w_ff2[kc * 128:(kc + 1) * 128,
                                      mp * 512:(mp + 1) * 512])
                        ch_g.tick(ins, 16)

                    @block.tensor
                    def _(t):
                        ch_g.wait(t)
                        if kc == 0:
                            ch_v.wait(t)
                        for m4 in range(4):
                            ins = t.matmul(
                                pb[m4][:, :],
                                p5_w2[:, kc % 2, m4 * 128:(m4 + 1) * 128],
                                p5_ag[:, kc, :],
                                start=(kc == 0), stop=(kc == 31))
                        ch_t.tick(ins)

                @block.vector
                def _(v):
                    ch_t.wait(v)
                    for m4 in range(4):
                        mi = mp * 4 + m4
                        ins = v.scalar_tensor_tensor(
                            out=p5_xm[:, mi, :], in0=pb[m4][:, :],
                            scalar=c_bff2[:, mi:mi + 1],
                            in1=p5_xm[:, mi, :], op0=OP.add, op1=OP.add)
                    ch_v.tick(ins)

            if debug:
                @block.sync
                def _(sync):
                    ch_v.wait(sync)
                    for k in range(8):
                        ch_st.tick(sync.dma_start(
                            out=dbg["attn"][k * 128:(k + 1) * 128, hsl],
                            in_=p5_xm[:, k, :]), 16)

            for k in range(8):
                @block.sync
                def _(sync):
                    if tq == 0 and k == 0:
                        ch_g.wait(sync)
                    ch_v.wait(sync)
                    if debug:
                        ch_st.wait(sync)
                    ch_st.tick(sync.dma_start(
                        out=p5_xt[:, :], in_=x_sl[k * 128:(k + 1) * 128, hsl]),
                        16)
                    ins = sync.dma_start(
                        out=p5_mt[:, :], in_=moe_dram[k * 128:(k + 1) * 128, hsl])
                    ch_st.tick(ins, 16)

                @block.vector
                def _(v):
                    ch_st.wait(v)
                    v.tensor_add(p5_xt[:, :], p5_xm[:, k, :], p5_xt[:, :])
                    vfence(v)
                    ins = v.tensor_add(p5_mt[:, :], p5_mt[:, :], p5_xt[:, :])
                    ch_v.tick(ins)

                @block.sync
                def _(sync):
                    ch_v.wait(sync)
                    ins = sync.dma_start(
                        out=out_fm[k * 128:(k + 1) * 128, hsl], in_=p5_mt[:, :])
                    ch_st.tick(ins, 16)

        @block.sync
        def _(sync):
            ch_st.wait(sync)
            ch_g.wait(sync)
            ch_cc.wait(sync)

    return nc, dbg


def _prep_inputs(inputs):
    f32 = np.float32
    x = inputs["L0_attn_x"]
    w_in = inputs["L0_attn_w_in"]
    conv_w = inputs["L0_attn_conv_w"]
    conv_b = inputs["L0_attn_conv_b"]
    w_x = inputs["L0_attn_w_x"]
    w_dt = inputs["L0_attn_w_dt"]
    b_dt = inputs["L0_attn_b_dt"]
    a_log = inputs["L0_attn_a_log"]
    d_skip = inputs["L0_attn_d_skip"]
    w_out = inputs["L0_attn_w_out"]

    def chcol(v, q):
        return np.ascontiguousarray(
            v[q * CH:(q + 1) * CH].reshape(4, 128).T).astype(f32)

    def dcol(v):
        return np.ascontiguousarray(v.reshape(8, 128).T).astype(f32)

    rtw1 = np.ascontiguousarray(
        np.concatenate([inputs["L0_ffn_router_w"].T,
                        inputs["L0_ffn_actor_w1"].T], axis=1)).astype(f32)
    gate_wT = np.ascontiguousarray(
        inputs["L0_ffn_gate_w"].transpose(0, 2, 1)).astype(f32)
    up_wT = np.ascontiguousarray(
        inputs["L0_ffn_up_w"].transpose(0, 2, 1)).astype(f32)
    down_wT = np.ascontiguousarray(
        inputs["L0_ffn_down_w"].transpose(0, 2, 1)).astype(f32)

    shared = dict(
        ln1g=dcol(inputs["L0_attn_ln1_g"]),
        ln1b=dcol(inputs["L0_attn_ln1_b"]),
        ln2g=dcol(inputs["L0_attn_ln2_g"]),
        ln2b=dcol(inputs["L0_attn_ln2_b"]),
        w_ff1=np.ascontiguousarray(inputs["L0_attn_w_ff1"]).astype(f32),
        b_ff1=np.ascontiguousarray(
            inputs["L0_attn_b_ff1"].reshape(64, 128).T).astype(f32),
        w_ff2=np.ascontiguousarray(inputs["L0_attn_w_ff2"]).astype(f32),
        b_ff2=dcol(inputs["L0_attn_b_ff2"]),
        rtw1=rtw1,
        b1row=np.ascontiguousarray(
            inputs["L0_ffn_actor_b1"][None, :]).astype(f32),
        aw2=np.ascontiguousarray(inputs["L0_ffn_actor_w2"].T).astype(f32),
        b2row=np.ascontiguousarray(
            inputs["L0_ffn_actor_b2"][None, :]).astype(f32),
        gate_wT=gate_wT, up_wT=up_wT, down_wT=down_wT,
        ident=np.eye(128, dtype=f32),
        iota6=np.arange(6, dtype=f32)[None, :].copy(),
        rev8=np.arange(7, -1, -1, dtype=f32)[None, :].copy(),
        ones_r=np.ones((1, 128), dtype=f32),
        ones_c=np.ones((128, 1), dtype=f32),
    )

    in_maps = []
    for c in range(8):
        b, q = c // 4, c % 4
        m = dict(shared)
        m["x_sl"] = np.ascontiguousarray(
            x[b, q * TOK:(q + 1) * TOK, :].T).astype(f32)
        m["w_in_xc"] = np.ascontiguousarray(
            w_in[:, q * CH:(q + 1) * CH]).astype(f32)
        m["w_in_z"] = np.ascontiguousarray(
            w_in[:, DIN + q * CH:DIN + (q + 1) * CH]).astype(f32)
        m["conv_w"] = np.ascontiguousarray(
            conv_w[q * CH:(q + 1) * CH].reshape(4, 128, 4)
            .transpose(1, 0, 2).reshape(128, 16)).astype(f32)
        m["conv_b"] = chcol(conv_b, q)
        m["w_x_sl"] = np.ascontiguousarray(
            w_x[q * CH:(q + 1) * CH, :]).astype(f32)
        m["w_dt_sl"] = np.ascontiguousarray(
            w_dt[:, q * CH:(q + 1) * CH]).astype(f32)
        m["b_dt"] = chcol(b_dt, q)
        m["a_log"] = np.ascontiguousarray(
            a_log[q * CH:(q + 1) * CH].reshape(4, 128, 16)
            .transpose(1, 0, 2).reshape(128, 64)).astype(f32)
        m["d_skip"] = chcol(d_skip, q)
        m["w_out_sl"] = np.ascontiguousarray(
            w_out[q * CH:(q + 1) * CH, :]).astype(f32)
        in_maps.append(m)
    return in_maps


def kernel(**inputs):
    in_maps = _prep_inputs(inputs)
    nc, dbg = build_nc()
    res = run_bass_kernel_spmd(nc, in_maps, core_ids=list(range(8)))
    outs = []
    for b in range(B):
        parts = [res.results[b * 4 + q]["out_fm"] for q in range(4)]
        outs.append(np.concatenate(parts, axis=1).T)
    return np.stack(outs, axis=0).astype(np.float32)


# version 02
# speedup vs baseline: 1.7596x; 1.7596x over previous
import numpy as np
import ml_dtypes
import concourse.bass as bass
import concourse.mybir as mybir
from concourse.bass_utils import run_bass_kernel_spmd

F32 = mybir.dt.float32
BF16 = mybir.dt.bfloat16
OP = mybir.AluOpType
AF = mybir.ActivationFunctionType
AX = mybir.AxisListType

B, L, D = 2, 8192, 1024
DIN = 2048
NST = 16
DTR = 64
FFI = 4096
NE = 8
EI = 2688
TOK = 2048
CH = 512
EPS = 1e-5
GROUPS = [[0, 1, 2, 3], [4, 5, 6, 7]]

DEBUG = False


class Chan:

    def __init__(self, nc, name):
        self.h = nc.alloc_semaphore(name)
        self.n = 0

    def tick(self, instr, amt=1):
        instr.then_inc(self.h, amt)
        self.n += amt
        return self.n

    def wait(self, eng, n=None):
        eng.wait_ge(self.h, self.n if n is None else n)


def build_nc(debug=DEBUG):
    nc = bass.Bass("TRN2", target_bir_lowering=False)

    def din(name, shape, dt=F32):
        return nc.dram_tensor(name, shape, dt, kind="ExternalInput").ap()

    def dout(name, shape, dt=F32):
        return nc.dram_tensor(name, shape, dt, kind="ExternalOutput").ap()

    x_sl = din("x_sl", [D, TOK])
    ln1g = din("ln1g", [128, 8])
    ln1b = din("ln1b", [128, 8])
    ln2g = din("ln2g", [128, 8])
    ln2b = din("ln2b", [128, 8])
    w_in_xc = din("w_in_xc", [D, CH])
    w_in_z = din("w_in_z", [D, CH])
    conv_w = din("conv_w", [128, 16])
    conv_b = din("conv_b", [128, 4])
    w_x_sl = din("w_x_sl", [CH, 96])
    w_dt_sl = din("w_dt_sl", [DTR, CH])
    b_dt = din("b_dt", [128, 4])
    a_log = din("a_log", [128, 64])
    d_skip = din("d_skip", [128, 4])
    w_out_sl = din("w_out_sl", [CH, D])
    w_ff1 = din("w_ff1", [D, 2 * FFI])
    b_ff1 = din("b_ff1", [128, 64])
    w_ff2 = din("w_ff2", [FFI, D])
    b_ff2 = din("b_ff2", [128, 8])
    rtw1 = din("rtw1", [D, 58])
    b1row = din("b1row", [1, 50])
    aw2 = din("aw2", [50, 6])
    b2row = din("b2row", [1, 6])
    gate_wT = din("gate_wT", [NE, D, EI])
    up_wT = din("up_wT", [NE, D, EI])
    down_wT = din("down_wT", [NE, EI, D])
    ident = din("ident", [128, 128])
    iota6 = din("iota6", [1, 6])
    rev8 = din("rev8", [1, 8])
    ones_r = din("ones_r", [1, 128])
    ones_c = din("ones_c", [128, 1])

    out_fm = dout("out_fm", [D, TOK])

    dbg = {}
    if debug:
        dbg["xn"] = dout("dbg_xn", [D, TOK])
        dbg["h"] = dout("dbg_h", [D, TOK], BF16)
        dbg["probs"] = dout("dbg_probs", [TOK, 8])
        dbg["kidx"] = dout("dbg_kidx", [TOK, 1])
        dbg["scores"] = dout("dbg_scores", [TOK, 8])
        dbg["hfull"] = dout("dbg_hfull", [4 * D, TOK], BF16)
        dbg["rtraw"] = dout("dbg_rtraw", [TOK, 58])
        dbg["ahT"] = dout("dbg_ahT", [50, 128])
        dbg["praw"] = dout("dbg_praw", [TOK, 6])
        dbg["st1"] = dout("dbg_st1", [128, 6])
        dbg["st2"] = dout("dbg_st2", [128, 8])
        dbg["st3"] = dout("dbg_st3", [128, 8])
        dbg["st4"] = dout("dbg_st4", [128, 1])
        dbg["xcs"] = dout("dbg_xcs", [CH, L], BF16)
        dbg["sz"] = dout("dbg_sz", [CH, L], BF16)
        dbg["dbc"] = dout("dbg_dbc", [96, L])
        dbg["dt"] = dout("dbg_dt", [CH, L], BF16)
        dbg["ys"] = dout("dbg_ys", [CH, L])
        dbg["mamba"] = dout("dbg_mamba", [D, TOK])
        dbg["attn"] = dout("dbg_attn", [D, TOK])
        dbg["moe"] = dout("dbg_moe", [D, TOK])
        dbg["alog2"] = dout("dbg_alog2", [TOK, 6])

    ag_in = nc.dram_tensor("ag_in", [D, TOK], BF16).ap()
    ag_out = nc.dram_tensor("ag_out", [4 * D, TOK], BF16).ap()
    ar_in = nc.dram_tensor("ar_in", [96, L], F32).ap()
    ar_out = nc.dram_tensor("ar_out", [96, L], F32).ap()
    rs_in = nc.dram_tensor("rs_in", [4 * D, TOK], F32).ap()
    rs_out = nc.dram_tensor("rs_out", [D, TOK], F32).ap()
    xn_dram = nc.dram_tensor("xn_dram", [D, TOK], F32).ap()
    xcs_dram = nc.dram_tensor("xcs_dram", [CH, L], BF16).ap()
    sz_dram = nc.dram_tensor("sz_dram", [CH, L], BF16).ap()
    sc_dram = nc.dram_tensor("sc_dram", [NE, TOK], F32).ap()
    moe_dram = nc.dram_tensor("moe_dram", [D, TOK], F32).ap()
    ysd_dram = nc.dram_tensor("ysd_dram", [CH, L], F32).ap()
    dtd_dram = nc.dram_tensor("dtd_dram", [CH, L], BF16).ap()

    sb = nc.alloc_sbuf_tensor

    c_ln1g = sb("c_ln1g", [128, 8], F32)
    c_ln1b = sb("c_ln1b", [128, 8], F32)
    c_ln2g = sb("c_ln2g", [128, 8], F32)
    c_ln2b = sb("c_ln2b", [128, 8], F32)
    c_convw = sb("c_convw", [128, 16], F32)
    c_convb = sb("c_convb", [128, 4], F32)
    c_bdt = sb("c_bdt", [128, 4], F32)
    c_alog = sb("c_alog", [128, 64], F32)
    c_dskip = sb("c_dskip", [128, 4], F32)
    c_ident = sb("c_ident", [128, 128], F32)
    c_iota6 = sb("c_iota6", [128, 6], F32)
    c_rev8 = sb("c_rev8", [128, 8], F32)
    c_ones = sb("c_ones", [1, 128], F32)
    c_onec = sb("c_onec", [128, 1], F32)
    c_b1row = sb("c_b1row", [1, 50], F32)
    c_b2row = sb("c_b2row", [1, 6], F32)
    c_bff1 = sb("c_bff1", [128, 64], F32)
    c_bff2 = sb("c_bff2", [128, 8], F32)
    c_acol = sb("c_acol", [128, 64], F32)
    c_zero = sb("c_zero", [128, 512], F32)

    w_rtw1 = sb("w_rtw1", [128, 8, 58], F32)
    w_aw2 = sb("w_aw2", [50, 6], F32)
    wb_inxc = sb("wb_inxc", [128, 8, CH], BF16)
    wb_inz = sb("wb_inz", [128, 8, CH], BF16)
    wb_x = sb("wb_x", [128, 4, 96], BF16)
    wb_dt = sb("wb_dt", [DTR, CH], BF16)
    wb_out = sb("wb_out", [128, 4, D], BF16)

    t_xnb = sb("t_xnb", [128, 8, TOK], BF16)
    r_mu = sb("r_mu", [1, 512], F32)
    r_m2 = sb("r_m2", [1, 512], F32)
    r_t = sb("r_t", [1, 512], F32)
    r_sd = sb("r_sd", [1, 512], F32)
    rt_probs = sb("rt_probs", [128, 8], F32)
    rt_ah = sb("rt_ah", [128, 50], F32)
    rt_ahT = sb("rt_ahT", [50, 128], F32)
    rt_lg = sb("rt_lg", [128, 8], F32)
    rt_col = sb("rt_col", [128, 1], F32)
    rt_col2 = sb("rt_col2", [128, 1], F32)
    rt_eq6 = sb("rt_eq6", [128, 6], F32)
    rt_srt = sb("rt_srt", [128, 8], F32)
    rt_sc = sb("rt_sc", [128, 16, 8], F32)
    rt_scrow = sb("rt_scrow", [8, 128], F32)
    rt_dbg6 = sb("rt_dbg6", [128, 6], F32)
    dbg_s1t = sb("dbg_s1t", [128, 6], F32)
    dbg_s2t = sb("dbg_s2t", [128, 8], F32)
    dbg_s3t = sb("dbg_s3t", [128, 8], F32)
    dbg_s4t = sb("dbg_s4t", [128, 1], F32)

    _rem = nc.sbuf_bytes_remaining - 256
    base_ov, top_ov = nc.bump_sbuf(_rem)
    OVSZ = top_ov - base_ov
    assert OVSZ >= 131 * 1024, OVSZ

    def ov(name, shape, dt, off):
        h = nc.alloc_sbuf_tensor_at(name, shape, dt, offset=base_ov + off)
        return h

    t_xn = ov("t_xn", [128, 8, TOK], F32, 0)
    t_x = ov("t_x", [128, 8, 512], F32, 64 * 1024)
    t_sq = ov("t_sq", [128, 8, 512], F32, 80 * 1024)
    t_hb = ov("t_hb", [128, 8, TOK], BF16, 96 * 1024)
    t_tmp = ov("t_tmp", [128, 512], F32, 128 * 1024)
    p2_h = ov("p2_h", [128, 8, 2048], BF16, 0)
    p2_xc = ov("p2_xc", [128, 4, 2048], F32, 32 * 1024)
    p2_xcs = ov("p2_xcs", [128, 4, 2048], F32, 64 * 1024)
    p2_xcsb = ov("p2_xcsb", [128, 4, 2048], BF16, 96 * 1024)
    p2_tail = ov("p2_tail", [128, 4, 4], F32, 112 * 1024)
    p2_szb = ov("p2_szb", [128, 2, 512], BF16, 113 * 1024)
    p2_dbst = ov("p2_dbst", [96, 2, 512], F32, 116 * 1024)
    p3_lo = ov("p3_lo", [64, L], BF16, 0)
    p3_dt = ov("p3_dt", [128, 2048], BF16, 16 * 1024)
    p3_dtx = ov("p3_dtx", [128, 2048], BF16, 20 * 1024)
    p3_xcs = ov("p3_xcs", [128, 2048], BF16, 24 * 1024)
    p3_a = ov("p3_a", [128, 2, 2048], F32, 117 * 1024)
    p3_u = ov("p3_u", [128, 2048], BF16, 36 * 1024)
    p3_hh = ov("p3_hh", [128, 2048], BF16, 40 * 1024)
    p3_y = ov("p3_y", [128, 2048], F32, 44 * 1024)
    p3_Bbc = ov("p3_Bbc", [128, 2, 2048], BF16, 52 * 1024)
    p3_Cbc = ov("p3_Cbc", [128, 2, 2048], BF16, 60 * 1024)
    p3_sz = ov("p3_sz", [128, 2048], BF16, 68 * 1024)
    p3_yf = ov("p3_yf", [128, 4, 2048], BF16, 72 * 1024)
    p3_wo = ov("p3_wo", [128, 2, 512], F32, 88 * 1024)
    p3_t1 = ov("p3_t1", [128, 512], F32, 92 * 1024)
    p3_t2 = ov("p3_t2", [128, 512], F32, 94 * 1024)
    p3_carry = ov("p3_carry", [128, 64], F32, 96 * 1024)
    p5_xm = ov("p5_xm", [128, 8, 512], F32, 0)
    p5_h2 = ov("p5_h2", [128, 8, 512], BF16, 16 * 1024)
    p5_sq = ov("p5_sq", [128, 8, 512], F32, 24 * 1024)
    p5_ag = ov("p5_ag", [128, 32, 512], BF16, 24 * 1024)
    p5_w1 = ov("p5_w1", [128, 2, 8, 1024], BF16, 56 * 1024)
    p5_w2 = ov("p5_w2", [128, 2, 512], BF16, 88 * 1024)
    p5_g = ov("p5_g", [128, 512], BF16, 90 * 1024)
    p5_xt = ov("p5_xt", [128, 512], F32, 91 * 1024)
    p5_mt = ov("p5_mt", [128, 512], F32, 93 * 1024)
    p4_dw = ov("p4_dw", [128, 21, 1024], BF16, 0)
    p4_gw = ov("p4_gw", [128, 8, 1408], BF16, 42 * 1024)
    p4_uw = ov("p4_uw", [128, 8, 1408], BF16, 64 * 1024)
    p4_prod = ov("p4_prod", [128, 11, 512], BF16, 86 * 1024)
    p4_sb = ov("p4_sb", [128, 4, 512], BF16, 97 * 1024)
    p4_ds = ov("p4_ds", [128, 2, 512], F32, 101 * 1024)
    p4_scb = ov("p4_scb", [128, TOK], F32, 105 * 1024)

    pb = [nc.alloc_psum_tensor(f"pb{i}", [128, 512], F32) for i in range(8)]

    ch_ld = Chan(nc, "ld")
    ch_g = Chan(nc, "chg")
    ch_t = Chan(nc, "cht")
    ch_v = Chan(nc, "chv")
    ch_s = Chan(nc, "chs")
    ch_st = Chan(nc, "chst")
    ch_cc = Chan(nc, "chcc")

    ch_rh = Chan(nc, "chrh0")

    slot_tbl = {}

    def slot_need(key, eng):
        if key in slot_tbl:
            ch, n = slot_tbl[key]
            eng.wait_ge(ch.h, n)

    def slot_done(key, chan):
        slot_tbl[key] = (chan, chan.n)

    def vfence(v):
        ch_rh.tick(v.engine_nop())
        ch_rh.wait(v)

    with nc.Block() as block:
        @block.sync
        def _(sync):
            pairs = [
                (c_ln1g[:], ln1g), (c_ln1b[:], ln1b), (c_ln2g[:], ln2g),
                (c_ln2b[:], ln2b), (c_convw[:], conv_w), (c_convb[:], conv_b),
                (c_bdt[:], b_dt), (c_alog[:], a_log), (c_dskip[:], d_skip),
                (c_ident[:], ident),
                (c_iota6[:], iota6.partition_broadcast(128)),
                (c_rev8[:], rev8.partition_broadcast(128)),
                (c_ones[:], ones_r), (c_onec[:], ones_c),
                (c_b1row[:], b1row), (c_b2row[:], b2row),
                (c_bff1[:], b_ff1), (c_bff2[:], b_ff2),
                (w_rtw1[:], rtw1.rearrange("(k p) m -> p k m", p=128)),
                (w_aw2[:], aw2),
            ]
            for o_, i_ in pairs:
                ch_ld.tick(sync.dma_start(out=o_, in_=i_), 16)

        @block.gpsimd
        def _(g):
            ch_g.tick(g.dma_start(out=wb_inxc[:], in_=w_in_xc.rearrange("(k p) m -> p k m", p=128)), 16)
            ch_g.tick(g.dma_start(out=wb_inz[:], in_=w_in_z.rearrange("(k p) m -> p k m", p=128)), 16)
            ch_g.tick(g.dma_start(out=wb_x[:], in_=w_x_sl.rearrange("(k p) m -> p k m", p=128)), 16)
            ch_g.tick(g.dma_start(out=wb_dt[:], in_=w_dt_sl), 16)
            ch_g.tick(g.dma_start(out=wb_out[:], in_=w_out_sl.rearrange("(k p) m -> p k m", p=128)), 16)

        @block.vector
        def _(v):
            ch_v.tick(v.memset(c_zero[:], 0.0))

        @block.sync
        def _(sync):
            ch_v.wait(sync)
            for m in range(8):
                for j in range(4):
                    ch_st.tick(sync.dma_start(
                        out=moe_dram[m * 128:(m + 1) * 128, j * 512:(j + 1) * 512],
                        in_=c_zero[:]), 16)

        @block.scalar
        def _(s):
            ch_ld.wait(s)
            ch_s.tick(s.activation(out=c_acol[:], in_=c_alog[:], func=AF.Exp))

        @block.vector
        def _(v):
            ch_s.wait(v)
            ch_v.tick(v.tensor_scalar(out=c_acol[:], in0=c_acol[:], scalar1=-1.0,
                                      scalar2=None, op0=OP.mult))

        def emit_ln(srcf, dst_write, pre=None):
            for tt in range(TOK // 512):
                sl = slice(tt * 512, (tt + 1) * 512)
                if pre is not None:
                    pre(tt)

                @block.scalar
                def _(s):
                    ch_st.wait(s)
                    for k in range(8):
                        ins = s.activation(out=t_sq[:, k, :], in_=srcf(k, tt),
                                           func=AF.Square)
                    ch_s.tick(ins)

                @block.tensor
                def _(t):
                    ch_s.wait(t)
                    ch_v.wait(t)
                    for k in range(8):
                        t.matmul(pb[6][0:1, :], c_onec[:, 0:1], srcf(k, tt),
                                 start=(k == 0), stop=(k == 7))
                    for k in range(8):
                        ins = t.matmul(pb[7][0:1, :], c_onec[:, 0:1], t_sq[:, k, :],
                                       start=(k == 0), stop=(k == 7))
                    ch_t.tick(ins)

                @block.vector
                def _(v):
                    ch_t.wait(v)
                    v.tensor_scalar(out=r_mu[:], in0=pb[6][0:1, :], scalar1=1.0 / D,
                                    scalar2=None, op0=OP.mult)
                    v.tensor_scalar(out=r_m2[:], in0=pb[7][0:1, :], scalar1=1.0 / D,
                                    scalar2=EPS, op0=OP.mult, op1=OP.add)
                    v.tensor_mul(r_t[:], r_mu[:], r_mu[:])
                    ch_v.tick(v.tensor_sub(r_m2[:], r_m2[:], r_t[:]))

                @block.scalar
                def _(s):
                    ch_v.wait(s)
                    ch_s.tick(s.activation(out=r_sd[:], in_=r_m2[:], func=AF.Sqrt))

                @block.vector
                def _(v):
                    ch_s.wait(v)
                    ch_v.tick(v.reciprocal(r_sd[:], r_sd[:]))

                @block.tensor
                def _(t):
                    ch_v.wait(t)
                    t.matmul(pb[4][:, :], c_ones[:, :], r_mu[:, :],
                             start=True, stop=True)
                    ins = t.matmul(pb[5][:, :], c_ones[:, :], r_sd[:, :],
                                   start=True, stop=True)
                    ch_t.tick(ins)

                @block.vector
                def _(v):
                    ch_t.wait(v)
                    for k in range(8):
                        ins = dst_write(v, k, sl, tt)
                    ch_v.tick(ins)

        def x_pre(tt):
            @block.sync
            def _(sync):
                if tt > 0:
                    ch_v.wait(sync)
                ins = sync.dma_start(
                    out=t_x[:],
                    in_=x_sl[:, tt * 512:(tt + 1) * 512]
                    .rearrange("(k p) t -> p k t", p=128))
                ch_st.tick(ins, 16)

        def xn_write(v, k, sl, tt):
            v.tensor_sub(t_xn[:, k, sl], t_x[:, k, :], pb[4][:, :])
            return v.tensor_mul(t_xn[:, k, sl], t_xn[:, k, sl], pb[5][:, :])

        emit_ln(lambda k, tt: t_x[:, k, :], xn_write, pre=x_pre)

        @block.sync
        def _(sync):
            ch_v.wait(sync)
            ins = sync.dma_start(out=xn_dram.rearrange("(k p) t -> p k t", p=128),
                                 in_=t_xn[:])
            ch_st.tick(ins, 16)

        @block.vector
        def _(v):
            for k in range(8):
                ins = v.tensor_copy(out=t_xnb[:, k, :], in_=t_xn[:, k, :])
            ch_v.tick(ins)

        def h_write(v, k, sl, tt):
            v.tensor_sub(t_tmp[:, :], t_xn[:, k, sl], pb[4][:, :])
            v.tensor_mul(t_tmp[:, :], t_tmp[:, :], pb[5][:, :])
            return v.tensor_scalar(out=t_hb[:, k, sl], in0=t_tmp[:, :],
                                   scalar1=c_ln1g[:, k:k + 1],
                                   scalar2=c_ln1b[:, k:k + 1],
                                   op0=OP.mult, op1=OP.add)

        emit_ln(lambda k, tt: t_xn[:, k, tt * 512:(tt + 1) * 512], h_write)

        @block.sync
        def _(sync):
            ch_v.wait(sync)
            ins = sync.dma_start(out=ag_in.rearrange("(k p) t -> p k t", p=128),
                                 in_=t_hb[:])
            ch_st.tick(ins, 16)

        @block.gpsimd
        def _(g):
            ch_st.wait(g)
            ins = g.collective_compute(
                "AllGather", OP.bypass, replica_groups=GROUPS,
                ins=[ag_in], outs=[ag_out])
            ch_cc.tick(ins)

        for tt in range(16):
            tsl = slice(tt * 128, (tt + 1) * 128)
            bank = pb[0] if tt % 2 == 0 else pb[1]

            @block.tensor
            def _(t):
                if tt >= 2:
                    ch_s.wait(t)
                    ch_v.wait(t)
                for k in range(8):
                    t.matmul(bank[:, 0:58], t_xn[:, k, tsl], w_rtw1[:, k, :],
                             start=(k == 0), stop=False)
                ins = t.matmul(bank[:, 8:58], c_ones[:, :], c_b1row[:, :],
                               start=False, stop=True)
                ch_t.tick(ins)

            if debug:
                @block.vector
                def _(v):
                    ch_t.wait(v)
                    ch_v.tick(v.tensor_copy(out=rt_ah[:, 0:58] if False else rt_sc[:, 0, :] if False else rt_ahT[0:128, 0:58] if False else rt_ah[:, :], in_=bank[:, 8:58]))

                @block.sync
                def _(sync):
                    ch_v.wait(sync)
                    ch_st.tick(sync.dma_start(out=dbg["rtraw"][tsl, 8:58],
                                              in_=rt_ah[:, :]), 16)

                @block.vector
                def _(v):
                    ch_st.wait(v)
                    ch_v.tick(v.tensor_copy(out=rt_lg[:, :], in_=bank[:, 0:8]))

                @block.sync
                def _(sync):
                    ch_v.wait(sync)
                    ch_st.tick(sync.dma_start(out=dbg["rtraw"][tsl, 0:8],
                                              in_=rt_lg[:, :]), 16)

            @block.vector
            def _(v):
                ch_t.wait(v)
                ch_s.wait(v)
                if debug:
                    ch_st.wait(v)
                v.reduce_max(out=rt_col[:], in_=bank[:, 0:8], axis=AX.X)
                vfence(v)
                ins = v.tensor_scalar(out=rt_col[:], in0=rt_col[:], scalar1=-1.0,
                                      scalar2=None, op0=OP.mult)
                ch_rh.tick(ins)

            @block.scalar
            def _(s):
                ch_rh.wait(s)
                if debug:
                    ch_st.wait(s)
                s.activation(out=rt_probs[:], in_=bank[:, 0:8], func=AF.Exp,
                             bias=rt_col[:, 0:1])
                ins = s.activation(out=rt_ah[:], in_=bank[:, 8:58],
                                   func=AF.Gelu_apprx_tanh)
                ch_s.tick(ins)

            @block.vector
            def _(v):
                ch_s.wait(v)
                v.tensor_reduce(out=rt_col[:], in_=rt_probs[:],
                                axis=AX.X, op=OP.add)
                vfence(v)
                v.reciprocal(rt_col[:], rt_col[:])
                vfence(v)
                ins = v.tensor_scalar(out=rt_probs[:], in0=rt_probs[:],
                                      scalar1=rt_col[:, 0:1], scalar2=None,
                                      op0=OP.mult)
                ch_v.tick(ins)

            @block.tensor
            def _(t):
                ch_s.wait(t)
                ch_t.tick(t.transpose(pb[2][0:50, 0:128], rt_ah[:, :],
                                      c_ident[:, :]))

            @block.vector
            def _(v):
                ch_t.wait(v)
                ins = v.tensor_copy(out=rt_ahT[:, :], in_=pb[2][0:50, 0:128])
                ch_v.tick(ins)

            if debug and tt == 0:
                @block.sync
                def _(sync):
                    ch_v.wait(sync)
                    ch_st.tick(sync.dma_start(out=dbg["ahT"], in_=rt_ahT[:, :]), 16)

            @block.tensor
            def _(t):
                ch_v.wait(t)
                if debug and tt == 0:
                    ch_st.wait(t)
                t.matmul(pb[3][:, 0:6], rt_ahT[:, :], w_aw2[:, :],
                         start=True, stop=False)
                ins = t.matmul(pb[3][:, 0:6], c_ones[:, :], c_b2row[:, :],
                               start=False, stop=True)
                ch_t.tick(ins)

            if debug:
                @block.vector
                def _(v):
                    ch_t.wait(v)
                    ch_st.wait(v)
                    ch_v.tick(v.tensor_copy(out=rt_dbg6[:, :], in_=pb[3][:, 0:6]))

                @block.sync
                def _(sync):
                    ch_v.wait(sync)
                    ch_st.tick(sync.dma_start(out=dbg["praw"][tsl, :],
                                              in_=rt_dbg6[:, :]), 16)

            @block.vector
            def _(v):
                ch_t.wait(v)
                if debug:
                    ch_st.wait(v)
                v.memset(rt_lg[:, 6:8], -1e30)
                v.tensor_copy(out=rt_eq6[:, :], in_=pb[3][:, 0:6])
                vfence(v)
                v.tensor_scalar(out=rt_lg[:, 0:6], in0=rt_eq6[:, :],
                                scalar1=-30.0, scalar2=30.0,
                                op0=OP.max, op1=OP.min)
                vfence(v)
                if debug:
                    v.tensor_copy(out=rt_dbg6[:, :], in_=rt_lg[:, 0:6])
                v.reduce_max(out=rt_col[:], in_=rt_lg[:, :], axis=AX.X)
                vfence(v)
                v.tensor_scalar(out=rt_srt[:, :], in0=rt_lg[:, :],
                                scalar1=rt_col[:, 0:1], scalar2=None,
                                op0=OP.is_ge)
                vfence(v)
                v.tensor_mul(rt_srt[:, :], rt_srt[:, :], c_rev8[:, :])
                vfence(v)
                v.tensor_reduce(out=rt_col2[:], in_=rt_srt[:, :],
                                axis=AX.X, op=OP.max)
                vfence(v)
                v.tensor_scalar(out=rt_col2[:], in0=rt_col2[:],
                                scalar1=-1.0, scalar2=7.0,
                                op0=OP.mult, op1=OP.add)
                vfence(v)
                if debug and tt == 0:
                    v.tensor_copy(out=dbg_s1t[:, :], in_=rt_eq6[:, :])
                    v.tensor_copy(out=dbg_s2t[:, :], in_=rt_lg[:, :])
                    v.tensor_copy(out=dbg_s3t[:, :], in_=rt_srt[:, :])
                    ch_v.tick(v.tensor_copy(out=dbg_s4t[:, :], in_=rt_col2[:, :]))
                v.max(out=rt_srt[:, :], in_=rt_probs[:, :])
                vfence(v)
                v.tensor_scalar(out=rt_eq6[:, :], in0=c_iota6[:, :],
                                scalar1=rt_col2[:, 0:1], scalar2=None,
                                op0=OP.is_equal)
                vfence(v)
                v.tensor_mul(rt_eq6[:, :], rt_eq6[:, :], rt_srt[:, 0:6])
                vfence(v)
                v.tensor_reduce(out=rt_col[:], in_=rt_eq6[:, :], axis=AX.X,
                                op=OP.add)
                vfence(v)
                v.tensor_scalar(out=rt_sc[:, tt, :], in0=rt_probs[:, :],
                                scalar1=rt_col[:, 0:1], scalar2=None,
                                op0=OP.is_ge)
                vfence(v)
                ins = v.tensor_mul(rt_sc[:, tt, :], rt_sc[:, tt, :],
                                   rt_probs[:, :])
                ch_v.tick(ins)

            if debug:
                @block.sync
                def _(sync):
                    ch_v.wait(sync)
                    ch_st.tick(sync.dma_start(out=dbg["probs"][tsl, :],
                                              in_=rt_probs[:, :]), 16)
                    ch_st.tick(sync.dma_start(out=dbg["kidx"][tsl, :],
                                              in_=rt_col2[:, :]), 16)
                    ch_st.tick(sync.dma_start(out=dbg["scores"][tsl, :],
                                              in_=rt_sc[:, tt, :]), 16)
                    ch_st.tick(sync.dma_start(out=dbg["alog2"][tsl, :],
                                              in_=rt_dbg6[:, :]), 16)
                    if tt == 0:
                        ch_st.tick(sync.dma_start(out=dbg["st1"], in_=dbg_s1t[:, :]), 16)
                        ch_st.tick(sync.dma_start(out=dbg["st2"], in_=dbg_s2t[:, :]), 16)
                        ch_st.tick(sync.dma_start(out=dbg["st3"], in_=dbg_s3t[:, :]), 16)
                        ch_st.tick(sync.dma_start(out=dbg["st4"], in_=dbg_s4t[:, :]), 16)

        for tt in range(16):
            bank = pb[2]

            @block.tensor
            def _(t):
                if tt > 0:
                    ch_st.wait(t)
                ins = t.transpose(bank[0:8, 0:128], rt_sc[:, tt, :], c_ident[:, :])
                ch_t.tick(ins)

            @block.vector
            def _(v):
                ch_t.wait(v)
                ins = v.tensor_copy(out=rt_scrow[:, :], in_=bank[0:8, 0:128])
                ch_v.tick(ins)

            @block.sync
            def _(sync):
                ch_v.wait(sync)
                ins = sync.dma_start(out=sc_dram[:, tt * 128:(tt + 1) * 128],
                                     in_=rt_scrow[:, :])
                ch_st.tick(ins, 16)

        if debug:
            @block.sync
            def _(sync):
                ch_v.wait(sync)
                ch_cc.wait(sync)
                ch_st.tick(sync.dma_start(
                    out=dbg["xn"].rearrange("(k p) t -> p k t", p=128),
                    in_=t_xn[:]), 16)
                ch_st.tick(sync.dma_start(
                    out=dbg["h"].rearrange("(k p) t -> p k t", p=128),
                    in_=t_hb[:]), 16)
                ch_st.tick(sync.dma_start(out=dbg["hfull"], in_=ag_out), 16)

        for blk in range(4):
            @block.sync
            def _(sync):
                if blk == 0:
                    ch_cc.wait(sync)
                    ch_v.wait(sync)
                    ch_s.wait(sync)
                else:
                    ch_v.wait(sync)
                for k in range(8):
                    ch_st.tick(sync.dma_start(
                        out=p2_h[:, k, :],
                        in_=ag_out[blk * D + k * 128:blk * D + (k + 1) * 128, :]), 16)

            h_ready = ch_st.n
            for m in range(8):
                wsl = wb_inxc if m < 4 else wb_inz
                mm = m % 4

                for tt in range(4):
                    gi = (m * 4 + tt) % 2
                    bank = pb[gi]

                    @block.tensor
                    def _(t):
                        if blk == 0 and m == 0 and tt == 0:
                            ch_g.wait(t)
                        if m == 0 and tt == 0:
                            t.wait_ge(ch_st.h, h_ready)
                        slot_need(("bank", gi), t)
                        for k in range(8):
                            ins = t.matmul(
                                bank[:, :],
                                wsl[:, k, mm * 128:(mm + 1) * 128],
                                p2_h[:, k, tt * 512:(tt + 1) * 512],
                                start=(k == 0), stop=(k == 7))
                        ch_t.tick(ins)
                    xz_done = ch_t.n

                    if m < 4:
                        @block.vector
                        def _(v):
                            v.wait_ge(ch_t.h, xz_done)
                            ins = v.tensor_copy(
                                out=p2_xc[:, mm, tt * 512:(tt + 1) * 512],
                                in_=bank[:, :])
                            ch_v.tick(ins)
                        slot_done(("bank", gi), ch_v)
                    else:
                        @block.scalar
                        def _(s_):
                            s_.wait_ge(ch_t.h, xz_done)
                            slot_need(("p2sz", tt % 2), s_)
                            ins = s_.activation(out=p2_szb[:, tt % 2, :],
                                                in_=bank[:, :], func=AF.Silu)
                            ch_s.tick(ins)
                        slot_done(("bank", gi), ch_s)
                        sz_done = ch_s.n

                        @block.sync
                        def _(sync):
                            sync.wait_ge(ch_s.h, sz_done)
                            ins = sync.dma_start(
                                out=sz_dram[mm * 128:(mm + 1) * 128,
                                            blk * 2048 + tt * 512:
                                            blk * 2048 + (tt + 1) * 512],
                                in_=p2_szb[:, tt % 2, :])
                            ch_st.tick(ins, 16)
                        slot_done(("p2sz", tt % 2), ch_st)

            for d in range(4):
                @block.vector
                def _(v):
                    ch_v.wait(v)
                    v.tensor_scalar(out=p2_xcs[:, d, :], in0=p2_xc[:, d, :],
                                    scalar1=c_convw[:, d * 4 + 3:d * 4 + 4],
                                    scalar2=None, op0=OP.mult)
                    for kk in range(3):
                        sh = 3 - kk
                        v.scalar_tensor_tensor(
                            out=p2_xcs[:, d, sh:2048],
                            in0=p2_xc[:, d, 0:2048 - sh],
                            scalar=c_convw[:, d * 4 + kk:d * 4 + kk + 1],
                            in1=p2_xcs[:, d, sh:2048],
                            op0=OP.mult, op1=OP.add)
                        if blk > 0:
                            v.scalar_tensor_tensor(
                                out=p2_xcs[:, d, 0:sh],
                                in0=p2_tail[:, d, 3 - sh:3],
                                scalar=c_convw[:, d * 4 + kk:d * 4 + kk + 1],
                                in1=p2_xcs[:, d, 0:sh],
                                op0=OP.mult, op1=OP.add)
                    ins = v.tensor_copy(out=p2_tail[:, d, 0:3],
                                        in_=p2_xc[:, d, 2045:2048])
                    ch_v.tick(ins)

                @block.scalar
                def _(s_):
                    ch_v.wait(s_)
                    ins = s_.activation(out=p2_xcs[:, d, :], in_=p2_xcs[:, d, :],
                                        func=AF.Silu,
                                        bias=c_convb[:, d:d + 1])
                    ch_s.tick(ins)

                @block.vector
                def _(v):
                    ch_s.wait(v)
                    ins = v.tensor_copy(out=p2_xcsb[:, d, :], in_=p2_xcs[:, d, :])
                    ch_v.tick(ins)

                @block.sync
                def _(sync):
                    ch_v.wait(sync)
                    ins = sync.dma_start(
                        out=xcs_dram[d * 128:(d + 1) * 128,
                                     blk * 2048:(blk + 1) * 2048],
                        in_=p2_xcsb[:, d, :])
                    ch_st.tick(ins, 16)

            xcsb_ready = ch_v.n
            for tt in range(4):
                @block.tensor
                def _(t):
                    if tt == 0:
                        t.wait_ge(ch_v.h, xcsb_ready)
                    slot_need(("bank", 2), t)
                    for d in range(4):
                        ins = t.matmul(
                            pb[2][0:96, :],
                            wb_x[:, d, :],
                            p2_xcsb[:, d, tt * 512:(tt + 1) * 512],
                            start=(d == 0), stop=(d == 3))
                    ch_t.tick(ins)
                dbc_done = ch_t.n

                @block.vector
                def _(v):
                    v.wait_ge(ch_t.h, dbc_done)
                    slot_need(("p2db", tt % 2), v)
                    ins = v.tensor_copy(out=p2_dbst[:, tt % 2, :],
                                        in_=pb[2][0:96, :])
                    ch_v.tick(ins)
                slot_done(("bank", 2), ch_v)
                db_done = ch_v.n

                @block.sync
                def _(sync):
                    sync.wait_ge(ch_v.h, db_done)
                    ins = sync.dma_start(
                        out=ar_in[:, blk * 2048 + tt * 512:
                                  blk * 2048 + (tt + 1) * 512],
                        in_=p2_dbst[:, tt % 2, :])
                    ch_st.tick(ins, 16)
                slot_done(("p2db", tt % 2), ch_st)

        @block.gpsimd
        def _(g):
            ch_st.wait(g)
            ins = g.collective_compute(
                "AllReduce", OP.add, replica_groups=GROUPS,
                ins=[ar_in], outs=[ar_out])
            ch_cc.tick(ins)

        if debug:
            @block.gpsimd
            def _(g):
                ch_cc.wait(g)
                ch_g.tick(g.dma_start(out=dbg["xcs"], in_=xcs_dram), 16)
                ch_g.tick(g.dma_start(out=dbg["sz"], in_=sz_dram), 16)
                ch_g.tick(g.dma_start(out=dbg["dbc"], in_=ar_out), 16)

        @block.gpsimd
        def _(g):
            ch_cc.wait(g)
            ch_g.tick(g.dma_start(out=p3_lo[:], in_=ar_out[0:64, :]), 16)

        @block.vector
        def _(v):
            ch_v.tick(v.memset(p3_carry[:], 0.0))

        bc_ready = {}
        for qt in range(4):
            for d in range(4):
                for tt in range(4):
                    csl = slice(qt * 2048 + tt * 512, qt * 2048 + (tt + 1) * 512)

                    @block.tensor
                    def _(t):
                        if qt == 0 and d == 0 and tt == 0:
                            ch_g.wait(t)
                        slot_need(("bank", 3), t)
                        ins = t.matmul(pb[3][:, :],
                                       wb_dt[:, d * 128:(d + 1) * 128],
                                       p3_lo[:, csl], start=True, stop=True)
                        ch_t.tick(ins)

                    @block.scalar
                    def _(s_):
                        ch_t.wait(s_)
                        ins = s_.activation(out=p3_t1[:, :], in_=pb[3][:, :],
                                            func=AF.Sigmoid,
                                            bias=c_bdt[:, d:d + 1])
                        ch_s.tick(ins)
                    slot_done(("bank", 3), ch_s)

                    @block.vector
                    def _(v):
                        ch_s.wait(v)
                        ins = v.tensor_scalar(out=p3_t2[:, :], in0=p3_t1[:, :],
                                              scalar1=-1.0, scalar2=1.0,
                                              op0=OP.mult, op1=OP.add)
                        ch_v.tick(ins)

                    @block.scalar
                    def _(s_):
                        ch_v.wait(s_)
                        ins = s_.activation(out=p3_t2[:, :], in_=p3_t2[:, :],
                                            func=AF.Ln)
                        ch_s.tick(ins)

                    @block.vector
                    def _(v):
                        ch_s.wait(v)
                        ins = v.tensor_scalar(
                            out=p3_dt[:, tt * 512:(tt + 1) * 512],
                            in0=p3_t2[:, :], scalar1=-1.0, scalar2=None,
                            op0=OP.mult)
                        ch_v.tick(ins)

                @block.sync
                def _(sync):
                    if qt > 0 or d > 0:
                        ch_v.wait(sync)
                    ins = sync.dma_start(
                        out=p3_xcs[:, :],
                        in_=xcs_dram[d * 128:(d + 1) * 128,
                                     qt * 2048:(qt + 1) * 2048])
                    ch_st.tick(ins, 16)

                @block.vector
                def _(v):
                    ch_st.wait(v)
                    ins = v.tensor_mul(p3_dtx[:, :], p3_dt[:, :], p3_xcs[:, :])
                    ch_v.tick(ins)
                dt_ready = ch_v.n
                a_ready = {}

                if debug:
                    @block.sync
                    def _(sync):
                        ch_v.wait(sync)
                        ins = sync.dma_start(
                            out=dtd_dram[d * 128:(d + 1) * 128,
                                         qt * 2048:(qt + 1) * 2048],
                            in_=p3_dt[:, :])
                        ch_st.tick(ins, 16)

                for n in range(16):
                    if (d, n) not in bc_ready or True:
                        pass
                    if n == 0:
                        @block.gpsimd
                        def _(g):
                            if qt > 0 or d > 0:
                                ch_v.wait(g)
                            g.dma_start(out=p3_Bbc[:, 0, :], in_=ar_out[
                                64 + 0:65 + 0, qt * 2048:(qt + 1) * 2048]
                                .partition_broadcast(128)).then_inc(ch_g.h, 16)
                            ch_g.n += 16
                            ins = g.dma_start(out=p3_Cbc[:, 0, :], in_=ar_out[
                                80 + 0:81 + 0, qt * 2048:(qt + 1) * 2048]
                                .partition_broadcast(128))
                            ch_g.tick(ins, 16)
                    bc_ready[n] = ch_g.n

                    if n + 1 < 16:
                        @block.gpsimd
                        def _(g):
                            if qt > 0 or d > 0 or n >= 1:
                                ch_v.wait(g)
                            g.dma_start(out=p3_Bbc[:, (n + 1) % 2, :],
                                        in_=ar_out[64 + n + 1:65 + n + 1,
                                                   qt * 2048:(qt + 1) * 2048]
                                        .partition_broadcast(128)).then_inc(ch_g.h, 16)
                            ch_g.n += 16
                            ins = g.dma_start(out=p3_Cbc[:, (n + 1) % 2, :],
                                              in_=ar_out[80 + n + 1:81 + n + 1,
                                                         qt * 2048:(qt + 1) * 2048]
                                              .partition_broadcast(128))
                            ch_g.tick(ins, 16)

                    @block.scalar
                    def _(s_):
                        s_.wait_ge(ch_v.h, dt_ready)
                        slot_need(("p3a", n % 2), s_)
                        ins = s_.activation(out=p3_a[:, n % 2, :],
                                            in_=p3_dt[:, :],
                                            func=AF.Exp,
                                            scale=c_acol[:, d * 16 + n:
                                                         d * 16 + n + 1])
                        ch_s.tick(ins)
                    a_ready[n] = ch_s.n

                    @block.vector
                    def _(v):
                        ch_g.wait(v, bc_ready[n])
                        v.wait_ge(ch_s.h, a_ready[n])
                        v.tensor_mul(p3_u[:, :], p3_dtx[:, :],
                                     p3_Bbc[:, n % 2, :])
                        vfence(v)
                        ini = 0.0 if qt == 0 else p3_carry[:, d * 16 + n:
                                                           d * 16 + n + 1]
                        v.tensor_tensor_scan(out=p3_hh[:, :],
                                             data0=p3_a[:, n % 2, :],
                                             data1=p3_u[:, :], initial=ini,
                                             op0=OP.mult, op1=OP.add)
                        vfence(v)
                        v.tensor_copy(out=p3_carry[:, d * 16 + n:d * 16 + n + 1],
                                      in_=p3_hh[:, 2047:2048])
                        if n == 0:
                            ins = v.tensor_mul(p3_y[:, :], p3_hh[:, :],
                                               p3_Cbc[:, n % 2, :])
                        else:
                            v.tensor_mul(p3_u[:, :], p3_hh[:, :],
                                         p3_Cbc[:, n % 2, :])
                            vfence(v)
                            ins = v.tensor_add(p3_y[:, :], p3_y[:, :], p3_u[:, :])
                        ch_v.tick(ins)
                    slot_done(("p3a", n % 2), ch_v)

                if debug:
                    @block.sync
                    def _(sync):
                        ch_v.wait(sync)
                        ins = sync.dma_start(
                            out=ysd_dram[d * 128:(d + 1) * 128,
                                         qt * 2048:(qt + 1) * 2048],
                            in_=p3_y[:, :])
                        ch_st.tick(ins, 16)

                @block.sync
                def _(sync):
                    if qt > 0 or d > 0:
                        ch_v.wait(sync)
                    ins = sync.dma_start(
                        out=p3_sz[:, :],
                        in_=sz_dram[d * 128:(d + 1) * 128,
                                    qt * 2048:(qt + 1) * 2048])
                    ch_st.tick(ins, 16)

                @block.vector
                def _(v):
                    ch_st.wait(v)
                    v.scalar_tensor_tensor(out=p3_y[:, :], in0=p3_xcs[:, :],
                                           scalar=c_dskip[:, d:d + 1],
                                           in1=p3_y[:, :], op0=OP.mult,
                                           op1=OP.add)
                    vfence(v)
                    ins = v.tensor_mul(p3_yf[:, d, :], p3_y[:, :], p3_sz[:, :])
                    ch_v.tick(ins)

            yf_ready = ch_v.n
            for m in range(8):
                for tt in range(4):
                    gi = (m * 4 + tt) % 2
                    bank = pb[3 + gi]

                    @block.tensor
                    def _(t):
                        if m == 0 and tt == 0:
                            t.wait_ge(ch_v.h, yf_ready)
                        slot_need(("bank", 3 + gi), t)
                        for d in range(4):
                            ins = t.matmul(
                                bank[:, :],
                                wb_out[:, d, m * 128:(m + 1) * 128],
                                p3_yf[:, d, tt * 512:(tt + 1) * 512],
                                start=(d == 0), stop=(d == 3))
                        ch_t.tick(ins)
                    mm_done = ch_t.n

                    @block.vector
                    def _(v):
                        v.wait_ge(ch_t.h, mm_done)
                        slot_need(("p3wo", gi), v)
                        ins = v.tensor_copy(out=p3_wo[:, gi, :],
                                            in_=bank[:, :])
                        ch_v.tick(ins)
                    slot_done(("bank", 3 + gi), ch_v)
                    ev_done = ch_v.n

                    @block.sync
                    def _(sync):
                        sync.wait_ge(ch_v.h, ev_done)
                        ins = sync.dma_start(
                            out=rs_in[qt * D + m * 128:qt * D + (m + 1) * 128,
                                      tt * 512:(tt + 1) * 512],
                            in_=p3_wo[:, gi, :])
                        ch_st.tick(ins, 16)
                    slot_done(("p3wo", gi), ch_st)

        @block.gpsimd
        def _(g):
            ch_st.wait(g)
            ins = g.collective_compute(
                "ReduceScatter", OP.add, replica_groups=GROUPS,
                ins=[rs_in], outs=[rs_out])
            ch_cc.tick(ins)

        if debug:
            @block.gpsimd
            def _(g):
                ch_cc.wait(g)
                ch_g.tick(g.dma_start(out=dbg["dt"], in_=dtd_dram), 16)
                ch_g.tick(g.dma_start(out=dbg["ys"], in_=ysd_dram), 16)
                ch_g.tick(g.dma_start(out=dbg["mamba"], in_=rs_out), 16)

        IH = [(0, 10), (10, 11)]
        for e in range(NE):
            @block.gpsimd
            def _(g):
                if e == 0:
                    ch_v.wait(g)
                else:
                    ch_t.wait(g)
                    ch_v.wait(g)
                g.dma_start(out=p4_scb[:, :],
                            in_=sc_dram[e:e + 1, :].partition_broadcast(128)
                            ).then_inc(ch_g.h, 16)
                ch_g.n += 16
                ins = g.dma_start(
                    out=p4_dw[:, :, :],
                    in_=down_wT[e].rearrange("(c p) m -> p c m", p=128))
                ch_g.tick(ins, 16)
            scb_ready = ch_g.n

            for ih, (ib, ihc) in enumerate(IH):
                @block.gpsimd
                def _(g):
                    if ih > 0:
                        ch_t.wait(g)
                    g.dma_start(
                        out=p4_gw[:, :, 0:ihc * 128],
                        in_=gate_wT[e][:, ib * 128:(ib + ihc) * 128]
                        .rearrange("(k p) m -> p k m", p=128)).then_inc(ch_g.h, 16)
                    ch_g.n += 16
                    ins = g.dma_start(
                        out=p4_uw[:, :, 0:ihc * 128],
                        in_=up_wT[e][:, ib * 128:(ib + ihc) * 128]
                        .rearrange("(k p) m -> p k m", p=128))
                    ch_g.tick(ins, 16)

                w_ready = ch_g.n
                for tt in range(4):
                    tsl = slice(tt * 512, (tt + 1) * 512)
                    for ic in range(ihc):
                        gb = ic % 2
                        sb0, sb1 = 2 * gb, 2 * gb + 1

                        @block.tensor
                        def _(t):
                            t.wait_ge(ch_g.h, w_ready)
                            slot_need(("bank", 0 + 2 * gb), t)
                            slot_need(("bank", 1 + 2 * gb), t)
                            slot_need(("prod", ic), t)
                            for k in range(8):
                                t.matmul(pb[0 + 2 * gb][:, :],
                                         p4_gw[:, k, ic * 128:(ic + 1) * 128],
                                         t_xnb[:, k, tsl],
                                         start=(k == 0), stop=(k == 7))
                            for k in range(8):
                                ins = t.matmul(
                                    pb[1 + 2 * gb][:, :],
                                    p4_uw[:, k, ic * 128:(ic + 1) * 128],
                                    t_xnb[:, k, tsl],
                                    start=(k == 0), stop=(k == 7))
                            ch_t.tick(ins)
                        gu_done = ch_t.n

                        @block.scalar
                        def _(s_):
                            s_.wait_ge(ch_t.h, gu_done)
                            slot_need(("p4sb", sb0), s_)
                            ins = s_.activation(out=p4_sb[:, sb0, :],
                                                in_=pb[0 + 2 * gb][:, :],
                                                func=AF.Silu)
                            ch_s.tick(ins)
                        slot_done(("bank", 0 + 2 * gb), ch_s)
                        g_done = ch_s.n

                        @block.vector
                        def _(v):
                            v.wait_ge(ch_s.h, g_done)
                            slot_need(("p4sb", sb1), v)
                            v.tensor_copy(out=p4_sb[:, sb1, :],
                                          in_=pb[1 + 2 * gb][:, :])
                            vfence(v)
                            ins = v.tensor_mul(p4_prod[:, ic, :],
                                               p4_sb[:, sb0, :],
                                               p4_sb[:, sb1, :])
                            ch_v.tick(ins)
                        slot_done(("bank", 1 + 2 * gb), ch_v)
                        slot_done(("p4sb", sb0), ch_v)
                        slot_done(("p4sb", sb1), ch_v)
                    prod_ready = ch_v.n

                    for m in range(8):
                        db = 4 + m % 2

                        @block.tensor
                        def _(t):
                            if m == 0:
                                t.wait_ge(ch_v.h, prod_ready)
                            slot_need(("bank", db), t)
                            for ic in range(ihc):
                                ins = t.matmul(
                                    pb[db][:, :],
                                    p4_dw[:, ib + ic, m * 128:(m + 1) * 128],
                                    p4_prod[:, ic, :],
                                    start=(ic == 0), stop=(ic == ihc - 1))
                            ch_t.tick(ins)
                        dn_done = ch_t.n
                        if m == 7:
                            for icx in range(ihc):
                                slot_done(("prod", icx), ch_t)

                        @block.vector
                        def _(v):
                            v.wait_ge(ch_t.h, dn_done)
                            v.wait_ge(ch_g.h, scb_ready)
                            slot_need(("p4ds", m % 2), v)
                            ins = v.tensor_mul(p4_ds[:, m % 2, :],
                                               pb[db][:, :],
                                               p4_scb[:, tsl])
                            ch_v.tick(ins)
                        slot_done(("bank", db), ch_v)
                        ds_done = ch_v.n

                        @block.gpsimd
                        def _(g):
                            g.wait_ge(ch_v.h, ds_done)
                            ins = g.dma_start(
                                out=moe_dram[m * 128:(m + 1) * 128, tsl],
                                in_=p4_ds[:, m % 2, :],
                                accum_op=OP.add)
                            ch_g.tick(ins, 16)
                        slot_done(("p4ds", m % 2), ch_g)

        if debug:
            @block.gpsimd
            def _(g):
                ch_g.wait(g)
                ch_g.tick(g.dma_start(out=dbg["moe"], in_=moe_dram), 16)

        for tq in range(4):
            hsl = slice(tq * 512, (tq + 1) * 512)

            for k in range(8):
                @block.sync
                def _(sync):
                    if tq == 0 and k == 0:
                        ch_cc.wait(sync)
                    ch_v.wait(sync)
                    ch_st.tick(sync.dma_start(
                        out=p5_xm[:, k, :],
                        in_=xn_dram[k * 128:(k + 1) * 128, hsl]), 16)
                    ins = sync.dma_start(out=p5_xt[:, :],
                                         in_=rs_out[k * 128:(k + 1) * 128, hsl])
                    ch_st.tick(ins, 16)

                @block.vector
                def _(v):
                    ch_st.wait(v)
                    ins = v.tensor_add(p5_xm[:, k, :], p5_xm[:, k, :],
                                       p5_xt[:, :])
                    ch_v.tick(ins)

            @block.scalar
            def _(s_):
                ch_v.wait(s_)
                for k in range(8):
                    ins = s_.activation(out=p5_sq[:, k, :], in_=p5_xm[:, k, :],
                                        func=AF.Square)
                ch_s.tick(ins)

            @block.tensor
            def _(t):
                ch_s.wait(t)
                ch_v.wait(t)
                for k in range(8):
                    t.matmul(pb[6][0:1, :], c_onec[:, 0:1], p5_xm[:, k, :],
                             start=(k == 0), stop=(k == 7))
                for k in range(8):
                    ins = t.matmul(pb[7][0:1, :], c_onec[:, 0:1], p5_sq[:, k, :],
                                   start=(k == 0), stop=(k == 7))
                ch_t.tick(ins)

            @block.vector
            def _(v):
                ch_t.wait(v)
                v.tensor_scalar(out=r_mu[:], in0=pb[6][0:1, :],
                                scalar1=1.0 / D, scalar2=None, op0=OP.mult)
                v.tensor_scalar(out=r_m2[:], in0=pb[7][0:1, :],
                                scalar1=1.0 / D, scalar2=EPS,
                                op0=OP.mult, op1=OP.add)
                vfence(v)
                v.tensor_mul(r_t[:], r_mu[:], r_mu[:])
                vfence(v)
                ch_v.tick(v.tensor_sub(r_m2[:], r_m2[:], r_t[:]))

            @block.scalar
            def _(s_):
                ch_v.wait(s_)
                ch_s.tick(s_.activation(out=r_sd[:], in_=r_m2[:], func=AF.Sqrt))

            @block.vector
            def _(v):
                ch_s.wait(v)
                ch_v.tick(v.reciprocal(r_sd[:], r_sd[:]))

            @block.tensor
            def _(t):
                ch_v.wait(t)
                t.matmul(pb[4][:, :], c_ones[:, :], r_mu[:, :],
                         start=True, stop=True)
                ins = t.matmul(pb[5][:, :], c_ones[:, :], r_sd[:, :],
                               start=True, stop=True)
                ch_t.tick(ins)

            @block.vector
            def _(v):
                ch_t.wait(v)
                for k in range(8):
                    v.tensor_sub(t_tmp[:, :], p5_xm[:, k, :], pb[4][:, :])
                    v.tensor_mul(t_tmp[:, :], t_tmp[:, :], pb[5][:, :])
                    ins = v.tensor_scalar(out=p5_h2[:, k, :], in0=t_tmp[:, :],
                                          scalar1=c_ln2g[:, k:k + 1],
                                          scalar2=c_ln2b[:, k:k + 1],
                                          op0=OP.mult, op1=OP.add)
                ch_v.tick(ins)

            h2_ready = ch_v.n
            w1_ready = {}
            for mg in range(8):
                wslot = mg % 2

                @block.gpsimd
                def _(g):
                    slot_need(("p5w1", wslot), g)
                    g.dma_start(
                        out=p5_w1[:, wslot, :, 0:512],
                        in_=w_ff1[:, mg * 512:(mg + 1) * 512]
                        .rearrange("(k p) m -> p k m", p=128)).then_inc(ch_g.h, 16)
                    ch_g.n += 16
                    ins = g.dma_start(
                        out=p5_w1[:, wslot, :, 512:1024],
                        in_=w_ff1[:, FFI + mg * 512:FFI + (mg + 1) * 512]
                        .rearrange("(k p) m -> p k m", p=128))
                    ch_g.tick(ins, 16)
                w1_ready[mg] = ch_g.n

                for sm in range(4):
                    ab = sm % 2

                    @block.tensor
                    def _(t):
                        t.wait_ge(ch_g.h, w1_ready[mg])
                        if mg == 0 and sm == 0:
                            t.wait_ge(ch_v.h, h2_ready)
                        slot_need(("bank", 0 + 2 * ab), t)
                        slot_need(("bank", 1 + 2 * ab), t)
                        for k in range(8):
                            t.matmul(pb[0 + 2 * ab][:, :],
                                     p5_w1[:, wslot, k, sm * 128:(sm + 1) * 128],
                                     p5_h2[:, k, :],
                                     start=(k == 0), stop=(k == 7))
                        for k in range(8):
                            ins = t.matmul(
                                pb[1 + 2 * ab][:, :],
                                p5_w1[:, wslot, k,
                                      512 + sm * 128:512 + (sm + 1) * 128],
                                p5_h2[:, k, :],
                                start=(k == 0), stop=(k == 7))
                        ch_t.tick(ins)
                    if sm == 3:
                        slot_done(("p5w1", wslot), ch_t)
                    mm_done = ch_t.n

                    mi = mg * 4 + sm

                    @block.scalar
                    def _(s_):
                        s_.wait_ge(ch_t.h, mm_done)
                        slot_need(("p5g",), s_)
                        s_.activation(out=p5_ag[:, mi, :], in_=pb[0 + 2 * ab][:, :],
                                      func=AF.Identity,
                                      bias=c_bff1[:, mi:mi + 1])
                        ins = s_.activation(out=p5_g[:, :], in_=pb[1 + 2 * ab][:, :],
                                            func=AF.Gelu_apprx_tanh,
                                            bias=c_bff1[:, 32 + mi:33 + mi])
                        ch_s.tick(ins)
                    slot_done(("bank", 0 + 2 * ab), ch_s)
                    slot_done(("bank", 1 + 2 * ab), ch_s)
                    sg_done = ch_s.n

                    @block.vector
                    def _(v):
                        v.wait_ge(ch_s.h, sg_done)
                        ins = v.tensor_mul(p5_ag[:, mi, :], p5_ag[:, mi, :],
                                           p5_g[:, :])
                        ch_v.tick(ins)
                    slot_done(("p5g",), ch_v)

            ag_ready = ch_v.n
            for mp in range(2):
                for kc in range(32):
                    @block.gpsimd
                    def _(g):
                        slot_need(("p5w2", kc % 2), g)
                        ins = g.dma_start(
                            out=p5_w2[:, kc % 2, :],
                            in_=w_ff2[kc * 128:(kc + 1) * 128,
                                      mp * 512:(mp + 1) * 512])
                        ch_g.tick(ins, 16)
                    w2r = ch_g.n

                    @block.tensor
                    def _(t):
                        t.wait_ge(ch_g.h, w2r)
                        if kc == 0:
                            t.wait_ge(ch_v.h, ag_ready)
                            for m4 in range(4):
                                slot_need(("bank", m4), t)
                        for m4 in range(4):
                            ins = t.matmul(
                                pb[m4][:, :],
                                p5_w2[:, kc % 2, m4 * 128:(m4 + 1) * 128],
                                p5_ag[:, kc, :],
                                start=(kc == 0), stop=(kc == 31))
                        ch_t.tick(ins)
                    slot_done(("p5w2", kc % 2), ch_t)

                ff2_done = ch_t.n

                @block.vector
                def _(v):
                    v.wait_ge(ch_t.h, ff2_done)
                    for m4 in range(4):
                        mi = mp * 4 + m4
                        ins = v.scalar_tensor_tensor(
                            out=p5_xm[:, mi, :], in0=pb[m4][:, :],
                            scalar=c_bff2[:, mi:mi + 1],
                            in1=p5_xm[:, mi, :], op0=OP.add, op1=OP.add)
                    ch_v.tick(ins)
                for m4 in range(4):
                    slot_done(("bank", m4), ch_v)

            if debug:
                @block.sync
                def _(sync):
                    ch_v.wait(sync)
                    for k in range(8):
                        ch_st.tick(sync.dma_start(
                            out=dbg["attn"][k * 128:(k + 1) * 128, hsl],
                            in_=p5_xm[:, k, :]), 16)

            for k in range(8):
                @block.sync
                def _(sync):
                    if tq == 0 and k == 0:
                        ch_g.wait(sync)
                    ch_v.wait(sync)
                    if debug:
                        ch_st.wait(sync)
                    ch_st.tick(sync.dma_start(
                        out=p5_xt[:, :], in_=x_sl[k * 128:(k + 1) * 128, hsl]),
                        16)
                    ins = sync.dma_start(
                        out=p5_mt[:, :], in_=moe_dram[k * 128:(k + 1) * 128, hsl])
                    ch_st.tick(ins, 16)

                @block.vector
                def _(v):
                    ch_st.wait(v)
                    v.tensor_add(p5_xt[:, :], p5_xm[:, k, :], p5_xt[:, :])
                    vfence(v)
                    ins = v.tensor_add(p5_mt[:, :], p5_mt[:, :], p5_xt[:, :])
                    ch_v.tick(ins)

                @block.sync
                def _(sync):
                    ch_v.wait(sync)
                    ins = sync.dma_start(
                        out=out_fm[k * 128:(k + 1) * 128, hsl], in_=p5_mt[:, :])
                    ch_st.tick(ins, 16)

        @block.sync
        def _(sync):
            ch_st.wait(sync)
            ch_g.wait(sync)
            ch_cc.wait(sync)

    return nc, dbg


def _prep_inputs(inputs):
    f32 = np.float32
    x = inputs["L0_attn_x"]
    w_in = inputs["L0_attn_w_in"]
    conv_w = inputs["L0_attn_conv_w"]
    conv_b = inputs["L0_attn_conv_b"]
    w_x = inputs["L0_attn_w_x"]
    w_dt = inputs["L0_attn_w_dt"]
    b_dt = inputs["L0_attn_b_dt"]
    a_log = inputs["L0_attn_a_log"]
    d_skip = inputs["L0_attn_d_skip"]
    w_out = inputs["L0_attn_w_out"]

    def chcol(v, q):
        return np.ascontiguousarray(
            v[q * CH:(q + 1) * CH].reshape(4, 128).T).astype(f32)

    def dcol(v):
        return np.ascontiguousarray(v.reshape(8, 128).T).astype(f32)

    rtw1 = np.ascontiguousarray(
        np.concatenate([inputs["L0_ffn_router_w"].T,
                        inputs["L0_ffn_actor_w1"].T], axis=1)).astype(f32)
    gate_wT = np.ascontiguousarray(
        inputs["L0_ffn_gate_w"].transpose(0, 2, 1)).astype(f32)
    up_wT = np.ascontiguousarray(
        inputs["L0_ffn_up_w"].transpose(0, 2, 1)).astype(f32)
    down_wT = np.ascontiguousarray(
        inputs["L0_ffn_down_w"].transpose(0, 2, 1)).astype(f32)

    shared = dict(
        ln1g=dcol(inputs["L0_attn_ln1_g"]),
        ln1b=dcol(inputs["L0_attn_ln1_b"]),
        ln2g=dcol(inputs["L0_attn_ln2_g"]),
        ln2b=dcol(inputs["L0_attn_ln2_b"]),
        w_ff1=np.ascontiguousarray(inputs["L0_attn_w_ff1"]).astype(f32),
        b_ff1=np.ascontiguousarray(
            inputs["L0_attn_b_ff1"].reshape(64, 128).T).astype(f32),
        w_ff2=np.ascontiguousarray(inputs["L0_attn_w_ff2"]).astype(f32),
        b_ff2=dcol(inputs["L0_attn_b_ff2"]),
        rtw1=rtw1,
        b1row=np.ascontiguousarray(
            inputs["L0_ffn_actor_b1"][None, :]).astype(f32),
        aw2=np.ascontiguousarray(inputs["L0_ffn_actor_w2"].T).astype(f32),
        b2row=np.ascontiguousarray(
            inputs["L0_ffn_actor_b2"][None, :]).astype(f32),
        gate_wT=gate_wT, up_wT=up_wT, down_wT=down_wT,
        ident=np.eye(128, dtype=f32),
        iota6=np.arange(6, dtype=f32)[None, :].copy(),
        rev8=np.arange(7, -1, -1, dtype=f32)[None, :].copy(),
        ones_r=np.ones((1, 128), dtype=f32),
        ones_c=np.ones((128, 1), dtype=f32),
    )

    in_maps = []
    for c in range(8):
        b, q = c // 4, c % 4
        m = dict(shared)
        m["x_sl"] = np.ascontiguousarray(
            x[b, q * TOK:(q + 1) * TOK, :].T).astype(f32)
        m["w_in_xc"] = np.ascontiguousarray(
            w_in[:, q * CH:(q + 1) * CH]).astype(f32)
        m["w_in_z"] = np.ascontiguousarray(
            w_in[:, DIN + q * CH:DIN + (q + 1) * CH]).astype(f32)
        m["conv_w"] = np.ascontiguousarray(
            conv_w[q * CH:(q + 1) * CH].reshape(4, 128, 4)
            .transpose(1, 0, 2).reshape(128, 16)).astype(f32)
        m["conv_b"] = chcol(conv_b, q)
        m["w_x_sl"] = np.ascontiguousarray(
            w_x[q * CH:(q + 1) * CH, :]).astype(f32)
        m["w_dt_sl"] = np.ascontiguousarray(
            w_dt[:, q * CH:(q + 1) * CH]).astype(f32)
        m["b_dt"] = chcol(b_dt, q)
        m["a_log"] = np.ascontiguousarray(
            a_log[q * CH:(q + 1) * CH].reshape(4, 128, 16)
            .transpose(1, 0, 2).reshape(128, 64)).astype(f32)
        m["d_skip"] = chcol(d_skip, q)
        m["w_out_sl"] = np.ascontiguousarray(
            w_out[q * CH:(q + 1) * CH, :]).astype(f32)
        in_maps.append(m)
    return in_maps


def kernel(**inputs):
    in_maps = _prep_inputs(inputs)
    nc, dbg = build_nc()
    res = run_bass_kernel_spmd(nc, in_maps, core_ids=list(range(8)))
    outs = []
    for b in range(B):
        parts = [res.results[b * 4 + q]["out_fm"] for q in range(4)]
        outs.append(np.concatenate(parts, axis=1).T)
    return np.stack(outs, axis=0).astype(np.float32)


# version 03
# speedup vs baseline: 1.8063x; 1.0265x over previous
import numpy as np
import ml_dtypes
import concourse.bass as bass
import concourse.mybir as mybir
from concourse.bass_utils import run_bass_kernel_spmd

F32 = mybir.dt.float32
BF16 = mybir.dt.bfloat16
OP = mybir.AluOpType
AF = mybir.ActivationFunctionType
AX = mybir.AxisListType

B, L, D = 2, 8192, 1024
DIN = 2048
NST = 16
DTR = 64
FFI = 4096
NE = 8
EI = 2688
TOK = 2048
CH = 512
EPS = 1e-5
GROUPS = [[0, 1, 2, 3], [4, 5, 6, 7]]

DEBUG = False


class Chan:

    def __init__(self, nc, name):
        self.h = nc.alloc_semaphore(name)
        self.n = 0

    def tick(self, instr, amt=1):
        instr.then_inc(self.h, amt)
        self.n += amt
        return self.n

    def wait(self, eng, n=None):
        eng.wait_ge(self.h, self.n if n is None else n)


def build_nc(debug=DEBUG):
    nc = bass.Bass("TRN2", target_bir_lowering=False)

    def din(name, shape, dt=F32):
        return nc.dram_tensor(name, shape, dt, kind="ExternalInput").ap()

    def dout(name, shape, dt=F32):
        return nc.dram_tensor(name, shape, dt, kind="ExternalOutput").ap()

    x_sl = din("x_sl", [D, TOK])
    ln1g = din("ln1g", [128, 8])
    ln1b = din("ln1b", [128, 8])
    ln2g = din("ln2g", [128, 8])
    ln2b = din("ln2b", [128, 8])
    w_in_xc = din("w_in_xc", [D, CH])
    w_in_z = din("w_in_z", [D, CH])
    conv_w = din("conv_w", [128, 16])
    conv_b = din("conv_b", [128, 4])
    w_x_sl = din("w_x_sl", [CH, 96])
    w_dt_sl = din("w_dt_sl", [DTR, CH])
    b_dt = din("b_dt", [128, 4])
    a_log = din("a_log", [128, 64])
    d_skip = din("d_skip", [128, 4])
    w_out_sl = din("w_out_sl", [CH, D])
    w_ff1 = din("w_ff1", [D, 2 * FFI])
    b_ff1 = din("b_ff1", [128, 64])
    w_ff2 = din("w_ff2", [FFI, D])
    b_ff2 = din("b_ff2", [128, 8])
    rtw1 = din("rtw1", [D, 58])
    b1row = din("b1row", [1, 50])
    aw2 = din("aw2", [50, 6])
    b2row = din("b2row", [1, 6])
    gate_wT = din("gate_wT", [NE, D, EI])
    up_wT = din("up_wT", [NE, D, EI])
    down_wT = din("down_wT", [NE, EI, D])
    ident = din("ident", [128, 128])
    iota6 = din("iota6", [1, 6])
    rev8 = din("rev8", [1, 8])
    ones_r = din("ones_r", [1, 128])
    ones_c = din("ones_c", [128, 1])

    out_fm = dout("out_fm", [D, TOK])

    dbg = {}
    if debug:
        dbg["xn"] = dout("dbg_xn", [D, TOK])
        dbg["h"] = dout("dbg_h", [D, TOK], BF16)
        dbg["probs"] = dout("dbg_probs", [TOK, 8])
        dbg["kidx"] = dout("dbg_kidx", [TOK, 1])
        dbg["scores"] = dout("dbg_scores", [TOK, 8])
        dbg["hfull"] = dout("dbg_hfull", [4 * D, TOK], BF16)
        dbg["rtraw"] = dout("dbg_rtraw", [TOK, 58])
        dbg["ahT"] = dout("dbg_ahT", [50, 128])
        dbg["praw"] = dout("dbg_praw", [TOK, 6])
        dbg["st1"] = dout("dbg_st1", [128, 6])
        dbg["st2"] = dout("dbg_st2", [128, 8])
        dbg["st3"] = dout("dbg_st3", [128, 8])
        dbg["st4"] = dout("dbg_st4", [128, 1])
        dbg["xcs"] = dout("dbg_xcs", [CH, L], BF16)
        dbg["sz"] = dout("dbg_sz", [CH, L], BF16)
        dbg["dbc"] = dout("dbg_dbc", [96, L])
        dbg["dt"] = dout("dbg_dt", [CH, L], BF16)
        dbg["ys"] = dout("dbg_ys", [CH, L])
        dbg["mamba"] = dout("dbg_mamba", [D, TOK])
        dbg["attn"] = dout("dbg_attn", [D, TOK])
        dbg["moe"] = dout("dbg_moe", [D, TOK])
        dbg["alog2"] = dout("dbg_alog2", [TOK, 6])

    ag_in = nc.dram_tensor("ag_in", [D, TOK], BF16).ap()
    ag_out = nc.dram_tensor("ag_out", [4 * D, TOK], BF16).ap()
    ar_in = nc.dram_tensor("ar_in", [96, L], F32).ap()
    ar_out = nc.dram_tensor("ar_out", [96, L], F32).ap()
    rs_in = nc.dram_tensor("rs_in", [4 * D, TOK], F32).ap()
    rs_out = nc.dram_tensor("rs_out", [D, TOK], F32).ap()
    xn_dram = nc.dram_tensor("xn_dram", [D, TOK], F32).ap()
    xcs_dram = nc.dram_tensor("xcs_dram", [CH, L], BF16).ap()
    sz_dram = nc.dram_tensor("sz_dram", [CH, L], BF16).ap()
    sc_dram = nc.dram_tensor("sc_dram", [NE, TOK], F32).ap()
    moe_dram = nc.dram_tensor("moe_dram", [D, TOK], F32).ap()
    ysd_dram = nc.dram_tensor("ysd_dram", [CH, L], F32).ap()
    dtd_dram = nc.dram_tensor("dtd_dram", [CH, L], BF16).ap()

    sb = nc.alloc_sbuf_tensor

    c_ln1g = sb("c_ln1g", [128, 8], F32)
    c_ln1b = sb("c_ln1b", [128, 8], F32)
    c_ln2g = sb("c_ln2g", [128, 8], F32)
    c_ln2b = sb("c_ln2b", [128, 8], F32)
    c_convw = sb("c_convw", [128, 16], F32)
    c_convb = sb("c_convb", [128, 4], F32)
    c_bdt = sb("c_bdt", [128, 4], F32)
    c_alog = sb("c_alog", [128, 64], F32)
    c_dskip = sb("c_dskip", [128, 4], F32)
    c_ident = sb("c_ident", [128, 128], F32)
    c_iota6 = sb("c_iota6", [128, 6], F32)
    c_rev8 = sb("c_rev8", [128, 8], F32)
    c_ones = sb("c_ones", [1, 128], F32)
    c_onec = sb("c_onec", [128, 1], F32)
    c_b1row = sb("c_b1row", [1, 50], F32)
    c_b2row = sb("c_b2row", [1, 6], F32)
    c_bff1 = sb("c_bff1", [128, 64], F32)
    c_bff2 = sb("c_bff2", [128, 8], F32)
    c_acol = sb("c_acol", [128, 64], F32)
    c_zero = sb("c_zero", [128, 512], F32)

    w_rtw1 = sb("w_rtw1", [128, 8, 58], F32)
    w_aw2 = sb("w_aw2", [50, 6], F32)
    wb_inxc = sb("wb_inxc", [128, 8, CH], BF16)
    wb_inz = sb("wb_inz", [128, 8, CH], BF16)
    wb_x = sb("wb_x", [128, 4, 96], BF16)
    wb_dt = sb("wb_dt", [DTR, CH], BF16)
    wb_out = sb("wb_out", [128, 4, D], BF16)

    t_xnb = sb("t_xnb", [128, 8, TOK], BF16)
    r_mu = sb("r_mu", [1, 512], F32)
    r_m2 = sb("r_m2", [1, 512], F32)
    r_t = sb("r_t", [1, 512], F32)
    r_sd = sb("r_sd", [1, 512], F32)
    rt_probs = sb("rt_probs", [128, 8], F32)
    rt_ah = sb("rt_ah", [128, 50], F32)
    rt_ahT = sb("rt_ahT", [50, 128], F32)
    rt_lg = sb("rt_lg", [128, 8], F32)
    rt_col = sb("rt_col", [128, 1], F32)
    rt_col2 = sb("rt_col2", [128, 1], F32)
    rt_eq6 = sb("rt_eq6", [128, 6], F32)
    rt_srt = sb("rt_srt", [128, 8], F32)
    rt_sc = sb("rt_sc", [128, 16, 8], F32)
    rt_scrow = sb("rt_scrow", [8, 128], F32)
    rt_dbg6 = sb("rt_dbg6", [128, 6], F32)
    dbg_s1t = sb("dbg_s1t", [128, 6], F32)
    dbg_s2t = sb("dbg_s2t", [128, 8], F32)
    dbg_s3t = sb("dbg_s3t", [128, 8], F32)
    dbg_s4t = sb("dbg_s4t", [128, 1], F32)

    _rem = nc.sbuf_bytes_remaining - 256
    base_ov, top_ov = nc.bump_sbuf(_rem)
    OVSZ = top_ov - base_ov
    assert OVSZ >= 131 * 1024, OVSZ

    def ov(name, shape, dt, off):
        h = nc.alloc_sbuf_tensor_at(name, shape, dt, offset=base_ov + off)
        return h

    t_xn = ov("t_xn", [128, 8, TOK], F32, 0)
    t_x = ov("t_x", [128, 8, 512], F32, 64 * 1024)
    t_sq = ov("t_sq", [128, 8, 512], F32, 80 * 1024)
    t_hb = ov("t_hb", [128, 8, TOK], BF16, 96 * 1024)
    t_tmp = ov("t_tmp", [128, 512], F32, 128 * 1024)
    p2_h = ov("p2_h", [128, 8, 2048], BF16, 0)
    p2_xc = ov("p2_xc", [128, 4, 2048], F32, 32 * 1024)
    p2_xcs = ov("p2_xcs", [128, 4, 2048], F32, 64 * 1024)
    p2_xcsb = ov("p2_xcsb", [128, 4, 2048], BF16, 96 * 1024)
    p2_tail = ov("p2_tail", [128, 4, 4], F32, 112 * 1024)
    p2_szb = ov("p2_szb", [128, 2, 512], BF16, 113 * 1024)
    p2_dbst = ov("p2_dbst", [96, 2, 512], F32, 116 * 1024)
    p3_lo = ov("p3_lo", [64, L], BF16, 0)
    p3_dt = ov("p3_dt", [128, 2048], BF16, 16 * 1024)
    p3_dtx = ov("p3_dtx", [128, 2048], BF16, 20 * 1024)
    p3_xcs = ov("p3_xcs", [128, 2048], BF16, 24 * 1024)
    p3_a = ov("p3_a", [128, 2, 2048], F32, 117 * 1024)
    p3_u = ov("p3_u", [128, 2048], BF16, 36 * 1024)
    p3_hh = ov("p3_hh", [128, 2048], BF16, 40 * 1024)
    p3_y = ov("p3_y", [128, 2048], F32, 44 * 1024)
    p3_Bbc = ov("p3_Bbc", [128, 2, 2048], BF16, 52 * 1024)
    p3_Cbc = ov("p3_Cbc", [128, 2, 2048], BF16, 60 * 1024)
    p3_sz = ov("p3_sz", [128, 2048], BF16, 68 * 1024)
    p3_yf = ov("p3_yf", [128, 4, 2048], BF16, 72 * 1024)
    p3_wo = ov("p3_wo", [128, 2, 512], F32, 88 * 1024)
    p3_t1 = ov("p3_t1", [128, 512], F32, 92 * 1024)
    p3_t2 = ov("p3_t2", [128, 512], F32, 94 * 1024)
    p3_carry = ov("p3_carry", [128, 64], F32, 96 * 1024)
    p5_xm = ov("p5_xm", [128, 8, 512], F32, 0)
    p5_h2 = ov("p5_h2", [128, 8, 512], BF16, 16 * 1024)
    p5_sq = ov("p5_sq", [128, 8, 512], F32, 24 * 1024)
    p5_ag = ov("p5_ag", [128, 32, 512], BF16, 24 * 1024)
    p5_w1 = ov("p5_w1", [128, 2, 8, 1024], BF16, 56 * 1024)
    p5_w2 = ov("p5_w2", [128, 32, 512], BF16, 88 * 1024)
    p5_g = ov("p5_g", [128, 512], BF16, 120 * 1024)
    p5_xt = ov("p5_xt", [128, 512], F32, 121 * 1024)
    p5_mt = ov("p5_mt", [128, 512], F32, 123 * 1024)
    p4_dw = ov("p4_dw", [128, 21, 1024], BF16, 0)
    p4_gw = ov("p4_gw", [128, 8, 1408], BF16, 42 * 1024)
    p4_uw = ov("p4_uw", [128, 8, 1408], BF16, 64 * 1024)
    p4_prod = ov("p4_prod", [128, 11, 512], BF16, 86 * 1024)
    p4_sb = ov("p4_sb", [128, 4, 512], BF16, 97 * 1024)
    p4_ds = ov("p4_ds", [128, 2, 512], F32, 101 * 1024)
    p4_scb = ov("p4_scb", [128, TOK], F32, 105 * 1024)

    pb = [nc.alloc_psum_tensor(f"pb{i}", [128, 512], F32) for i in range(8)]

    ch_ld = Chan(nc, "ld")
    ch_g = Chan(nc, "chg")
    ch_t = Chan(nc, "cht")
    ch_v = Chan(nc, "chv")
    ch_s = Chan(nc, "chs")
    ch_st = Chan(nc, "chst")
    ch_cc = Chan(nc, "chcc")

    ch_rh = Chan(nc, "chrh0")

    slot_tbl = {}

    def slot_need(key, eng):
        if key in slot_tbl:
            ch, n = slot_tbl[key]
            eng.wait_ge(ch.h, n)

    def slot_done(key, chan):
        slot_tbl[key] = (chan, chan.n)

    def vfence(v):
        ch_rh.tick(v.engine_nop())
        ch_rh.wait(v)

    with nc.Block() as block:
        @block.sync
        def _(sync):
            pairs = [
                (c_ln1g[:], ln1g), (c_ln1b[:], ln1b), (c_ln2g[:], ln2g),
                (c_ln2b[:], ln2b), (c_convw[:], conv_w), (c_convb[:], conv_b),
                (c_bdt[:], b_dt), (c_alog[:], a_log), (c_dskip[:], d_skip),
                (c_ident[:], ident),
                (c_iota6[:], iota6.partition_broadcast(128)),
                (c_rev8[:], rev8.partition_broadcast(128)),
                (c_ones[:], ones_r), (c_onec[:], ones_c),
                (c_b1row[:], b1row), (c_b2row[:], b2row),
                (c_bff1[:], b_ff1), (c_bff2[:], b_ff2),
                (w_rtw1[:], rtw1.rearrange("(k p) m -> p k m", p=128)),
                (w_aw2[:], aw2),
            ]
            for o_, i_ in pairs:
                ch_ld.tick(sync.dma_start(out=o_, in_=i_), 16)

        @block.gpsimd
        def _(g):
            ch_g.tick(g.dma_start(out=wb_inxc[:], in_=w_in_xc.rearrange("(k p) m -> p k m", p=128)), 16)
            ch_g.tick(g.dma_start(out=wb_inz[:], in_=w_in_z.rearrange("(k p) m -> p k m", p=128)), 16)
            ch_g.tick(g.dma_start(out=wb_x[:], in_=w_x_sl.rearrange("(k p) m -> p k m", p=128)), 16)
            ch_g.tick(g.dma_start(out=wb_dt[:], in_=w_dt_sl), 16)
            ch_g.tick(g.dma_start(out=wb_out[:], in_=w_out_sl.rearrange("(k p) m -> p k m", p=128)), 16)

        @block.vector
        def _(v):
            ch_v.tick(v.memset(c_zero[:], 0.0))

        @block.sync
        def _(sync):
            ch_v.wait(sync)
            for m in range(8):
                for j in range(4):
                    ch_st.tick(sync.dma_start(
                        out=moe_dram[m * 128:(m + 1) * 128, j * 512:(j + 1) * 512],
                        in_=c_zero[:]), 16)

        @block.scalar
        def _(s):
            ch_ld.wait(s)
            ch_s.tick(s.activation(out=c_acol[:], in_=c_alog[:], func=AF.Exp))

        @block.vector
        def _(v):
            ch_s.wait(v)
            ch_v.tick(v.tensor_scalar(out=c_acol[:], in0=c_acol[:], scalar1=-1.0,
                                      scalar2=None, op0=OP.mult))

        def emit_ln(srcf, dst_write, pre=None):
            for tt in range(TOK // 512):
                sl = slice(tt * 512, (tt + 1) * 512)
                if pre is not None:
                    pre(tt)

                @block.scalar
                def _(s):
                    ch_st.wait(s)
                    for k in range(8):
                        ins = s.activation(out=t_sq[:, k, :], in_=srcf(k, tt),
                                           func=AF.Square)
                    ch_s.tick(ins)

                @block.tensor
                def _(t):
                    ch_s.wait(t)
                    ch_v.wait(t)
                    for k in range(8):
                        t.matmul(pb[6][0:1, :], c_onec[:, 0:1], srcf(k, tt),
                                 start=(k == 0), stop=(k == 7))
                    for k in range(8):
                        ins = t.matmul(pb[7][0:1, :], c_onec[:, 0:1], t_sq[:, k, :],
                                       start=(k == 0), stop=(k == 7))
                    ch_t.tick(ins)

                @block.vector
                def _(v):
                    ch_t.wait(v)
                    v.tensor_scalar(out=r_mu[:], in0=pb[6][0:1, :], scalar1=1.0 / D,
                                    scalar2=None, op0=OP.mult)
                    v.tensor_scalar(out=r_m2[:], in0=pb[7][0:1, :], scalar1=1.0 / D,
                                    scalar2=EPS, op0=OP.mult, op1=OP.add)
                    v.tensor_mul(r_t[:], r_mu[:], r_mu[:])
                    ch_v.tick(v.tensor_sub(r_m2[:], r_m2[:], r_t[:]))

                @block.scalar
                def _(s):
                    ch_v.wait(s)
                    ch_s.tick(s.activation(out=r_sd[:], in_=r_m2[:], func=AF.Sqrt))

                @block.vector
                def _(v):
                    ch_s.wait(v)
                    ch_v.tick(v.reciprocal(r_sd[:], r_sd[:]))

                @block.tensor
                def _(t):
                    ch_v.wait(t)
                    t.matmul(pb[4][:, :], c_ones[:, :], r_mu[:, :],
                             start=True, stop=True)
                    ins = t.matmul(pb[5][:, :], c_ones[:, :], r_sd[:, :],
                                   start=True, stop=True)
                    ch_t.tick(ins)

                @block.vector
                def _(v):
                    ch_t.wait(v)
                    for k in range(8):
                        ins = dst_write(v, k, sl, tt)
                    ch_v.tick(ins)

        def x_pre(tt):
            @block.sync
            def _(sync):
                if tt > 0:
                    ch_v.wait(sync)
                ins = sync.dma_start(
                    out=t_x[:],
                    in_=x_sl[:, tt * 512:(tt + 1) * 512]
                    .rearrange("(k p) t -> p k t", p=128))
                ch_st.tick(ins, 16)

        def xn_write(v, k, sl, tt):
            v.tensor_sub(t_xn[:, k, sl], t_x[:, k, :], pb[4][:, :])
            return v.tensor_mul(t_xn[:, k, sl], t_xn[:, k, sl], pb[5][:, :])

        emit_ln(lambda k, tt: t_x[:, k, :], xn_write, pre=x_pre)

        @block.sync
        def _(sync):
            ch_v.wait(sync)
            ins = sync.dma_start(out=xn_dram.rearrange("(k p) t -> p k t", p=128),
                                 in_=t_xn[:])
            ch_st.tick(ins, 16)

        @block.vector
        def _(v):
            for k in range(8):
                ins = v.tensor_copy(out=t_xnb[:, k, :], in_=t_xn[:, k, :])
            ch_v.tick(ins)

        def h_write(v, k, sl, tt):
            v.tensor_sub(t_tmp[:, :], t_xn[:, k, sl], pb[4][:, :])
            v.tensor_mul(t_tmp[:, :], t_tmp[:, :], pb[5][:, :])
            return v.tensor_scalar(out=t_hb[:, k, sl], in0=t_tmp[:, :],
                                   scalar1=c_ln1g[:, k:k + 1],
                                   scalar2=c_ln1b[:, k:k + 1],
                                   op0=OP.mult, op1=OP.add)

        emit_ln(lambda k, tt: t_xn[:, k, tt * 512:(tt + 1) * 512], h_write)

        @block.sync
        def _(sync):
            ch_v.wait(sync)
            ins = sync.dma_start(out=ag_in.rearrange("(k p) t -> p k t", p=128),
                                 in_=t_hb[:])
            ch_st.tick(ins, 16)

        @block.gpsimd
        def _(g):
            ch_st.wait(g)
            ins = g.collective_compute(
                "AllGather", OP.bypass, replica_groups=GROUPS,
                ins=[ag_in], outs=[ag_out])
            ch_cc.tick(ins)

        for tt in range(16):
            tsl = slice(tt * 128, (tt + 1) * 128)
            bank = pb[0] if tt % 2 == 0 else pb[1]

            @block.tensor
            def _(t):
                if tt >= 2:
                    ch_s.wait(t)
                    ch_v.wait(t)
                for k in range(8):
                    t.matmul(bank[:, 0:58], t_xn[:, k, tsl], w_rtw1[:, k, :],
                             start=(k == 0), stop=False)
                ins = t.matmul(bank[:, 8:58], c_ones[:, :], c_b1row[:, :],
                               start=False, stop=True)
                ch_t.tick(ins)

            if debug:
                @block.vector
                def _(v):
                    ch_t.wait(v)
                    ch_v.tick(v.tensor_copy(out=rt_ah[:, 0:58] if False else rt_sc[:, 0, :] if False else rt_ahT[0:128, 0:58] if False else rt_ah[:, :], in_=bank[:, 8:58]))

                @block.sync
                def _(sync):
                    ch_v.wait(sync)
                    ch_st.tick(sync.dma_start(out=dbg["rtraw"][tsl, 8:58],
                                              in_=rt_ah[:, :]), 16)

                @block.vector
                def _(v):
                    ch_st.wait(v)
                    ch_v.tick(v.tensor_copy(out=rt_lg[:, :], in_=bank[:, 0:8]))

                @block.sync
                def _(sync):
                    ch_v.wait(sync)
                    ch_st.tick(sync.dma_start(out=dbg["rtraw"][tsl, 0:8],
                                              in_=rt_lg[:, :]), 16)

            @block.vector
            def _(v):
                ch_t.wait(v)
                ch_s.wait(v)
                if debug:
                    ch_st.wait(v)
                v.reduce_max(out=rt_col[:], in_=bank[:, 0:8], axis=AX.X)
                vfence(v)
                ins = v.tensor_scalar(out=rt_col[:], in0=rt_col[:], scalar1=-1.0,
                                      scalar2=None, op0=OP.mult)
                ch_rh.tick(ins)

            @block.scalar
            def _(s):
                ch_rh.wait(s)
                if debug:
                    ch_st.wait(s)
                s.activation(out=rt_probs[:], in_=bank[:, 0:8], func=AF.Exp,
                             bias=rt_col[:, 0:1])
                ins = s.activation(out=rt_ah[:], in_=bank[:, 8:58],
                                   func=AF.Gelu_apprx_tanh)
                ch_s.tick(ins)

            @block.vector
            def _(v):
                ch_s.wait(v)
                v.tensor_reduce(out=rt_col[:], in_=rt_probs[:],
                                axis=AX.X, op=OP.add)
                vfence(v)
                v.reciprocal(rt_col[:], rt_col[:])
                vfence(v)
                ins = v.tensor_scalar(out=rt_probs[:], in0=rt_probs[:],
                                      scalar1=rt_col[:, 0:1], scalar2=None,
                                      op0=OP.mult)
                ch_v.tick(ins)

            @block.tensor
            def _(t):
                ch_s.wait(t)
                ch_t.tick(t.transpose(pb[2][0:50, 0:128], rt_ah[:, :],
                                      c_ident[:, :]))

            @block.vector
            def _(v):
                ch_t.wait(v)
                ins = v.tensor_copy(out=rt_ahT[:, :], in_=pb[2][0:50, 0:128])
                ch_v.tick(ins)

            if debug and tt == 0:
                @block.sync
                def _(sync):
                    ch_v.wait(sync)
                    ch_st.tick(sync.dma_start(out=dbg["ahT"], in_=rt_ahT[:, :]), 16)

            @block.tensor
            def _(t):
                ch_v.wait(t)
                if debug and tt == 0:
                    ch_st.wait(t)
                t.matmul(pb[3][:, 0:6], rt_ahT[:, :], w_aw2[:, :],
                         start=True, stop=False)
                ins = t.matmul(pb[3][:, 0:6], c_ones[:, :], c_b2row[:, :],
                               start=False, stop=True)
                ch_t.tick(ins)

            if debug:
                @block.vector
                def _(v):
                    ch_t.wait(v)
                    ch_st.wait(v)
                    ch_v.tick(v.tensor_copy(out=rt_dbg6[:, :], in_=pb[3][:, 0:6]))

                @block.sync
                def _(sync):
                    ch_v.wait(sync)
                    ch_st.tick(sync.dma_start(out=dbg["praw"][tsl, :],
                                              in_=rt_dbg6[:, :]), 16)

            @block.vector
            def _(v):
                ch_t.wait(v)
                if debug:
                    ch_st.wait(v)
                v.memset(rt_lg[:, 6:8], -1e30)
                v.tensor_copy(out=rt_eq6[:, :], in_=pb[3][:, 0:6])
                vfence(v)
                v.tensor_scalar(out=rt_lg[:, 0:6], in0=rt_eq6[:, :],
                                scalar1=-30.0, scalar2=30.0,
                                op0=OP.max, op1=OP.min)
                vfence(v)
                if debug:
                    v.tensor_copy(out=rt_dbg6[:, :], in_=rt_lg[:, 0:6])
                v.reduce_max(out=rt_col[:], in_=rt_lg[:, :], axis=AX.X)
                vfence(v)
                v.tensor_scalar(out=rt_srt[:, :], in0=rt_lg[:, :],
                                scalar1=rt_col[:, 0:1], scalar2=None,
                                op0=OP.is_ge)
                vfence(v)
                v.tensor_mul(rt_srt[:, :], rt_srt[:, :], c_rev8[:, :])
                vfence(v)
                v.tensor_reduce(out=rt_col2[:], in_=rt_srt[:, :],
                                axis=AX.X, op=OP.max)
                vfence(v)
                v.tensor_scalar(out=rt_col2[:], in0=rt_col2[:],
                                scalar1=-1.0, scalar2=7.0,
                                op0=OP.mult, op1=OP.add)
                vfence(v)
                if debug and tt == 0:
                    v.tensor_copy(out=dbg_s1t[:, :], in_=rt_eq6[:, :])
                    v.tensor_copy(out=dbg_s2t[:, :], in_=rt_lg[:, :])
                    v.tensor_copy(out=dbg_s3t[:, :], in_=rt_srt[:, :])
                    ch_v.tick(v.tensor_copy(out=dbg_s4t[:, :], in_=rt_col2[:, :]))
                v.max(out=rt_srt[:, :], in_=rt_probs[:, :])
                vfence(v)
                v.tensor_scalar(out=rt_eq6[:, :], in0=c_iota6[:, :],
                                scalar1=rt_col2[:, 0:1], scalar2=None,
                                op0=OP.is_equal)
                vfence(v)
                v.tensor_mul(rt_eq6[:, :], rt_eq6[:, :], rt_srt[:, 0:6])
                vfence(v)
                v.tensor_reduce(out=rt_col[:], in_=rt_eq6[:, :], axis=AX.X,
                                op=OP.add)
                vfence(v)
                v.tensor_scalar(out=rt_sc[:, tt, :], in0=rt_probs[:, :],
                                scalar1=rt_col[:, 0:1], scalar2=None,
                                op0=OP.is_ge)
                vfence(v)
                ins = v.tensor_mul(rt_sc[:, tt, :], rt_sc[:, tt, :],
                                   rt_probs[:, :])
                ch_v.tick(ins)

            if debug:
                @block.sync
                def _(sync):
                    ch_v.wait(sync)
                    ch_st.tick(sync.dma_start(out=dbg["probs"][tsl, :],
                                              in_=rt_probs[:, :]), 16)
                    ch_st.tick(sync.dma_start(out=dbg["kidx"][tsl, :],
                                              in_=rt_col2[:, :]), 16)
                    ch_st.tick(sync.dma_start(out=dbg["scores"][tsl, :],
                                              in_=rt_sc[:, tt, :]), 16)
                    ch_st.tick(sync.dma_start(out=dbg["alog2"][tsl, :],
                                              in_=rt_dbg6[:, :]), 16)
                    if tt == 0:
                        ch_st.tick(sync.dma_start(out=dbg["st1"], in_=dbg_s1t[:, :]), 16)
                        ch_st.tick(sync.dma_start(out=dbg["st2"], in_=dbg_s2t[:, :]), 16)
                        ch_st.tick(sync.dma_start(out=dbg["st3"], in_=dbg_s3t[:, :]), 16)
                        ch_st.tick(sync.dma_start(out=dbg["st4"], in_=dbg_s4t[:, :]), 16)

        for tt in range(16):
            bank = pb[2]

            @block.tensor
            def _(t):
                if tt > 0:
                    ch_st.wait(t)
                ins = t.transpose(bank[0:8, 0:128], rt_sc[:, tt, :], c_ident[:, :])
                ch_t.tick(ins)

            @block.vector
            def _(v):
                ch_t.wait(v)
                ins = v.tensor_copy(out=rt_scrow[:, :], in_=bank[0:8, 0:128])
                ch_v.tick(ins)

            @block.sync
            def _(sync):
                ch_v.wait(sync)
                ins = sync.dma_start(out=sc_dram[:, tt * 128:(tt + 1) * 128],
                                     in_=rt_scrow[:, :])
                ch_st.tick(ins, 16)

        if debug:
            @block.sync
            def _(sync):
                ch_v.wait(sync)
                ch_cc.wait(sync)
                ch_st.tick(sync.dma_start(
                    out=dbg["xn"].rearrange("(k p) t -> p k t", p=128),
                    in_=t_xn[:]), 16)
                ch_st.tick(sync.dma_start(
                    out=dbg["h"].rearrange("(k p) t -> p k t", p=128),
                    in_=t_hb[:]), 16)
                ch_st.tick(sync.dma_start(out=dbg["hfull"], in_=ag_out), 16)

        for blk in range(4):
            @block.sync
            def _(sync):
                if blk == 0:
                    ch_cc.wait(sync)
                    ch_v.wait(sync)
                    ch_s.wait(sync)
                else:
                    ch_v.wait(sync)
                for k in range(8):
                    ch_st.tick(sync.dma_start(
                        out=p2_h[:, k, :],
                        in_=ag_out[blk * D + k * 128:blk * D + (k + 1) * 128, :]), 16)

            h_ready = ch_st.n
            for m in range(8):
                wsl = wb_inxc if m < 4 else wb_inz
                mm = m % 4

                for tt in range(4):
                    gi = (m * 4 + tt) % 2
                    bank = pb[gi]

                    @block.tensor
                    def _(t):
                        if blk == 0 and m == 0 and tt == 0:
                            ch_g.wait(t)
                        if m == 0 and tt == 0:
                            t.wait_ge(ch_st.h, h_ready)
                        slot_need(("bank", gi), t)
                        for k in range(8):
                            ins = t.matmul(
                                bank[:, :],
                                wsl[:, k, mm * 128:(mm + 1) * 128],
                                p2_h[:, k, tt * 512:(tt + 1) * 512],
                                start=(k == 0), stop=(k == 7))
                        ch_t.tick(ins)
                    xz_done = ch_t.n

                    if m < 4:
                        @block.vector
                        def _(v):
                            v.wait_ge(ch_t.h, xz_done)
                            ins = v.tensor_copy(
                                out=p2_xc[:, mm, tt * 512:(tt + 1) * 512],
                                in_=bank[:, :])
                            ch_v.tick(ins)
                        slot_done(("bank", gi), ch_v)
                    else:
                        @block.scalar
                        def _(s_):
                            s_.wait_ge(ch_t.h, xz_done)
                            slot_need(("p2sz", tt % 2), s_)
                            ins = s_.activation(out=p2_szb[:, tt % 2, :],
                                                in_=bank[:, :], func=AF.Silu)
                            ch_s.tick(ins)
                        slot_done(("bank", gi), ch_s)
                        sz_done = ch_s.n

                        @block.sync
                        def _(sync):
                            sync.wait_ge(ch_s.h, sz_done)
                            ins = sync.dma_start(
                                out=sz_dram[mm * 128:(mm + 1) * 128,
                                            blk * 2048 + tt * 512:
                                            blk * 2048 + (tt + 1) * 512],
                                in_=p2_szb[:, tt % 2, :])
                            ch_st.tick(ins, 16)
                        slot_done(("p2sz", tt % 2), ch_st)

            for d in range(4):
                @block.vector
                def _(v):
                    ch_v.wait(v)
                    v.tensor_scalar(out=p2_xcs[:, d, :], in0=p2_xc[:, d, :],
                                    scalar1=c_convw[:, d * 4 + 3:d * 4 + 4],
                                    scalar2=None, op0=OP.mult)
                    for kk in range(3):
                        sh = 3 - kk
                        v.scalar_tensor_tensor(
                            out=p2_xcs[:, d, sh:2048],
                            in0=p2_xc[:, d, 0:2048 - sh],
                            scalar=c_convw[:, d * 4 + kk:d * 4 + kk + 1],
                            in1=p2_xcs[:, d, sh:2048],
                            op0=OP.mult, op1=OP.add)
                        if blk > 0:
                            v.scalar_tensor_tensor(
                                out=p2_xcs[:, d, 0:sh],
                                in0=p2_tail[:, d, 3 - sh:3],
                                scalar=c_convw[:, d * 4 + kk:d * 4 + kk + 1],
                                in1=p2_xcs[:, d, 0:sh],
                                op0=OP.mult, op1=OP.add)
                    ins = v.tensor_copy(out=p2_tail[:, d, 0:3],
                                        in_=p2_xc[:, d, 2045:2048])
                    ch_v.tick(ins)

                @block.scalar
                def _(s_):
                    ch_v.wait(s_)
                    ins = s_.activation(out=p2_xcs[:, d, :], in_=p2_xcs[:, d, :],
                                        func=AF.Silu,
                                        bias=c_convb[:, d:d + 1])
                    ch_s.tick(ins)

                @block.vector
                def _(v):
                    ch_s.wait(v)
                    ins = v.tensor_copy(out=p2_xcsb[:, d, :], in_=p2_xcs[:, d, :])
                    ch_v.tick(ins)

                @block.sync
                def _(sync):
                    ch_v.wait(sync)
                    ins = sync.dma_start(
                        out=xcs_dram[d * 128:(d + 1) * 128,
                                     blk * 2048:(blk + 1) * 2048],
                        in_=p2_xcsb[:, d, :])
                    ch_st.tick(ins, 16)

            xcsb_ready = ch_v.n
            for tt in range(4):
                @block.tensor
                def _(t):
                    if tt == 0:
                        t.wait_ge(ch_v.h, xcsb_ready)
                    slot_need(("bank", 2), t)
                    for d in range(4):
                        ins = t.matmul(
                            pb[2][0:96, :],
                            wb_x[:, d, :],
                            p2_xcsb[:, d, tt * 512:(tt + 1) * 512],
                            start=(d == 0), stop=(d == 3))
                    ch_t.tick(ins)
                dbc_done = ch_t.n

                @block.vector
                def _(v):
                    v.wait_ge(ch_t.h, dbc_done)
                    slot_need(("p2db", tt % 2), v)
                    ins = v.tensor_copy(out=p2_dbst[:, tt % 2, :],
                                        in_=pb[2][0:96, :])
                    ch_v.tick(ins)
                slot_done(("bank", 2), ch_v)
                db_done = ch_v.n

                @block.sync
                def _(sync):
                    sync.wait_ge(ch_v.h, db_done)
                    ins = sync.dma_start(
                        out=ar_in[:, blk * 2048 + tt * 512:
                                  blk * 2048 + (tt + 1) * 512],
                        in_=p2_dbst[:, tt % 2, :])
                    ch_st.tick(ins, 16)
                slot_done(("p2db", tt % 2), ch_st)

        @block.gpsimd
        def _(g):
            ch_st.wait(g)
            ins = g.collective_compute(
                "AllReduce", OP.add, replica_groups=GROUPS,
                ins=[ar_in], outs=[ar_out])
            ch_cc.tick(ins)

        if debug:
            @block.gpsimd
            def _(g):
                ch_cc.wait(g)
                ch_g.tick(g.dma_start(out=dbg["xcs"], in_=xcs_dram), 16)
                ch_g.tick(g.dma_start(out=dbg["sz"], in_=sz_dram), 16)
                ch_g.tick(g.dma_start(out=dbg["dbc"], in_=ar_out), 16)

        @block.gpsimd
        def _(g):
            ch_cc.wait(g)
            ch_g.tick(g.dma_start(out=p3_lo[:], in_=ar_out[0:64, :]), 16)

        @block.vector
        def _(v):
            ch_v.tick(v.memset(p3_carry[:], 0.0))

        bc_ready = {}
        for qt in range(4):
            for d in range(4):
                for tt in range(4):
                    csl = slice(qt * 2048 + tt * 512, qt * 2048 + (tt + 1) * 512)

                    @block.tensor
                    def _(t):
                        if qt == 0 and d == 0 and tt == 0:
                            ch_g.wait(t)
                        slot_need(("bank", 3), t)
                        ins = t.matmul(pb[3][:, :],
                                       wb_dt[:, d * 128:(d + 1) * 128],
                                       p3_lo[:, csl], start=True, stop=True)
                        ch_t.tick(ins)

                    @block.scalar
                    def _(s_):
                        ch_t.wait(s_)
                        ins = s_.activation(out=p3_t1[:, :], in_=pb[3][:, :],
                                            func=AF.Sigmoid,
                                            bias=c_bdt[:, d:d + 1])
                        ch_s.tick(ins)
                    slot_done(("bank", 3), ch_s)

                    @block.vector
                    def _(v):
                        ch_s.wait(v)
                        ins = v.tensor_scalar(out=p3_t2[:, :], in0=p3_t1[:, :],
                                              scalar1=-1.0, scalar2=1.0,
                                              op0=OP.mult, op1=OP.add)
                        ch_v.tick(ins)

                    @block.scalar
                    def _(s_):
                        ch_v.wait(s_)
                        ins = s_.activation(out=p3_t2[:, :], in_=p3_t2[:, :],
                                            func=AF.Ln)
                        ch_s.tick(ins)

                    @block.vector
                    def _(v):
                        ch_s.wait(v)
                        ins = v.tensor_scalar(
                            out=p3_dt[:, tt * 512:(tt + 1) * 512],
                            in0=p3_t2[:, :], scalar1=-1.0, scalar2=None,
                            op0=OP.mult)
                        ch_v.tick(ins)

                @block.sync
                def _(sync):
                    if qt > 0 or d > 0:
                        ch_v.wait(sync)
                    ins = sync.dma_start(
                        out=p3_xcs[:, :],
                        in_=xcs_dram[d * 128:(d + 1) * 128,
                                     qt * 2048:(qt + 1) * 2048])
                    ch_st.tick(ins, 16)

                @block.vector
                def _(v):
                    ch_st.wait(v)
                    ins = v.tensor_mul(p3_dtx[:, :], p3_dt[:, :], p3_xcs[:, :])
                    ch_v.tick(ins)
                dt_ready = ch_v.n
                a_ready = {}

                if debug:
                    @block.sync
                    def _(sync):
                        ch_v.wait(sync)
                        ins = sync.dma_start(
                            out=dtd_dram[d * 128:(d + 1) * 128,
                                         qt * 2048:(qt + 1) * 2048],
                            in_=p3_dt[:, :])
                        ch_st.tick(ins, 16)

                for n in range(16):
                    if (d, n) not in bc_ready or True:
                        pass
                    if n == 0:
                        @block.gpsimd
                        def _(g):
                            if qt > 0 or d > 0:
                                ch_v.wait(g)
                            g.dma_start(out=p3_Bbc[:, 0, :], in_=ar_out[
                                64 + 0:65 + 0, qt * 2048:(qt + 1) * 2048]
                                .partition_broadcast(128)).then_inc(ch_g.h, 16)
                            ch_g.n += 16
                            ins = g.dma_start(out=p3_Cbc[:, 0, :], in_=ar_out[
                                80 + 0:81 + 0, qt * 2048:(qt + 1) * 2048]
                                .partition_broadcast(128))
                            ch_g.tick(ins, 16)
                    bc_ready[n] = ch_g.n

                    if n + 1 < 16:
                        @block.gpsimd
                        def _(g):
                            if qt > 0 or d > 0 or n >= 1:
                                ch_v.wait(g)
                            g.dma_start(out=p3_Bbc[:, (n + 1) % 2, :],
                                        in_=ar_out[64 + n + 1:65 + n + 1,
                                                   qt * 2048:(qt + 1) * 2048]
                                        .partition_broadcast(128)).then_inc(ch_g.h, 16)
                            ch_g.n += 16
                            ins = g.dma_start(out=p3_Cbc[:, (n + 1) % 2, :],
                                              in_=ar_out[80 + n + 1:81 + n + 1,
                                                         qt * 2048:(qt + 1) * 2048]
                                              .partition_broadcast(128))
                            ch_g.tick(ins, 16)

                    @block.scalar
                    def _(s_):
                        s_.wait_ge(ch_v.h, dt_ready)
                        slot_need(("p3a", n % 2), s_)
                        ins = s_.activation(out=p3_a[:, n % 2, :],
                                            in_=p3_dt[:, :],
                                            func=AF.Exp,
                                            scale=c_acol[:, d * 16 + n:
                                                         d * 16 + n + 1])
                        ch_s.tick(ins)
                    a_ready[n] = ch_s.n

                    @block.vector
                    def _(v):
                        ch_g.wait(v, bc_ready[n])
                        v.wait_ge(ch_s.h, a_ready[n])
                        v.tensor_mul(p3_u[:, :], p3_dtx[:, :],
                                     p3_Bbc[:, n % 2, :])
                        vfence(v)
                        ini = 0.0 if qt == 0 else p3_carry[:, d * 16 + n:
                                                           d * 16 + n + 1]
                        v.tensor_tensor_scan(out=p3_hh[:, :],
                                             data0=p3_a[:, n % 2, :],
                                             data1=p3_u[:, :], initial=ini,
                                             op0=OP.mult, op1=OP.add)
                        vfence(v)
                        v.tensor_copy(out=p3_carry[:, d * 16 + n:d * 16 + n + 1],
                                      in_=p3_hh[:, 2047:2048])
                        if n == 0:
                            ins = v.tensor_mul(p3_y[:, :], p3_hh[:, :],
                                               p3_Cbc[:, n % 2, :])
                        else:
                            v.tensor_mul(p3_u[:, :], p3_hh[:, :],
                                         p3_Cbc[:, n % 2, :])
                            vfence(v)
                            ins = v.tensor_add(p3_y[:, :], p3_y[:, :], p3_u[:, :])
                        ch_v.tick(ins)
                    slot_done(("p3a", n % 2), ch_v)

                if debug:
                    @block.sync
                    def _(sync):
                        ch_v.wait(sync)
                        ins = sync.dma_start(
                            out=ysd_dram[d * 128:(d + 1) * 128,
                                         qt * 2048:(qt + 1) * 2048],
                            in_=p3_y[:, :])
                        ch_st.tick(ins, 16)

                @block.sync
                def _(sync):
                    if qt > 0 or d > 0:
                        ch_v.wait(sync)
                    ins = sync.dma_start(
                        out=p3_sz[:, :],
                        in_=sz_dram[d * 128:(d + 1) * 128,
                                    qt * 2048:(qt + 1) * 2048])
                    ch_st.tick(ins, 16)

                @block.vector
                def _(v):
                    ch_st.wait(v)
                    v.scalar_tensor_tensor(out=p3_y[:, :], in0=p3_xcs[:, :],
                                           scalar=c_dskip[:, d:d + 1],
                                           in1=p3_y[:, :], op0=OP.mult,
                                           op1=OP.add)
                    vfence(v)
                    ins = v.tensor_mul(p3_yf[:, d, :], p3_y[:, :], p3_sz[:, :])
                    ch_v.tick(ins)

            yf_ready = ch_v.n
            for m in range(8):
                for tt in range(4):
                    gi = (m * 4 + tt) % 2
                    bank = pb[3 + gi]

                    @block.tensor
                    def _(t):
                        if m == 0 and tt == 0:
                            t.wait_ge(ch_v.h, yf_ready)
                        slot_need(("bank", 3 + gi), t)
                        for d in range(4):
                            ins = t.matmul(
                                bank[:, :],
                                wb_out[:, d, m * 128:(m + 1) * 128],
                                p3_yf[:, d, tt * 512:(tt + 1) * 512],
                                start=(d == 0), stop=(d == 3))
                        ch_t.tick(ins)
                    mm_done = ch_t.n

                    @block.vector
                    def _(v):
                        v.wait_ge(ch_t.h, mm_done)
                        slot_need(("p3wo", gi), v)
                        ins = v.tensor_copy(out=p3_wo[:, gi, :],
                                            in_=bank[:, :])
                        ch_v.tick(ins)
                    slot_done(("bank", 3 + gi), ch_v)
                    ev_done = ch_v.n

                    @block.sync
                    def _(sync):
                        sync.wait_ge(ch_v.h, ev_done)
                        ins = sync.dma_start(
                            out=rs_in[qt * D + m * 128:qt * D + (m + 1) * 128,
                                      tt * 512:(tt + 1) * 512],
                            in_=p3_wo[:, gi, :])
                        ch_st.tick(ins, 16)
                    slot_done(("p3wo", gi), ch_st)

        @block.gpsimd
        def _(g):
            ch_st.wait(g)
            ins = g.collective_compute(
                "ReduceScatter", OP.add, replica_groups=GROUPS,
                ins=[rs_in], outs=[rs_out])
            ch_cc.tick(ins)

        if debug:
            @block.gpsimd
            def _(g):
                ch_cc.wait(g)
                ch_g.tick(g.dma_start(out=dbg["dt"], in_=dtd_dram), 16)
                ch_g.tick(g.dma_start(out=dbg["ys"], in_=ysd_dram), 16)
                ch_g.tick(g.dma_start(out=dbg["mamba"], in_=rs_out), 16)

        IH = [(0, 10), (10, 11)]
        for e in range(NE):
            @block.gpsimd
            def _(g):
                if e == 0:
                    ch_v.wait(g)
                else:
                    ch_t.wait(g)
                    ch_v.wait(g)
                g.dma_start(out=p4_scb[:, :],
                            in_=sc_dram[e:e + 1, :].partition_broadcast(128)
                            ).then_inc(ch_g.h, 16)
                ch_g.n += 16
                ins = g.dma_start(
                    out=p4_dw[:, :, :],
                    in_=down_wT[e].rearrange("(c p) m -> p c m", p=128))
                ch_g.tick(ins, 16)
            scb_ready = ch_g.n

            for ih, (ib, ihc) in enumerate(IH):
                @block.gpsimd
                def _(g):
                    if ih > 0:
                        ch_t.wait(g)
                    g.dma_start(
                        out=p4_gw[:, :, 0:ihc * 128],
                        in_=gate_wT[e][:, ib * 128:(ib + ihc) * 128]
                        .rearrange("(k p) m -> p k m", p=128)).then_inc(ch_g.h, 16)
                    ch_g.n += 16
                    ins = g.dma_start(
                        out=p4_uw[:, :, 0:ihc * 128],
                        in_=up_wT[e][:, ib * 128:(ib + ihc) * 128]
                        .rearrange("(k p) m -> p k m", p=128))
                    ch_g.tick(ins, 16)

                w_ready = ch_g.n
                for tt in range(4):
                    tsl = slice(tt * 512, (tt + 1) * 512)
                    for ic in range(ihc):
                        gb = ic % 2
                        sb0, sb1 = 2 * gb, 2 * gb + 1

                        @block.tensor
                        def _(t):
                            t.wait_ge(ch_g.h, w_ready)
                            slot_need(("bank", 0 + 2 * gb), t)
                            slot_need(("bank", 1 + 2 * gb), t)
                            slot_need(("prod", ic), t)
                            for k in range(8):
                                t.matmul(pb[0 + 2 * gb][:, :],
                                         p4_gw[:, k, ic * 128:(ic + 1) * 128],
                                         t_xnb[:, k, tsl],
                                         start=(k == 0), stop=(k == 7))
                            for k in range(8):
                                ins = t.matmul(
                                    pb[1 + 2 * gb][:, :],
                                    p4_uw[:, k, ic * 128:(ic + 1) * 128],
                                    t_xnb[:, k, tsl],
                                    start=(k == 0), stop=(k == 7))
                            ch_t.tick(ins)
                        gu_done = ch_t.n

                        @block.scalar
                        def _(s_):
                            s_.wait_ge(ch_t.h, gu_done)
                            slot_need(("p4sb", sb0), s_)
                            ins = s_.activation(out=p4_sb[:, sb0, :],
                                                in_=pb[0 + 2 * gb][:, :],
                                                func=AF.Silu)
                            ch_s.tick(ins)
                        slot_done(("bank", 0 + 2 * gb), ch_s)
                        g_done = ch_s.n

                        @block.vector
                        def _(v):
                            v.wait_ge(ch_s.h, g_done)
                            slot_need(("p4sb", sb1), v)
                            v.tensor_copy(out=p4_sb[:, sb1, :],
                                          in_=pb[1 + 2 * gb][:, :])
                            vfence(v)
                            ins = v.tensor_mul(p4_prod[:, ic, :],
                                               p4_sb[:, sb0, :],
                                               p4_sb[:, sb1, :])
                            ch_v.tick(ins)
                        slot_done(("bank", 1 + 2 * gb), ch_v)
                        slot_done(("p4sb", sb0), ch_v)
                        slot_done(("p4sb", sb1), ch_v)
                    prod_ready = ch_v.n

                    for m in range(8):
                        db = 4 + m % 2

                        @block.tensor
                        def _(t):
                            if m == 0:
                                t.wait_ge(ch_v.h, prod_ready)
                            slot_need(("bank", db), t)
                            for ic in range(ihc):
                                ins = t.matmul(
                                    pb[db][:, :],
                                    p4_dw[:, ib + ic, m * 128:(m + 1) * 128],
                                    p4_prod[:, ic, :],
                                    start=(ic == 0), stop=(ic == ihc - 1))
                            ch_t.tick(ins)
                        dn_done = ch_t.n
                        if m == 7:
                            for icx in range(ihc):
                                slot_done(("prod", icx), ch_t)

                        @block.vector
                        def _(v):
                            v.wait_ge(ch_t.h, dn_done)
                            v.wait_ge(ch_g.h, scb_ready)
                            slot_need(("p4ds", m % 2), v)
                            ins = v.tensor_mul(p4_ds[:, m % 2, :],
                                               pb[db][:, :],
                                               p4_scb[:, tsl])
                            ch_v.tick(ins)
                        slot_done(("bank", db), ch_v)
                        ds_done = ch_v.n

                        @block.gpsimd
                        def _(g):
                            g.wait_ge(ch_v.h, ds_done)
                            ins = g.dma_start(
                                out=moe_dram[m * 128:(m + 1) * 128, tsl],
                                in_=p4_ds[:, m % 2, :],
                                accum_op=OP.add)
                            ch_g.tick(ins, 16)
                        slot_done(("p4ds", m % 2), ch_g)

        if debug:
            @block.gpsimd
            def _(g):
                ch_g.wait(g)
                ch_g.tick(g.dma_start(out=dbg["moe"], in_=moe_dram), 16)

        for tq in range(4):
            hsl = slice(tq * 512, (tq + 1) * 512)

            for k in range(8):
                @block.sync
                def _(sync):
                    if tq == 0 and k == 0:
                        ch_cc.wait(sync)
                    ch_v.wait(sync)
                    ch_st.tick(sync.dma_start(
                        out=p5_xm[:, k, :],
                        in_=xn_dram[k * 128:(k + 1) * 128, hsl]), 16)
                    ins = sync.dma_start(out=p5_xt[:, :],
                                         in_=rs_out[k * 128:(k + 1) * 128, hsl])
                    ch_st.tick(ins, 16)

                @block.vector
                def _(v):
                    ch_st.wait(v)
                    ins = v.tensor_add(p5_xm[:, k, :], p5_xm[:, k, :],
                                       p5_xt[:, :])
                    ch_v.tick(ins)

            @block.scalar
            def _(s_):
                ch_v.wait(s_)
                for k in range(8):
                    ins = s_.activation(out=p5_sq[:, k, :], in_=p5_xm[:, k, :],
                                        func=AF.Square)
                ch_s.tick(ins)

            @block.tensor
            def _(t):
                ch_s.wait(t)
                ch_v.wait(t)
                for k in range(8):
                    t.matmul(pb[6][0:1, :], c_onec[:, 0:1], p5_xm[:, k, :],
                             start=(k == 0), stop=(k == 7))
                for k in range(8):
                    ins = t.matmul(pb[7][0:1, :], c_onec[:, 0:1], p5_sq[:, k, :],
                                   start=(k == 0), stop=(k == 7))
                ch_t.tick(ins)

            @block.vector
            def _(v):
                ch_t.wait(v)
                v.tensor_scalar(out=r_mu[:], in0=pb[6][0:1, :],
                                scalar1=1.0 / D, scalar2=None, op0=OP.mult)
                v.tensor_scalar(out=r_m2[:], in0=pb[7][0:1, :],
                                scalar1=1.0 / D, scalar2=EPS,
                                op0=OP.mult, op1=OP.add)
                vfence(v)
                v.tensor_mul(r_t[:], r_mu[:], r_mu[:])
                vfence(v)
                ch_v.tick(v.tensor_sub(r_m2[:], r_m2[:], r_t[:]))

            @block.scalar
            def _(s_):
                ch_v.wait(s_)
                ch_s.tick(s_.activation(out=r_sd[:], in_=r_m2[:], func=AF.Sqrt))

            @block.vector
            def _(v):
                ch_s.wait(v)
                ch_v.tick(v.reciprocal(r_sd[:], r_sd[:]))

            @block.tensor
            def _(t):
                ch_v.wait(t)
                t.matmul(pb[4][:, :], c_ones[:, :], r_mu[:, :],
                         start=True, stop=True)
                ins = t.matmul(pb[5][:, :], c_ones[:, :], r_sd[:, :],
                               start=True, stop=True)
                ch_t.tick(ins)

            @block.vector
            def _(v):
                ch_t.wait(v)
                for k in range(8):
                    v.tensor_sub(t_tmp[:, :], p5_xm[:, k, :], pb[4][:, :])
                    v.tensor_mul(t_tmp[:, :], t_tmp[:, :], pb[5][:, :])
                    ins = v.tensor_scalar(out=p5_h2[:, k, :], in0=t_tmp[:, :],
                                          scalar1=c_ln2g[:, k:k + 1],
                                          scalar2=c_ln2b[:, k:k + 1],
                                          op0=OP.mult, op1=OP.add)
                ch_v.tick(ins)

            h2_ready = ch_v.n
            w1_ready = {}
            for mg in range(8):
                wslot = mg % 2

                @block.gpsimd
                def _(g):
                    slot_need(("p5w1", wslot), g)
                    g.dma_start(
                        out=p5_w1[:, wslot, :, 0:512],
                        in_=w_ff1[:, mg * 512:(mg + 1) * 512]
                        .rearrange("(k p) m -> p k m", p=128)).then_inc(ch_g.h, 16)
                    ch_g.n += 16
                    ins = g.dma_start(
                        out=p5_w1[:, wslot, :, 512:1024],
                        in_=w_ff1[:, FFI + mg * 512:FFI + (mg + 1) * 512]
                        .rearrange("(k p) m -> p k m", p=128))
                    ch_g.tick(ins, 16)
                w1_ready[mg] = ch_g.n

                for sm in range(4):
                    ab = sm % 2

                    @block.tensor
                    def _(t):
                        t.wait_ge(ch_g.h, w1_ready[mg])
                        if mg == 0 and sm == 0:
                            t.wait_ge(ch_v.h, h2_ready)
                        slot_need(("bank", 0 + 2 * ab), t)
                        slot_need(("bank", 1 + 2 * ab), t)
                        for k in range(8):
                            t.matmul(pb[0 + 2 * ab][:, :],
                                     p5_w1[:, wslot, k, sm * 128:(sm + 1) * 128],
                                     p5_h2[:, k, :],
                                     start=(k == 0), stop=(k == 7))
                        for k in range(8):
                            ins = t.matmul(
                                pb[1 + 2 * ab][:, :],
                                p5_w1[:, wslot, k,
                                      512 + sm * 128:512 + (sm + 1) * 128],
                                p5_h2[:, k, :],
                                start=(k == 0), stop=(k == 7))
                        ch_t.tick(ins)
                    if sm == 3:
                        slot_done(("p5w1", wslot), ch_t)
                    mm_done = ch_t.n

                    mi = mg * 4 + sm

                    @block.scalar
                    def _(s_):
                        s_.wait_ge(ch_t.h, mm_done)
                        slot_need(("p5g",), s_)
                        s_.activation(out=p5_ag[:, mi, :], in_=pb[0 + 2 * ab][:, :],
                                      func=AF.Identity,
                                      bias=c_bff1[:, mi:mi + 1])
                        ins = s_.activation(out=p5_g[:, :], in_=pb[1 + 2 * ab][:, :],
                                            func=AF.Gelu_apprx_tanh,
                                            bias=c_bff1[:, 32 + mi:33 + mi])
                        ch_s.tick(ins)
                    slot_done(("bank", 0 + 2 * ab), ch_s)
                    slot_done(("bank", 1 + 2 * ab), ch_s)
                    sg_done = ch_s.n

                    @block.vector
                    def _(v):
                        v.wait_ge(ch_s.h, sg_done)
                        ins = v.tensor_mul(p5_ag[:, mi, :], p5_ag[:, mi, :],
                                           p5_g[:, :])
                        ch_v.tick(ins)
                    slot_done(("p5g",), ch_v)

            ag_ready = ch_v.n
            for mp in range(2):
                @block.gpsimd
                def _(g):
                    slot_need(("p5w2full",), g)
                    ins = g.dma_start(
                        out=p5_w2[:, :, :],
                        in_=w_ff2[:, mp * 512:(mp + 1) * 512]
                        .rearrange("(c p) m -> p c m", p=128))
                    ch_g.tick(ins, 16)
                w2r = ch_g.n

                for kc in range(32):
                    @block.tensor
                    def _(t):
                        if kc == 0:
                            t.wait_ge(ch_g.h, w2r)
                            t.wait_ge(ch_v.h, ag_ready)
                            for m4 in range(4):
                                slot_need(("bank", m4), t)
                        for m4 in range(4):
                            ins = t.matmul(
                                pb[m4][:, :],
                                p5_w2[:, kc, m4 * 128:(m4 + 1) * 128],
                                p5_ag[:, kc, :],
                                start=(kc == 0), stop=(kc == 31))
                        if kc == 31:
                            ch_t.tick(ins)

                ff2_done = ch_t.n
                slot_done(("p5w2full",), ch_t)

                @block.vector
                def _(v):
                    v.wait_ge(ch_t.h, ff2_done)
                    for m4 in range(4):
                        mi = mp * 4 + m4
                        ins = v.scalar_tensor_tensor(
                            out=p5_xm[:, mi, :], in0=pb[m4][:, :],
                            scalar=c_bff2[:, mi:mi + 1],
                            in1=p5_xm[:, mi, :], op0=OP.add, op1=OP.add)
                    ch_v.tick(ins)
                for m4 in range(4):
                    slot_done(("bank", m4), ch_v)

            if debug:
                @block.sync
                def _(sync):
                    ch_v.wait(sync)
                    for k in range(8):
                        ch_st.tick(sync.dma_start(
                            out=dbg["attn"][k * 128:(k + 1) * 128, hsl],
                            in_=p5_xm[:, k, :]), 16)

            for k in range(8):
                @block.sync
                def _(sync):
                    if tq == 0 and k == 0:
                        ch_g.wait(sync)
                    ch_v.wait(sync)
                    if debug:
                        ch_st.wait(sync)
                    ch_st.tick(sync.dma_start(
                        out=p5_xt[:, :], in_=x_sl[k * 128:(k + 1) * 128, hsl]),
                        16)
                    ins = sync.dma_start(
                        out=p5_mt[:, :], in_=moe_dram[k * 128:(k + 1) * 128, hsl])
                    ch_st.tick(ins, 16)

                @block.vector
                def _(v):
                    ch_st.wait(v)
                    v.tensor_add(p5_xt[:, :], p5_xm[:, k, :], p5_xt[:, :])
                    vfence(v)
                    ins = v.tensor_add(p5_mt[:, :], p5_mt[:, :], p5_xt[:, :])
                    ch_v.tick(ins)

                @block.sync
                def _(sync):
                    ch_v.wait(sync)
                    ins = sync.dma_start(
                        out=out_fm[k * 128:(k + 1) * 128, hsl], in_=p5_mt[:, :])
                    ch_st.tick(ins, 16)

        @block.sync
        def _(sync):
            ch_st.wait(sync)
            ch_g.wait(sync)
            ch_cc.wait(sync)

    return nc, dbg


def _prep_inputs(inputs):
    f32 = np.float32
    x = inputs["L0_attn_x"]
    w_in = inputs["L0_attn_w_in"]
    conv_w = inputs["L0_attn_conv_w"]
    conv_b = inputs["L0_attn_conv_b"]
    w_x = inputs["L0_attn_w_x"]
    w_dt = inputs["L0_attn_w_dt"]
    b_dt = inputs["L0_attn_b_dt"]
    a_log = inputs["L0_attn_a_log"]
    d_skip = inputs["L0_attn_d_skip"]
    w_out = inputs["L0_attn_w_out"]

    def chcol(v, q):
        return np.ascontiguousarray(
            v[q * CH:(q + 1) * CH].reshape(4, 128).T).astype(f32)

    def dcol(v):
        return np.ascontiguousarray(v.reshape(8, 128).T).astype(f32)

    rtw1 = np.ascontiguousarray(
        np.concatenate([inputs["L0_ffn_router_w"].T,
                        inputs["L0_ffn_actor_w1"].T], axis=1)).astype(f32)
    gate_wT = np.ascontiguousarray(
        inputs["L0_ffn_gate_w"].transpose(0, 2, 1)).astype(f32)
    up_wT = np.ascontiguousarray(
        inputs["L0_ffn_up_w"].transpose(0, 2, 1)).astype(f32)
    down_wT = np.ascontiguousarray(
        inputs["L0_ffn_down_w"].transpose(0, 2, 1)).astype(f32)

    shared = dict(
        ln1g=dcol(inputs["L0_attn_ln1_g"]),
        ln1b=dcol(inputs["L0_attn_ln1_b"]),
        ln2g=dcol(inputs["L0_attn_ln2_g"]),
        ln2b=dcol(inputs["L0_attn_ln2_b"]),
        w_ff1=np.ascontiguousarray(inputs["L0_attn_w_ff1"]).astype(f32),
        b_ff1=np.ascontiguousarray(
            inputs["L0_attn_b_ff1"].reshape(64, 128).T).astype(f32),
        w_ff2=np.ascontiguousarray(inputs["L0_attn_w_ff2"]).astype(f32),
        b_ff2=dcol(inputs["L0_attn_b_ff2"]),
        rtw1=rtw1,
        b1row=np.ascontiguousarray(
            inputs["L0_ffn_actor_b1"][None, :]).astype(f32),
        aw2=np.ascontiguousarray(inputs["L0_ffn_actor_w2"].T).astype(f32),
        b2row=np.ascontiguousarray(
            inputs["L0_ffn_actor_b2"][None, :]).astype(f32),
        gate_wT=gate_wT, up_wT=up_wT, down_wT=down_wT,
        ident=np.eye(128, dtype=f32),
        iota6=np.arange(6, dtype=f32)[None, :].copy(),
        rev8=np.arange(7, -1, -1, dtype=f32)[None, :].copy(),
        ones_r=np.ones((1, 128), dtype=f32),
        ones_c=np.ones((128, 1), dtype=f32),
    )

    in_maps = []
    for c in range(8):
        b, q = c // 4, c % 4
        m = dict(shared)
        m["x_sl"] = np.ascontiguousarray(
            x[b, q * TOK:(q + 1) * TOK, :].T).astype(f32)
        m["w_in_xc"] = np.ascontiguousarray(
            w_in[:, q * CH:(q + 1) * CH]).astype(f32)
        m["w_in_z"] = np.ascontiguousarray(
            w_in[:, DIN + q * CH:DIN + (q + 1) * CH]).astype(f32)
        m["conv_w"] = np.ascontiguousarray(
            conv_w[q * CH:(q + 1) * CH].reshape(4, 128, 4)
            .transpose(1, 0, 2).reshape(128, 16)).astype(f32)
        m["conv_b"] = chcol(conv_b, q)
        m["w_x_sl"] = np.ascontiguousarray(
            w_x[q * CH:(q + 1) * CH, :]).astype(f32)
        m["w_dt_sl"] = np.ascontiguousarray(
            w_dt[:, q * CH:(q + 1) * CH]).astype(f32)
        m["b_dt"] = chcol(b_dt, q)
        m["a_log"] = np.ascontiguousarray(
            a_log[q * CH:(q + 1) * CH].reshape(4, 128, 16)
            .transpose(1, 0, 2).reshape(128, 64)).astype(f32)
        m["d_skip"] = chcol(d_skip, q)
        m["w_out_sl"] = np.ascontiguousarray(
            w_out[q * CH:(q + 1) * CH, :]).astype(f32)
        in_maps.append(m)
    return in_maps


def kernel(**inputs):
    in_maps = _prep_inputs(inputs)
    nc, dbg = build_nc()
    res = run_bass_kernel_spmd(nc, in_maps, core_ids=list(range(8)))
    outs = []
    for b in range(B):
        parts = [res.results[b * 4 + q]["out_fm"] for q in range(4)]
        outs.append(np.concatenate(parts, axis=1).T)
    return np.stack(outs, axis=0).astype(np.float32)


# version 05
# speedup vs baseline: 1.8163x; 1.0055x over previous
import numpy as np
import ml_dtypes
import concourse.bass as bass
import concourse.mybir as mybir
from concourse.bass_utils import run_bass_kernel_spmd

F32 = mybir.dt.float32
BF16 = mybir.dt.bfloat16
OP = mybir.AluOpType
AF = mybir.ActivationFunctionType
AX = mybir.AxisListType

B, L, D = 2, 8192, 1024
DIN = 2048
NST = 16
DTR = 64
FFI = 4096
NE = 8
EI = 2688
TOK = 2048
CH = 512
EPS = 1e-5
GROUPS = [[0, 1, 2, 3], [4, 5, 6, 7]]

DEBUG = False


class Chan:

    def __init__(self, nc, name):
        self.h = nc.alloc_semaphore(name)
        self.n = 0

    def tick(self, instr, amt=1):
        instr.then_inc(self.h, amt)
        self.n += amt
        return self.n

    def wait(self, eng, n=None):
        eng.wait_ge(self.h, self.n if n is None else n)


def build_nc(debug=DEBUG):
    nc = bass.Bass("TRN2", target_bir_lowering=False)

    def din(name, shape, dt=F32):
        return nc.dram_tensor(name, shape, dt, kind="ExternalInput").ap()

    def dout(name, shape, dt=F32):
        return nc.dram_tensor(name, shape, dt, kind="ExternalOutput").ap()

    x_sl = din("x_sl", [D, TOK])
    ln1g = din("ln1g", [128, 8])
    ln1b = din("ln1b", [128, 8])
    ln2g = din("ln2g", [128, 8])
    ln2b = din("ln2b", [128, 8])
    w_in_xc = din("w_in_xc", [D, CH])
    w_in_z = din("w_in_z", [D, CH])
    conv_w = din("conv_w", [128, 16])
    conv_b = din("conv_b", [128, 4])
    w_x_sl = din("w_x_sl", [CH, 96])
    w_dt_sl = din("w_dt_sl", [DTR, CH])
    b_dt = din("b_dt", [128, 4])
    a_log = din("a_log", [128, 64])
    d_skip = din("d_skip", [128, 4])
    w_out_sl = din("w_out_sl", [CH, D])
    w_ff1 = din("w_ff1", [D, 2 * FFI])
    b_ff1 = din("b_ff1", [128, 64])
    w_ff2 = din("w_ff2", [FFI, D])
    b_ff2 = din("b_ff2", [128, 8])
    rtw1 = din("rtw1", [D, 58])
    b1row = din("b1row", [1, 50])
    aw2 = din("aw2", [50, 6])
    b2row = din("b2row", [1, 6])
    gate_wT = din("gate_wT", [NE, D, EI])
    up_wT = din("up_wT", [NE, D, EI])
    down_wT = din("down_wT", [NE, EI, D])
    ident = din("ident", [128, 128])
    iota6 = din("iota6", [1, 6])
    rev8 = din("rev8", [1, 8])
    ones_r = din("ones_r", [1, 128])
    ones_c = din("ones_c", [128, 1])

    out_fm = dout("out_fm", [D, TOK])

    dbg = {}
    if debug:
        dbg["xn"] = dout("dbg_xn", [D, TOK])
        dbg["h"] = dout("dbg_h", [D, TOK], BF16)
        dbg["probs"] = dout("dbg_probs", [TOK, 8])
        dbg["kidx"] = dout("dbg_kidx", [TOK, 1])
        dbg["scores"] = dout("dbg_scores", [TOK, 8])
        dbg["hfull"] = dout("dbg_hfull", [4 * D, TOK], BF16)
        dbg["rtraw"] = dout("dbg_rtraw", [TOK, 58])
        dbg["ahT"] = dout("dbg_ahT", [50, 128])
        dbg["praw"] = dout("dbg_praw", [TOK, 6])
        dbg["st1"] = dout("dbg_st1", [128, 6])
        dbg["st2"] = dout("dbg_st2", [128, 8])
        dbg["st3"] = dout("dbg_st3", [128, 8])
        dbg["st4"] = dout("dbg_st4", [128, 1])
        dbg["xcs"] = dout("dbg_xcs", [CH, L], BF16)
        dbg["sz"] = dout("dbg_sz", [CH, L], BF16)
        dbg["dbc"] = dout("dbg_dbc", [96, L])
        dbg["dt"] = dout("dbg_dt", [CH, L], BF16)
        dbg["ys"] = dout("dbg_ys", [CH, L])
        dbg["mamba"] = dout("dbg_mamba", [D, TOK])
        dbg["attn"] = dout("dbg_attn", [D, TOK])
        dbg["moe"] = dout("dbg_moe", [D, TOK])
        dbg["alog2"] = dout("dbg_alog2", [TOK, 6])

    ag_in = nc.dram_tensor("ag_in", [D, TOK], BF16).ap()
    ag_out = nc.dram_tensor("ag_out", [4 * D, TOK], BF16).ap()
    ar_in = nc.dram_tensor("ar_in", [96, L], F32).ap()
    ar_out = nc.dram_tensor("ar_out", [96, L], F32).ap()
    rs_in = nc.dram_tensor("rs_in", [4 * D, TOK], F32).ap()
    rs_out = nc.dram_tensor("rs_out", [D, TOK], F32).ap()
    xn_dram = nc.dram_tensor("xn_dram", [D, TOK], F32).ap()
    xcs_dram = nc.dram_tensor("xcs_dram", [CH, L], BF16).ap()
    sz_dram = nc.dram_tensor("sz_dram", [CH, L], BF16).ap()
    sc_dram = nc.dram_tensor("sc_dram", [NE, TOK], F32).ap()
    moe_dram = nc.dram_tensor("moe_dram", [D, TOK], F32).ap()
    ysd_dram = nc.dram_tensor("ysd_dram", [CH, L], F32).ap()
    dtd_dram = nc.dram_tensor("dtd_dram", [CH, L], BF16).ap()

    sb = nc.alloc_sbuf_tensor

    c_ln1g = sb("c_ln1g", [128, 8], F32)
    c_ln1b = sb("c_ln1b", [128, 8], F32)
    c_ln2g = sb("c_ln2g", [128, 8], F32)
    c_ln2b = sb("c_ln2b", [128, 8], F32)
    c_convw = sb("c_convw", [128, 16], F32)
    c_convb = sb("c_convb", [128, 4], F32)
    c_bdt = sb("c_bdt", [128, 4], F32)
    c_alog = sb("c_alog", [128, 64], F32)
    c_dskip = sb("c_dskip", [128, 4], F32)
    c_ident = sb("c_ident", [128, 128], F32)
    c_iota6 = sb("c_iota6", [128, 6], F32)
    c_rev8 = sb("c_rev8", [128, 8], F32)
    c_ones = sb("c_ones", [1, 128], F32)
    c_onec = sb("c_onec", [128, 1], F32)
    c_b1row = sb("c_b1row", [1, 50], F32)
    c_b2row = sb("c_b2row", [1, 6], F32)
    c_bff1 = sb("c_bff1", [128, 64], F32)
    c_bff2 = sb("c_bff2", [128, 8], F32)
    c_acol = sb("c_acol", [128, 64], F32)
    c_zero = sb("c_zero", [128, 512], F32)

    w_rtw1 = sb("w_rtw1", [128, 8, 58], F32)
    w_aw2 = sb("w_aw2", [50, 6], F32)
    wb_inxc = sb("wb_inxc", [128, 8, CH], BF16)
    wb_inz = sb("wb_inz", [128, 8, CH], BF16)
    wb_x = sb("wb_x", [128, 4, 96], BF16)
    wb_dt = sb("wb_dt", [DTR, CH], BF16)
    wb_out = sb("wb_out", [128, 4, D], BF16)

    t_xnb = sb("t_xnb", [128, 8, TOK], BF16)
    r_mu = sb("r_mu", [1, 512], F32)
    r_m2 = sb("r_m2", [1, 512], F32)
    r_t = sb("r_t", [1, 512], F32)
    r_sd = sb("r_sd", [1, 512], F32)
    rt_probs = sb("rt_probs", [128, 8], F32)
    rt_ah = sb("rt_ah", [128, 50], F32)
    rt_ahT = sb("rt_ahT", [50, 128], F32)
    rt_lg = sb("rt_lg", [128, 8], F32)
    rt_col = sb("rt_col", [128, 1], F32)
    rt_col2 = sb("rt_col2", [128, 1], F32)
    rt_eq6 = sb("rt_eq6", [128, 6], F32)
    rt_srt = sb("rt_srt", [128, 8], F32)
    rt_sc = sb("rt_sc", [128, 16, 8], F32)
    rt_scrow = sb("rt_scrow", [8, 128], F32)
    rt_dbg6 = sb("rt_dbg6", [128, 6], F32)
    dbg_s1t = sb("dbg_s1t", [128, 6], F32)
    dbg_s2t = sb("dbg_s2t", [128, 8], F32)
    dbg_s3t = sb("dbg_s3t", [128, 8], F32)
    dbg_s4t = sb("dbg_s4t", [128, 1], F32)

    _rem = nc.sbuf_bytes_remaining - 256
    base_ov, top_ov = nc.bump_sbuf(_rem)
    OVSZ = top_ov - base_ov
    assert OVSZ >= 131 * 1024, OVSZ

    def ov(name, shape, dt, off):
        h = nc.alloc_sbuf_tensor_at(name, shape, dt, offset=base_ov + off)
        return h

    t_xn = ov("t_xn", [128, 8, TOK], F32, 0)
    t_x = ov("t_x", [128, 8, 512], F32, 64 * 1024)
    t_sq = ov("t_sq", [128, 8, 512], F32, 80 * 1024)
    t_hb = ov("t_hb", [128, 8, TOK], BF16, 96 * 1024)
    t_tmp = ov("t_tmp", [128, 512], F32, 128 * 1024)
    p2_h = ov("p2_h", [128, 8, 2048], BF16, 0)
    p2_xc = ov("p2_xc", [128, 4, 2048], F32, 32 * 1024)
    p2_xcs = ov("p2_xcs", [128, 4, 2048], F32, 64 * 1024)
    p2_xcsb = ov("p2_xcsb", [128, 4, 2048], BF16, 96 * 1024)
    p2_tail = ov("p2_tail", [128, 4, 4], F32, 112 * 1024)
    p2_szb = ov("p2_szb", [128, 2, 512], BF16, 113 * 1024)
    p2_dbst = ov("p2_dbst", [96, 2, 512], F32, 116 * 1024)
    p3_lo = ov("p3_lo", [64, L], BF16, 0)
    p3_dt = ov("p3_dt", [128, 2048], BF16, 16 * 1024)
    p3_dtx = ov("p3_dtx", [128, 2048], BF16, 20 * 1024)
    p3_xcs = ov("p3_xcs", [128, 2048], BF16, 24 * 1024)
    p3_a = ov("p3_a", [128, 2, 2048], F32, 117 * 1024)
    p3_u = ov("p3_u", [128, 2048], BF16, 36 * 1024)
    p3_hh = ov("p3_hh", [128, 2048], BF16, 40 * 1024)
    p3_y = ov("p3_y", [128, 2048], F32, 44 * 1024)
    p3_Bbc = ov("p3_Bbc", [128, 2, 2048], BF16, 52 * 1024)
    p3_Cbc = ov("p3_Cbc", [128, 2, 2048], BF16, 60 * 1024)
    p3_sz = ov("p3_sz", [128, 2048], BF16, 68 * 1024)
    p3_yf = ov("p3_yf", [128, 4, 2048], BF16, 72 * 1024)
    p3_wo = ov("p3_wo", [128, 2, 512], F32, 88 * 1024)
    p3_t1 = ov("p3_t1", [128, 512], F32, 92 * 1024)
    p3_t2 = ov("p3_t2", [128, 512], F32, 94 * 1024)
    p3_carry = ov("p3_carry", [128, 64], F32, 96 * 1024)
    p5_xm = ov("p5_xm", [128, 8, 512], F32, 0)
    p5_h2 = ov("p5_h2", [128, 8, 512], BF16, 16 * 1024)
    p5_sq = ov("p5_sq", [128, 8, 512], F32, 24 * 1024)
    p5_ag = ov("p5_ag", [128, 32, 512], BF16, 24 * 1024)
    p5_w1 = ov("p5_w1", [128, 2, 8, 1024], BF16, 56 * 1024)
    p5_w2 = ov("p5_w2", [128, 32, 512], BF16, 88 * 1024)
    p5_g = ov("p5_g", [128, 512], BF16, 120 * 1024)
    p5_xt = ov("p5_xt", [128, 512], F32, 121 * 1024)
    p5_mt = ov("p5_mt", [128, 512], F32, 123 * 1024)
    p4_dw = ov("p4_dw", [128, 21, 1024], BF16, 0)
    p4_gw = ov("p4_gw", [128, 8, 1408], BF16, 42 * 1024)
    p4_uw = ov("p4_uw", [128, 8, 1408], BF16, 64 * 1024)
    p4_prod = ov("p4_prod", [128, 11, 512], BF16, 86 * 1024)
    p4_sb = ov("p4_sb", [128, 4, 512], BF16, 97 * 1024)
    p4_ds = ov("p4_ds", [128, 2, 512], F32, 101 * 1024)
    p4_scb = ov("p4_scb", [128, TOK], F32, 105 * 1024)

    pb = [nc.alloc_psum_tensor(f"pb{i}", [128, 512], F32) for i in range(8)]

    ch_ld = Chan(nc, "ld")
    ch_g = Chan(nc, "chg")
    ch_t = Chan(nc, "cht")
    ch_v = Chan(nc, "chv")
    ch_s = Chan(nc, "chs")
    ch_st = Chan(nc, "chst")
    ch_cc = Chan(nc, "chcc")

    ch_rh = Chan(nc, "chrh0")

    slot_tbl = {}

    def slot_need(key, eng):
        if key in slot_tbl:
            ch, n = slot_tbl[key]
            eng.wait_ge(ch.h, n)

    def slot_done(key, chan):
        slot_tbl[key] = (chan, chan.n)

    def vfence(v):
        ch_rh.tick(v.engine_nop())
        ch_rh.wait(v)

    with nc.Block() as block:
        @block.sync
        def _(sync):
            pairs = [
                (c_ln1g[:], ln1g), (c_ln1b[:], ln1b), (c_ln2g[:], ln2g),
                (c_ln2b[:], ln2b), (c_convw[:], conv_w), (c_convb[:], conv_b),
                (c_bdt[:], b_dt), (c_alog[:], a_log), (c_dskip[:], d_skip),
                (c_ident[:], ident),
                (c_iota6[:], iota6.partition_broadcast(128)),
                (c_rev8[:], rev8.partition_broadcast(128)),
                (c_ones[:], ones_r), (c_onec[:], ones_c),
                (c_b1row[:], b1row), (c_b2row[:], b2row),
                (c_bff1[:], b_ff1), (c_bff2[:], b_ff2),
                (w_rtw1[:], rtw1.rearrange("(k p) m -> p k m", p=128)),
                (w_aw2[:], aw2),
            ]
            for o_, i_ in pairs:
                ch_ld.tick(sync.dma_start(out=o_, in_=i_), 16)

        @block.gpsimd
        def _(g):
            ch_g.tick(g.dma_start(out=wb_inxc[:], in_=w_in_xc.rearrange("(k p) m -> p k m", p=128)), 16)
            ch_g.tick(g.dma_start(out=wb_inz[:], in_=w_in_z.rearrange("(k p) m -> p k m", p=128)), 16)
            ch_g.tick(g.dma_start(out=wb_x[:], in_=w_x_sl.rearrange("(k p) m -> p k m", p=128)), 16)
            ch_g.tick(g.dma_start(out=wb_dt[:], in_=w_dt_sl), 16)
            ch_g.tick(g.dma_start(out=wb_out[:], in_=w_out_sl.rearrange("(k p) m -> p k m", p=128)), 16)

        @block.vector
        def _(v):
            ch_v.tick(v.memset(c_zero[:], 0.0))

        @block.sync
        def _(sync):
            ch_v.wait(sync)
            for m in range(8):
                for j in range(4):
                    ch_st.tick(sync.dma_start(
                        out=moe_dram[m * 128:(m + 1) * 128, j * 512:(j + 1) * 512],
                        in_=c_zero[:]), 16)

        @block.scalar
        def _(s):
            ch_ld.wait(s)
            ch_s.tick(s.activation(out=c_acol[:], in_=c_alog[:], func=AF.Exp))

        @block.vector
        def _(v):
            ch_s.wait(v)
            ch_v.tick(v.tensor_scalar(out=c_acol[:], in0=c_acol[:], scalar1=-1.0,
                                      scalar2=None, op0=OP.mult))

        def emit_ln(srcf, dst_write, pre=None):
            for tt in range(TOK // 512):
                sl = slice(tt * 512, (tt + 1) * 512)
                if pre is not None:
                    pre(tt)

                @block.scalar
                def _(s):
                    ch_st.wait(s)
                    for k in range(8):
                        ins = s.activation(out=t_sq[:, k, :], in_=srcf(k, tt),
                                           func=AF.Square)
                    ch_s.tick(ins)

                @block.tensor
                def _(t):
                    ch_s.wait(t)
                    ch_v.wait(t)
                    for k in range(8):
                        t.matmul(pb[6][0:1, :], c_onec[:, 0:1], srcf(k, tt),
                                 start=(k == 0), stop=(k == 7))
                    for k in range(8):
                        ins = t.matmul(pb[7][0:1, :], c_onec[:, 0:1], t_sq[:, k, :],
                                       start=(k == 0), stop=(k == 7))
                    ch_t.tick(ins)

                @block.vector
                def _(v):
                    ch_t.wait(v)
                    v.tensor_scalar(out=r_mu[:], in0=pb[6][0:1, :], scalar1=1.0 / D,
                                    scalar2=None, op0=OP.mult)
                    v.tensor_scalar(out=r_m2[:], in0=pb[7][0:1, :], scalar1=1.0 / D,
                                    scalar2=EPS, op0=OP.mult, op1=OP.add)
                    v.tensor_mul(r_t[:], r_mu[:], r_mu[:])
                    ch_v.tick(v.tensor_sub(r_m2[:], r_m2[:], r_t[:]))

                @block.scalar
                def _(s):
                    ch_v.wait(s)
                    ch_s.tick(s.activation(out=r_sd[:], in_=r_m2[:], func=AF.Sqrt))

                @block.vector
                def _(v):
                    ch_s.wait(v)
                    ch_v.tick(v.reciprocal(r_sd[:], r_sd[:]))

                @block.tensor
                def _(t):
                    ch_v.wait(t)
                    t.matmul(pb[4][:, :], c_ones[:, :], r_mu[:, :],
                             start=True, stop=True)
                    ins = t.matmul(pb[5][:, :], c_ones[:, :], r_sd[:, :],
                                   start=True, stop=True)
                    ch_t.tick(ins)

                @block.vector
                def _(v):
                    ch_t.wait(v)
                    for k in range(8):
                        ins = dst_write(v, k, sl, tt)
                    ch_v.tick(ins)

        def x_pre(tt):
            @block.sync
            def _(sync):
                if tt > 0:
                    ch_v.wait(sync)
                ins = sync.dma_start(
                    out=t_x[:],
                    in_=x_sl[:, tt * 512:(tt + 1) * 512]
                    .rearrange("(k p) t -> p k t", p=128))
                ch_st.tick(ins, 16)

        def xn_write(v, k, sl, tt):
            v.tensor_sub(t_xn[:, k, sl], t_x[:, k, :], pb[4][:, :])
            return v.tensor_mul(t_xn[:, k, sl], t_xn[:, k, sl], pb[5][:, :])

        emit_ln(lambda k, tt: t_x[:, k, :], xn_write, pre=x_pre)

        @block.sync
        def _(sync):
            ch_v.wait(sync)
            ins = sync.dma_start(out=xn_dram.rearrange("(k p) t -> p k t", p=128),
                                 in_=t_xn[:])
            ch_st.tick(ins, 16)

        @block.vector
        def _(v):
            for k in range(8):
                ins = v.tensor_copy(out=t_xnb[:, k, :], in_=t_xn[:, k, :])
            ch_v.tick(ins)

        def h_write(v, k, sl, tt):
            v.tensor_sub(t_tmp[:, :], t_xn[:, k, sl], pb[4][:, :])
            v.tensor_mul(t_tmp[:, :], t_tmp[:, :], pb[5][:, :])
            return v.tensor_scalar(out=t_hb[:, k, sl], in0=t_tmp[:, :],
                                   scalar1=c_ln1g[:, k:k + 1],
                                   scalar2=c_ln1b[:, k:k + 1],
                                   op0=OP.mult, op1=OP.add)

        emit_ln(lambda k, tt: t_xn[:, k, tt * 512:(tt + 1) * 512], h_write)

        @block.sync
        def _(sync):
            ch_v.wait(sync)
            ins = sync.dma_start(out=ag_in.rearrange("(k p) t -> p k t", p=128),
                                 in_=t_hb[:])
            ch_st.tick(ins, 16)

        @block.gpsimd
        def _(g):
            ch_st.wait(g)
            ins = g.collective_compute(
                "AllGather", OP.bypass, replica_groups=GROUPS,
                ins=[ag_in], outs=[ag_out])
            ch_cc.tick(ins)

        for tt in range(16):
            tsl = slice(tt * 128, (tt + 1) * 128)
            bank = pb[0] if tt % 2 == 0 else pb[1]

            @block.tensor
            def _(t):
                if tt >= 2:
                    ch_s.wait(t)
                    ch_v.wait(t)
                for k in range(8):
                    t.matmul(bank[:, 0:58], t_xn[:, k, tsl], w_rtw1[:, k, :],
                             start=(k == 0), stop=False)
                ins = t.matmul(bank[:, 8:58], c_ones[:, :], c_b1row[:, :],
                               start=False, stop=True)
                ch_t.tick(ins)

            if debug:
                @block.vector
                def _(v):
                    ch_t.wait(v)
                    ch_v.tick(v.tensor_copy(out=rt_ah[:, 0:58] if False else rt_sc[:, 0, :] if False else rt_ahT[0:128, 0:58] if False else rt_ah[:, :], in_=bank[:, 8:58]))

                @block.sync
                def _(sync):
                    ch_v.wait(sync)
                    ch_st.tick(sync.dma_start(out=dbg["rtraw"][tsl, 8:58],
                                              in_=rt_ah[:, :]), 16)

                @block.vector
                def _(v):
                    ch_st.wait(v)
                    ch_v.tick(v.tensor_copy(out=rt_lg[:, :], in_=bank[:, 0:8]))

                @block.sync
                def _(sync):
                    ch_v.wait(sync)
                    ch_st.tick(sync.dma_start(out=dbg["rtraw"][tsl, 0:8],
                                              in_=rt_lg[:, :]), 16)

            @block.vector
            def _(v):
                ch_t.wait(v)
                ch_s.wait(v)
                if debug:
                    ch_st.wait(v)
                v.reduce_max(out=rt_col[:], in_=bank[:, 0:8], axis=AX.X)
                vfence(v)
                ins = v.tensor_scalar(out=rt_col[:], in0=rt_col[:], scalar1=-1.0,
                                      scalar2=None, op0=OP.mult)
                ch_rh.tick(ins)

            @block.scalar
            def _(s):
                ch_rh.wait(s)
                if debug:
                    ch_st.wait(s)
                s.activation(out=rt_probs[:], in_=bank[:, 0:8], func=AF.Exp,
                             bias=rt_col[:, 0:1])
                ins = s.activation(out=rt_ah[:], in_=bank[:, 8:58],
                                   func=AF.Gelu_apprx_tanh)
                ch_s.tick(ins)

            @block.vector
            def _(v):
                ch_s.wait(v)
                v.tensor_reduce(out=rt_col[:], in_=rt_probs[:],
                                axis=AX.X, op=OP.add)
                vfence(v)
                v.reciprocal(rt_col[:], rt_col[:])
                vfence(v)
                ins = v.tensor_scalar(out=rt_probs[:], in0=rt_probs[:],
                                      scalar1=rt_col[:, 0:1], scalar2=None,
                                      op0=OP.mult)
                ch_v.tick(ins)

            @block.tensor
            def _(t):
                ch_s.wait(t)
                ch_t.tick(t.transpose(pb[2][0:50, 0:128], rt_ah[:, :],
                                      c_ident[:, :]))

            @block.vector
            def _(v):
                ch_t.wait(v)
                ins = v.tensor_copy(out=rt_ahT[:, :], in_=pb[2][0:50, 0:128])
                ch_v.tick(ins)

            if debug and tt == 0:
                @block.sync
                def _(sync):
                    ch_v.wait(sync)
                    ch_st.tick(sync.dma_start(out=dbg["ahT"], in_=rt_ahT[:, :]), 16)

            @block.tensor
            def _(t):
                ch_v.wait(t)
                if debug and tt == 0:
                    ch_st.wait(t)
                t.matmul(pb[3][:, 0:6], rt_ahT[:, :], w_aw2[:, :],
                         start=True, stop=False)
                ins = t.matmul(pb[3][:, 0:6], c_ones[:, :], c_b2row[:, :],
                               start=False, stop=True)
                ch_t.tick(ins)

            if debug:
                @block.vector
                def _(v):
                    ch_t.wait(v)
                    ch_st.wait(v)
                    ch_v.tick(v.tensor_copy(out=rt_dbg6[:, :], in_=pb[3][:, 0:6]))

                @block.sync
                def _(sync):
                    ch_v.wait(sync)
                    ch_st.tick(sync.dma_start(out=dbg["praw"][tsl, :],
                                              in_=rt_dbg6[:, :]), 16)

            @block.vector
            def _(v):
                ch_t.wait(v)
                if debug:
                    ch_st.wait(v)
                v.memset(rt_lg[:, 6:8], -1e30)
                v.tensor_copy(out=rt_eq6[:, :], in_=pb[3][:, 0:6])
                vfence(v)
                v.tensor_scalar(out=rt_lg[:, 0:6], in0=rt_eq6[:, :],
                                scalar1=-30.0, scalar2=30.0,
                                op0=OP.max, op1=OP.min)
                vfence(v)
                if debug:
                    v.tensor_copy(out=rt_dbg6[:, :], in_=rt_lg[:, 0:6])
                v.reduce_max(out=rt_col[:], in_=rt_lg[:, :], axis=AX.X)
                vfence(v)
                v.tensor_scalar(out=rt_srt[:, :], in0=rt_lg[:, :],
                                scalar1=rt_col[:, 0:1], scalar2=None,
                                op0=OP.is_ge)
                vfence(v)
                v.tensor_mul(rt_srt[:, :], rt_srt[:, :], c_rev8[:, :])
                vfence(v)
                v.tensor_reduce(out=rt_col2[:], in_=rt_srt[:, :],
                                axis=AX.X, op=OP.max)
                vfence(v)
                v.tensor_scalar(out=rt_col2[:], in0=rt_col2[:],
                                scalar1=-1.0, scalar2=7.0,
                                op0=OP.mult, op1=OP.add)
                vfence(v)
                if debug and tt == 0:
                    v.tensor_copy(out=dbg_s1t[:, :], in_=rt_eq6[:, :])
                    v.tensor_copy(out=dbg_s2t[:, :], in_=rt_lg[:, :])
                    v.tensor_copy(out=dbg_s3t[:, :], in_=rt_srt[:, :])
                    ch_v.tick(v.tensor_copy(out=dbg_s4t[:, :], in_=rt_col2[:, :]))
                v.max(out=rt_srt[:, :], in_=rt_probs[:, :])
                vfence(v)
                v.tensor_scalar(out=rt_eq6[:, :], in0=c_iota6[:, :],
                                scalar1=rt_col2[:, 0:1], scalar2=None,
                                op0=OP.is_equal)
                vfence(v)
                v.tensor_mul(rt_eq6[:, :], rt_eq6[:, :], rt_srt[:, 0:6])
                vfence(v)
                v.tensor_reduce(out=rt_col[:], in_=rt_eq6[:, :], axis=AX.X,
                                op=OP.add)
                vfence(v)
                v.tensor_scalar(out=rt_sc[:, tt, :], in0=rt_probs[:, :],
                                scalar1=rt_col[:, 0:1], scalar2=None,
                                op0=OP.is_ge)
                vfence(v)
                ins = v.tensor_mul(rt_sc[:, tt, :], rt_sc[:, tt, :],
                                   rt_probs[:, :])
                ch_v.tick(ins)

            if debug:
                @block.sync
                def _(sync):
                    ch_v.wait(sync)
                    ch_st.tick(sync.dma_start(out=dbg["probs"][tsl, :],
                                              in_=rt_probs[:, :]), 16)
                    ch_st.tick(sync.dma_start(out=dbg["kidx"][tsl, :],
                                              in_=rt_col2[:, :]), 16)
                    ch_st.tick(sync.dma_start(out=dbg["scores"][tsl, :],
                                              in_=rt_sc[:, tt, :]), 16)
                    ch_st.tick(sync.dma_start(out=dbg["alog2"][tsl, :],
                                              in_=rt_dbg6[:, :]), 16)
                    if tt == 0:
                        ch_st.tick(sync.dma_start(out=dbg["st1"], in_=dbg_s1t[:, :]), 16)
                        ch_st.tick(sync.dma_start(out=dbg["st2"], in_=dbg_s2t[:, :]), 16)
                        ch_st.tick(sync.dma_start(out=dbg["st3"], in_=dbg_s3t[:, :]), 16)
                        ch_st.tick(sync.dma_start(out=dbg["st4"], in_=dbg_s4t[:, :]), 16)

        for tt in range(16):
            bank = pb[2]

            @block.tensor
            def _(t):
                if tt > 0:
                    ch_st.wait(t)
                ins = t.transpose(bank[0:8, 0:128], rt_sc[:, tt, :], c_ident[:, :])
                ch_t.tick(ins)

            @block.vector
            def _(v):
                ch_t.wait(v)
                ins = v.tensor_copy(out=rt_scrow[:, :], in_=bank[0:8, 0:128])
                ch_v.tick(ins)

            @block.sync
            def _(sync):
                ch_v.wait(sync)
                ins = sync.dma_start(out=sc_dram[:, tt * 128:(tt + 1) * 128],
                                     in_=rt_scrow[:, :])
                ch_st.tick(ins, 16)

        if debug:
            @block.sync
            def _(sync):
                ch_v.wait(sync)
                ch_cc.wait(sync)
                ch_st.tick(sync.dma_start(
                    out=dbg["xn"].rearrange("(k p) t -> p k t", p=128),
                    in_=t_xn[:]), 16)
                ch_st.tick(sync.dma_start(
                    out=dbg["h"].rearrange("(k p) t -> p k t", p=128),
                    in_=t_hb[:]), 16)
                ch_st.tick(sync.dma_start(out=dbg["hfull"], in_=ag_out), 16)

        for blk in range(4):
            @block.sync
            def _(sync):
                if blk == 0:
                    ch_cc.wait(sync)
                    ch_v.wait(sync)
                    ch_s.wait(sync)
                else:
                    ch_v.wait(sync)
                for k in range(8):
                    ch_st.tick(sync.dma_start(
                        out=p2_h[:, k, :],
                        in_=ag_out[blk * D + k * 128:blk * D + (k + 1) * 128, :]), 16)

            h_ready = ch_st.n
            for m in range(8):
                wsl = wb_inxc if m < 4 else wb_inz
                mm = m % 4

                for tt in range(4):
                    gi = (m * 4 + tt) % 2
                    bank = pb[gi]

                    @block.tensor
                    def _(t):
                        if blk == 0 and m == 0 and tt == 0:
                            ch_g.wait(t)
                        if m == 0 and tt == 0:
                            t.wait_ge(ch_st.h, h_ready)
                        slot_need(("bank", gi), t)
                        for k in range(8):
                            ins = t.matmul(
                                bank[:, :],
                                wsl[:, k, mm * 128:(mm + 1) * 128],
                                p2_h[:, k, tt * 512:(tt + 1) * 512],
                                start=(k == 0), stop=(k == 7))
                        ch_t.tick(ins)
                    xz_done = ch_t.n

                    if m < 4:
                        @block.vector
                        def _(v):
                            v.wait_ge(ch_t.h, xz_done)
                            ins = v.tensor_copy(
                                out=p2_xc[:, mm, tt * 512:(tt + 1) * 512],
                                in_=bank[:, :])
                            ch_v.tick(ins)
                        slot_done(("bank", gi), ch_v)
                    else:
                        @block.scalar
                        def _(s_):
                            s_.wait_ge(ch_t.h, xz_done)
                            slot_need(("p2sz", tt % 2), s_)
                            ins = s_.activation(out=p2_szb[:, tt % 2, :],
                                                in_=bank[:, :], func=AF.Silu)
                            ch_s.tick(ins)
                        slot_done(("bank", gi), ch_s)
                        sz_done = ch_s.n

                        @block.sync
                        def _(sync):
                            sync.wait_ge(ch_s.h, sz_done)
                            ins = sync.dma_start(
                                out=sz_dram[mm * 128:(mm + 1) * 128,
                                            blk * 2048 + tt * 512:
                                            blk * 2048 + (tt + 1) * 512],
                                in_=p2_szb[:, tt % 2, :])
                            ch_st.tick(ins, 16)
                        slot_done(("p2sz", tt % 2), ch_st)

            for d in range(4):
                @block.vector
                def _(v):
                    ch_v.wait(v)
                    v.tensor_scalar(out=p2_xcs[:, d, :], in0=p2_xc[:, d, :],
                                    scalar1=c_convw[:, d * 4 + 3:d * 4 + 4],
                                    scalar2=None, op0=OP.mult)
                    for kk in range(3):
                        sh = 3 - kk
                        v.scalar_tensor_tensor(
                            out=p2_xcs[:, d, sh:2048],
                            in0=p2_xc[:, d, 0:2048 - sh],
                            scalar=c_convw[:, d * 4 + kk:d * 4 + kk + 1],
                            in1=p2_xcs[:, d, sh:2048],
                            op0=OP.mult, op1=OP.add)
                        if blk > 0:
                            v.scalar_tensor_tensor(
                                out=p2_xcs[:, d, 0:sh],
                                in0=p2_tail[:, d, 3 - sh:3],
                                scalar=c_convw[:, d * 4 + kk:d * 4 + kk + 1],
                                in1=p2_xcs[:, d, 0:sh],
                                op0=OP.mult, op1=OP.add)
                    ins = v.tensor_copy(out=p2_tail[:, d, 0:3],
                                        in_=p2_xc[:, d, 2045:2048])
                    ch_v.tick(ins)

                @block.scalar
                def _(s_):
                    ch_v.wait(s_)
                    ins = s_.activation(out=p2_xcs[:, d, :], in_=p2_xcs[:, d, :],
                                        func=AF.Silu,
                                        bias=c_convb[:, d:d + 1])
                    ch_s.tick(ins)

                @block.vector
                def _(v):
                    ch_s.wait(v)
                    ins = v.tensor_copy(out=p2_xcsb[:, d, :], in_=p2_xcs[:, d, :])
                    ch_v.tick(ins)

                @block.sync
                def _(sync):
                    ch_v.wait(sync)
                    ins = sync.dma_start(
                        out=xcs_dram[d * 128:(d + 1) * 128,
                                     blk * 2048:(blk + 1) * 2048],
                        in_=p2_xcsb[:, d, :])
                    ch_st.tick(ins, 16)

            xcsb_ready = ch_v.n
            for tt in range(4):
                @block.tensor
                def _(t):
                    if tt == 0:
                        t.wait_ge(ch_v.h, xcsb_ready)
                    slot_need(("bank", 2), t)
                    for d in range(4):
                        ins = t.matmul(
                            pb[2][0:96, :],
                            wb_x[:, d, :],
                            p2_xcsb[:, d, tt * 512:(tt + 1) * 512],
                            start=(d == 0), stop=(d == 3))
                    ch_t.tick(ins)
                dbc_done = ch_t.n

                @block.vector
                def _(v):
                    v.wait_ge(ch_t.h, dbc_done)
                    slot_need(("p2db", tt % 2), v)
                    ins = v.tensor_copy(out=p2_dbst[:, tt % 2, :],
                                        in_=pb[2][0:96, :])
                    ch_v.tick(ins)
                slot_done(("bank", 2), ch_v)
                db_done = ch_v.n

                @block.sync
                def _(sync):
                    sync.wait_ge(ch_v.h, db_done)
                    ins = sync.dma_start(
                        out=ar_in[:, blk * 2048 + tt * 512:
                                  blk * 2048 + (tt + 1) * 512],
                        in_=p2_dbst[:, tt % 2, :])
                    ch_st.tick(ins, 16)
                slot_done(("p2db", tt % 2), ch_st)

        @block.gpsimd
        def _(g):
            ch_st.wait(g)
            ins = g.collective_compute(
                "AllReduce", OP.add, replica_groups=GROUPS,
                ins=[ar_in], outs=[ar_out])
            ch_cc.tick(ins)

        if debug:
            @block.gpsimd
            def _(g):
                ch_cc.wait(g)
                ch_g.tick(g.dma_start(out=dbg["xcs"], in_=xcs_dram), 16)
                ch_g.tick(g.dma_start(out=dbg["sz"], in_=sz_dram), 16)
                ch_g.tick(g.dma_start(out=dbg["dbc"], in_=ar_out), 16)

        @block.gpsimd
        def _(g):
            ch_cc.wait(g)
            ch_g.tick(g.dma_start(out=p3_lo[:], in_=ar_out[0:64, :]), 16)

        @block.vector
        def _(v):
            ch_v.tick(v.memset(p3_carry[:], 0.0))

        bc_ready = {}
        for qt in range(4):
            for d in range(4):
                for tt in range(4):
                    csl = slice(qt * 2048 + tt * 512, qt * 2048 + (tt + 1) * 512)

                    @block.tensor
                    def _(t):
                        if qt == 0 and d == 0 and tt == 0:
                            ch_g.wait(t)
                        slot_need(("bank", 3), t)
                        ins = t.matmul(pb[3][:, :],
                                       wb_dt[:, d * 128:(d + 1) * 128],
                                       p3_lo[:, csl], start=True, stop=True)
                        ch_t.tick(ins)

                    @block.scalar
                    def _(s_):
                        ch_t.wait(s_)
                        ins = s_.activation(out=p3_t1[:, :], in_=pb[3][:, :],
                                            func=AF.Sigmoid,
                                            bias=c_bdt[:, d:d + 1])
                        ch_s.tick(ins)
                    slot_done(("bank", 3), ch_s)

                    @block.vector
                    def _(v):
                        ch_s.wait(v)
                        ins = v.tensor_scalar(out=p3_t2[:, :], in0=p3_t1[:, :],
                                              scalar1=-1.0, scalar2=1.0,
                                              op0=OP.mult, op1=OP.add)
                        ch_v.tick(ins)

                    @block.scalar
                    def _(s_):
                        ch_v.wait(s_)
                        ins = s_.activation(out=p3_t2[:, :], in_=p3_t2[:, :],
                                            func=AF.Ln)
                        ch_s.tick(ins)

                    @block.vector
                    def _(v):
                        ch_s.wait(v)
                        ins = v.tensor_scalar(
                            out=p3_dt[:, tt * 512:(tt + 1) * 512],
                            in0=p3_t2[:, :], scalar1=-1.0, scalar2=None,
                            op0=OP.mult)
                        ch_v.tick(ins)

                @block.sync
                def _(sync):
                    if qt > 0 or d > 0:
                        ch_v.wait(sync)
                    ins = sync.dma_start(
                        out=p3_xcs[:, :],
                        in_=xcs_dram[d * 128:(d + 1) * 128,
                                     qt * 2048:(qt + 1) * 2048])
                    ch_st.tick(ins, 16)

                @block.vector
                def _(v):
                    ch_st.wait(v)
                    ins = v.tensor_mul(p3_dtx[:, :], p3_dt[:, :], p3_xcs[:, :])
                    ch_v.tick(ins)
                dt_ready = ch_v.n
                a_ready = {}

                if debug:
                    @block.sync
                    def _(sync):
                        ch_v.wait(sync)
                        ins = sync.dma_start(
                            out=dtd_dram[d * 128:(d + 1) * 128,
                                         qt * 2048:(qt + 1) * 2048],
                            in_=p3_dt[:, :])
                        ch_st.tick(ins, 16)

                for n in range(16):
                    if (d, n) not in bc_ready or True:
                        pass
                    if n == 0:
                        @block.gpsimd
                        def _(g):
                            if qt > 0 or d > 0:
                                ch_v.wait(g)
                            g.dma_start(out=p3_Bbc[:, 0, :], in_=ar_out[
                                64 + 0:65 + 0, qt * 2048:(qt + 1) * 2048]
                                .partition_broadcast(128)).then_inc(ch_g.h, 16)
                            ch_g.n += 16
                            ins = g.dma_start(out=p3_Cbc[:, 0, :], in_=ar_out[
                                80 + 0:81 + 0, qt * 2048:(qt + 1) * 2048]
                                .partition_broadcast(128))
                            ch_g.tick(ins, 16)
                    bc_ready[n] = ch_g.n

                    if n + 1 < 16:
                        @block.gpsimd
                        def _(g):
                            if qt > 0 or d > 0 or n >= 1:
                                ch_v.wait(g)
                            g.dma_start(out=p3_Bbc[:, (n + 1) % 2, :],
                                        in_=ar_out[64 + n + 1:65 + n + 1,
                                                   qt * 2048:(qt + 1) * 2048]
                                        .partition_broadcast(128)).then_inc(ch_g.h, 16)
                            ch_g.n += 16
                            ins = g.dma_start(out=p3_Cbc[:, (n + 1) % 2, :],
                                              in_=ar_out[80 + n + 1:81 + n + 1,
                                                         qt * 2048:(qt + 1) * 2048]
                                              .partition_broadcast(128))
                            ch_g.tick(ins, 16)

                    @block.scalar
                    def _(s_):
                        s_.wait_ge(ch_v.h, dt_ready)
                        slot_need(("p3a", n % 2), s_)
                        ins = s_.activation(out=p3_a[:, n % 2, :],
                                            in_=p3_dt[:, :],
                                            func=AF.Exp,
                                            scale=c_acol[:, d * 16 + n:
                                                         d * 16 + n + 1])
                        ch_s.tick(ins)
                    a_ready[n] = ch_s.n

                    @block.vector
                    def _(v):
                        ch_g.wait(v, bc_ready[n])
                        v.wait_ge(ch_s.h, a_ready[n])
                        v.tensor_mul(p3_u[:, :], p3_dtx[:, :],
                                     p3_Bbc[:, n % 2, :])
                        vfence(v)
                        ini = 0.0 if qt == 0 else p3_carry[:, d * 16 + n:
                                                           d * 16 + n + 1]
                        v.tensor_tensor_scan(out=p3_hh[:, :],
                                             data0=p3_a[:, n % 2, :],
                                             data1=p3_u[:, :], initial=ini,
                                             op0=OP.mult, op1=OP.add)
                        vfence(v)
                        v.tensor_copy(out=p3_carry[:, d * 16 + n:d * 16 + n + 1],
                                      in_=p3_hh[:, 2047:2048])
                        if n == 0:
                            ins = v.tensor_mul(p3_y[:, :], p3_hh[:, :],
                                               p3_Cbc[:, n % 2, :])
                        else:
                            v.tensor_mul(p3_u[:, :], p3_hh[:, :],
                                         p3_Cbc[:, n % 2, :])
                            vfence(v)
                            ins = v.tensor_add(p3_y[:, :], p3_y[:, :], p3_u[:, :])
                        ch_v.tick(ins)
                    slot_done(("p3a", n % 2), ch_v)

                if debug:
                    @block.sync
                    def _(sync):
                        ch_v.wait(sync)
                        ins = sync.dma_start(
                            out=ysd_dram[d * 128:(d + 1) * 128,
                                         qt * 2048:(qt + 1) * 2048],
                            in_=p3_y[:, :])
                        ch_st.tick(ins, 16)

                @block.sync
                def _(sync):
                    if qt > 0 or d > 0:
                        ch_v.wait(sync)
                    ins = sync.dma_start(
                        out=p3_sz[:, :],
                        in_=sz_dram[d * 128:(d + 1) * 128,
                                    qt * 2048:(qt + 1) * 2048])
                    ch_st.tick(ins, 16)

                @block.vector
                def _(v):
                    ch_st.wait(v)
                    v.scalar_tensor_tensor(out=p3_y[:, :], in0=p3_xcs[:, :],
                                           scalar=c_dskip[:, d:d + 1],
                                           in1=p3_y[:, :], op0=OP.mult,
                                           op1=OP.add)
                    vfence(v)
                    ins = v.tensor_mul(p3_yf[:, d, :], p3_y[:, :], p3_sz[:, :])
                    ch_v.tick(ins)

            yf_ready = ch_v.n
            for m in range(8):
                for tt in range(4):
                    gi = (m * 4 + tt) % 2
                    bank = pb[3 + gi]

                    @block.tensor
                    def _(t):
                        if m == 0 and tt == 0:
                            t.wait_ge(ch_v.h, yf_ready)
                        slot_need(("bank", 3 + gi), t)
                        for d in range(4):
                            ins = t.matmul(
                                bank[:, :],
                                wb_out[:, d, m * 128:(m + 1) * 128],
                                p3_yf[:, d, tt * 512:(tt + 1) * 512],
                                start=(d == 0), stop=(d == 3))
                        ch_t.tick(ins)
                    mm_done = ch_t.n

                    @block.vector
                    def _(v):
                        v.wait_ge(ch_t.h, mm_done)
                        slot_need(("p3wo", gi), v)
                        ins = v.tensor_copy(out=p3_wo[:, gi, :],
                                            in_=bank[:, :])
                        ch_v.tick(ins)
                    slot_done(("bank", 3 + gi), ch_v)
                    ev_done = ch_v.n

                    @block.sync
                    def _(sync):
                        sync.wait_ge(ch_v.h, ev_done)
                        ins = sync.dma_start(
                            out=rs_in[qt * D + m * 128:qt * D + (m + 1) * 128,
                                      tt * 512:(tt + 1) * 512],
                            in_=p3_wo[:, gi, :])
                        ch_st.tick(ins, 16)
                    slot_done(("p3wo", gi), ch_st)

        @block.gpsimd
        def _(g):
            ch_st.wait(g)
            ins = g.collective_compute(
                "ReduceScatter", OP.add, replica_groups=GROUPS,
                ins=[rs_in], outs=[rs_out])
            ch_cc.tick(ins)

        if debug:
            @block.gpsimd
            def _(g):
                ch_cc.wait(g)
                ch_g.tick(g.dma_start(out=dbg["dt"], in_=dtd_dram), 16)
                ch_g.tick(g.dma_start(out=dbg["ys"], in_=ysd_dram), 16)
                ch_g.tick(g.dma_start(out=dbg["mamba"], in_=rs_out), 16)

        IH = [(0, 10), (10, 11)]
        for e in range(NE):
            @block.gpsimd
            def _(g):
                if e == 0:
                    ch_v.wait(g)
                    ch_st.wait(g)
                    ch_s.wait(g)
                else:
                    ch_t.wait(g)
                    ch_v.wait(g)
                g.dma_start(out=p4_scb[:, :],
                            in_=sc_dram[e:e + 1, :].partition_broadcast(128)
                            ).then_inc(ch_g.h, 16)
                ch_g.n += 16
                ins = g.dma_start(
                    out=p4_dw[:, :, :],
                    in_=down_wT[e].rearrange("(c p) m -> p c m", p=128))
                ch_g.tick(ins, 16)
            scb_ready = ch_g.n

            for ih, (ib, ihc) in enumerate(IH):
                @block.gpsimd
                def _(g):
                    if ih > 0:
                        ch_t.wait(g)
                    g.dma_start(
                        out=p4_gw[:, :, 0:ihc * 128],
                        in_=gate_wT[e][:, ib * 128:(ib + ihc) * 128]
                        .rearrange("(k p) m -> p k m", p=128)).then_inc(ch_g.h, 16)
                    ch_g.n += 16
                    ins = g.dma_start(
                        out=p4_uw[:, :, 0:ihc * 128],
                        in_=up_wT[e][:, ib * 128:(ib + ihc) * 128]
                        .rearrange("(k p) m -> p k m", p=128))
                    ch_g.tick(ins, 16)

                w_ready = ch_g.n
                for tt in range(4):
                    tsl = slice(tt * 512, (tt + 1) * 512)
                    for ic in range(ihc):
                        gb = ic % 2
                        sb0, sb1 = 2 * gb, 2 * gb + 1

                        @block.tensor
                        def _(t):
                            t.wait_ge(ch_g.h, w_ready)
                            slot_need(("bank", 0 + 2 * gb), t)
                            slot_need(("bank", 1 + 2 * gb), t)
                            slot_need(("prod", ic), t)
                            for k in range(8):
                                t.matmul(pb[0 + 2 * gb][:, :],
                                         p4_gw[:, k, ic * 128:(ic + 1) * 128],
                                         t_xnb[:, k, tsl],
                                         start=(k == 0), stop=(k == 7))
                            for k in range(8):
                                ins = t.matmul(
                                    pb[1 + 2 * gb][:, :],
                                    p4_uw[:, k, ic * 128:(ic + 1) * 128],
                                    t_xnb[:, k, tsl],
                                    start=(k == 0), stop=(k == 7))
                            ch_t.tick(ins)
                        gu_done = ch_t.n

                        @block.scalar
                        def _(s_):
                            s_.wait_ge(ch_t.h, gu_done)
                            slot_need(("p4sb", sb0), s_)
                            ins = s_.activation(out=p4_sb[:, sb0, :],
                                                in_=pb[0 + 2 * gb][:, :],
                                                func=AF.Silu)
                            ch_s.tick(ins)
                        slot_done(("bank", 0 + 2 * gb), ch_s)
                        g_done = ch_s.n

                        @block.vector
                        def _(v):
                            v.wait_ge(ch_s.h, g_done)
                            if e == 0 and ih == 0 and tt == 0 and ic == 0:
                                ch_st.wait(v)
                            slot_need(("p4sb", sb1), v)
                            v.tensor_copy(out=p4_sb[:, sb1, :],
                                          in_=pb[1 + 2 * gb][:, :])
                            vfence(v)
                            ins = v.tensor_mul(p4_prod[:, ic, :],
                                               p4_sb[:, sb0, :],
                                               p4_sb[:, sb1, :])
                            ch_v.tick(ins)
                        slot_done(("bank", 1 + 2 * gb), ch_v)
                        slot_done(("p4sb", sb0), ch_v)
                        slot_done(("p4sb", sb1), ch_v)
                    prod_ready = ch_v.n

                    for m in range(8):
                        db = 4 + m % 2

                        @block.tensor
                        def _(t):
                            if m == 0:
                                t.wait_ge(ch_v.h, prod_ready)
                            slot_need(("bank", db), t)
                            for ic in range(ihc):
                                ins = t.matmul(
                                    pb[db][:, :],
                                    p4_dw[:, ib + ic, m * 128:(m + 1) * 128],
                                    p4_prod[:, ic, :],
                                    start=(ic == 0), stop=(ic == ihc - 1))
                            ch_t.tick(ins)
                        dn_done = ch_t.n
                        if m == 7:
                            for icx in range(ihc):
                                slot_done(("prod", icx), ch_t)

                        @block.vector
                        def _(v):
                            v.wait_ge(ch_t.h, dn_done)
                            v.wait_ge(ch_g.h, scb_ready)
                            slot_need(("p4ds", m % 2), v)
                            ins = v.tensor_mul(p4_ds[:, m % 2, :],
                                               pb[db][:, :],
                                               p4_scb[:, tsl])
                            ch_v.tick(ins)
                        slot_done(("bank", db), ch_v)
                        ds_done = ch_v.n

                        @block.gpsimd
                        def _(g):
                            g.wait_ge(ch_v.h, ds_done)
                            ins = g.dma_start(
                                out=moe_dram[m * 128:(m + 1) * 128, tsl],
                                in_=p4_ds[:, m % 2, :],
                                accum_op=OP.add)
                            ch_g.tick(ins, 16)
                        slot_done(("p4ds", m % 2), ch_g)

        if debug:
            @block.gpsimd
            def _(g):
                ch_g.wait(g)
                ch_g.tick(g.dma_start(out=dbg["moe"], in_=moe_dram), 16)

        for tq in range(4):
            hsl = slice(tq * 512, (tq + 1) * 512)

            for k in range(8):
                @block.sync
                def _(sync):
                    if tq == 0 and k == 0:
                        ch_cc.wait(sync)
                    ch_v.wait(sync)
                    ch_st.tick(sync.dma_start(
                        out=p5_xm[:, k, :],
                        in_=xn_dram[k * 128:(k + 1) * 128, hsl]), 16)
                    ins = sync.dma_start(out=p5_xt[:, :],
                                         in_=rs_out[k * 128:(k + 1) * 128, hsl])
                    ch_st.tick(ins, 16)

                @block.vector
                def _(v):
                    ch_st.wait(v)
                    ins = v.tensor_add(p5_xm[:, k, :], p5_xm[:, k, :],
                                       p5_xt[:, :])
                    ch_v.tick(ins)

            @block.scalar
            def _(s_):
                ch_v.wait(s_)
                for k in range(8):
                    ins = s_.activation(out=p5_sq[:, k, :], in_=p5_xm[:, k, :],
                                        func=AF.Square)
                ch_s.tick(ins)

            @block.tensor
            def _(t):
                ch_s.wait(t)
                ch_v.wait(t)
                for k in range(8):
                    t.matmul(pb[6][0:1, :], c_onec[:, 0:1], p5_xm[:, k, :],
                             start=(k == 0), stop=(k == 7))
                for k in range(8):
                    ins = t.matmul(pb[7][0:1, :], c_onec[:, 0:1], p5_sq[:, k, :],
                                   start=(k == 0), stop=(k == 7))
                ch_t.tick(ins)

            @block.vector
            def _(v):
                ch_t.wait(v)
                v.tensor_scalar(out=r_mu[:], in0=pb[6][0:1, :],
                                scalar1=1.0 / D, scalar2=None, op0=OP.mult)
                v.tensor_scalar(out=r_m2[:], in0=pb[7][0:1, :],
                                scalar1=1.0 / D, scalar2=EPS,
                                op0=OP.mult, op1=OP.add)
                vfence(v)
                v.tensor_mul(r_t[:], r_mu[:], r_mu[:])
                vfence(v)
                ch_v.tick(v.tensor_sub(r_m2[:], r_m2[:], r_t[:]))

            @block.scalar
            def _(s_):
                ch_v.wait(s_)
                ch_s.tick(s_.activation(out=r_sd[:], in_=r_m2[:], func=AF.Sqrt))

            @block.vector
            def _(v):
                ch_s.wait(v)
                ch_v.tick(v.reciprocal(r_sd[:], r_sd[:]))

            @block.tensor
            def _(t):
                ch_v.wait(t)
                t.matmul(pb[4][:, :], c_ones[:, :], r_mu[:, :],
                         start=True, stop=True)
                ins = t.matmul(pb[5][:, :], c_ones[:, :], r_sd[:, :],
                               start=True, stop=True)
                ch_t.tick(ins)

            @block.vector
            def _(v):
                ch_t.wait(v)
                for k in range(8):
                    v.tensor_sub(t_tmp[:, :], p5_xm[:, k, :], pb[4][:, :])
                    v.tensor_mul(t_tmp[:, :], t_tmp[:, :], pb[5][:, :])
                    ins = v.tensor_scalar(out=p5_h2[:, k, :], in0=t_tmp[:, :],
                                          scalar1=c_ln2g[:, k:k + 1],
                                          scalar2=c_ln2b[:, k:k + 1],
                                          op0=OP.mult, op1=OP.add)
                ch_v.tick(ins)

            h2_ready = ch_v.n
            w1_ready = {}
            for mg in range(8):
                wslot = mg % 2

                @block.gpsimd
                def _(g):
                    if tq == 0 and mg == 0:
                        ch_t.wait(g)
                        ch_v.wait(g)
                    slot_need(("p5w1", wslot), g)
                    g.dma_start(
                        out=p5_w1[:, wslot, :, 0:512],
                        in_=w_ff1[:, mg * 512:(mg + 1) * 512]
                        .rearrange("(k p) m -> p k m", p=128)).then_inc(ch_g.h, 16)
                    ch_g.n += 16
                    ins = g.dma_start(
                        out=p5_w1[:, wslot, :, 512:1024],
                        in_=w_ff1[:, FFI + mg * 512:FFI + (mg + 1) * 512]
                        .rearrange("(k p) m -> p k m", p=128))
                    ch_g.tick(ins, 16)
                w1_ready[mg] = ch_g.n

                for sm in range(4):
                    ab = sm % 2

                    @block.tensor
                    def _(t):
                        t.wait_ge(ch_g.h, w1_ready[mg])
                        if mg == 0 and sm == 0:
                            t.wait_ge(ch_v.h, h2_ready)
                        slot_need(("bank", 0 + 2 * ab), t)
                        slot_need(("bank", 1 + 2 * ab), t)
                        for k in range(8):
                            t.matmul(pb[0 + 2 * ab][:, :],
                                     p5_w1[:, wslot, k, sm * 128:(sm + 1) * 128],
                                     p5_h2[:, k, :],
                                     start=(k == 0), stop=(k == 7))
                        for k in range(8):
                            ins = t.matmul(
                                pb[1 + 2 * ab][:, :],
                                p5_w1[:, wslot, k,
                                      512 + sm * 128:512 + (sm + 1) * 128],
                                p5_h2[:, k, :],
                                start=(k == 0), stop=(k == 7))
                        ch_t.tick(ins)
                    if sm == 3:
                        slot_done(("p5w1", wslot), ch_t)
                    mm_done = ch_t.n

                    mi = mg * 4 + sm

                    @block.scalar
                    def _(s_):
                        s_.wait_ge(ch_t.h, mm_done)
                        slot_need(("p5g",), s_)
                        s_.activation(out=p5_ag[:, mi, :], in_=pb[0 + 2 * ab][:, :],
                                      func=AF.Identity,
                                      bias=c_bff1[:, mi:mi + 1])
                        ins = s_.activation(out=p5_g[:, :], in_=pb[1 + 2 * ab][:, :],
                                            func=AF.Gelu_apprx_tanh,
                                            bias=c_bff1[:, 32 + mi:33 + mi])
                        ch_s.tick(ins)
                    slot_done(("bank", 0 + 2 * ab), ch_s)
                    slot_done(("bank", 1 + 2 * ab), ch_s)
                    sg_done = ch_s.n

                    @block.vector
                    def _(v):
                        v.wait_ge(ch_s.h, sg_done)
                        ins = v.tensor_mul(p5_ag[:, mi, :], p5_ag[:, mi, :],
                                           p5_g[:, :])
                        ch_v.tick(ins)
                    slot_done(("p5g",), ch_v)

            ag_ready = ch_v.n
            for mp in range(2):
                @block.gpsimd
                def _(g):
                    if tq == 0 and mp == 0:
                        ch_t.wait(g)
                        ch_v.wait(g)
                    slot_need(("p5w2full",), g)
                    ins = g.dma_start(
                        out=p5_w2[:, :, :],
                        in_=w_ff2[:, mp * 512:(mp + 1) * 512]
                        .rearrange("(c p) m -> p c m", p=128))
                    ch_g.tick(ins, 16)
                w2r = ch_g.n

                for kc in range(32):
                    @block.tensor
                    def _(t):
                        if kc == 0:
                            t.wait_ge(ch_g.h, w2r)
                            t.wait_ge(ch_v.h, ag_ready)
                            for m4 in range(4):
                                slot_need(("bank", m4), t)
                        for m4 in range(4):
                            ins = t.matmul(
                                pb[m4][:, :],
                                p5_w2[:, kc, m4 * 128:(m4 + 1) * 128],
                                p5_ag[:, kc, :],
                                start=(kc == 0), stop=(kc == 31))
                        if kc == 31:
                            ch_t.tick(ins)

                ff2_done = ch_t.n
                slot_done(("p5w2full",), ch_t)

                @block.vector
                def _(v):
                    v.wait_ge(ch_t.h, ff2_done)
                    for m4 in range(4):
                        mi = mp * 4 + m4
                        ins = v.scalar_tensor_tensor(
                            out=p5_xm[:, mi, :], in0=pb[m4][:, :],
                            scalar=c_bff2[:, mi:mi + 1],
                            in1=p5_xm[:, mi, :], op0=OP.add, op1=OP.add)
                    ch_v.tick(ins)
                for m4 in range(4):
                    slot_done(("bank", m4), ch_v)

            if debug:
                @block.sync
                def _(sync):
                    ch_v.wait(sync)
                    for k in range(8):
                        ch_st.tick(sync.dma_start(
                            out=dbg["attn"][k * 128:(k + 1) * 128, hsl],
                            in_=p5_xm[:, k, :]), 16)

            for k in range(8):
                @block.sync
                def _(sync):
                    if tq == 0 and k == 0:
                        ch_g.wait(sync)
                    ch_v.wait(sync)
                    if debug:
                        ch_st.wait(sync)
                    ch_st.tick(sync.dma_start(
                        out=p5_xt[:, :], in_=x_sl[k * 128:(k + 1) * 128, hsl]),
                        16)
                    ins = sync.dma_start(
                        out=p5_mt[:, :], in_=moe_dram[k * 128:(k + 1) * 128, hsl])
                    ch_st.tick(ins, 16)

                @block.vector
                def _(v):
                    ch_st.wait(v)
                    v.tensor_add(p5_xt[:, :], p5_xm[:, k, :], p5_xt[:, :])
                    vfence(v)
                    ins = v.tensor_add(p5_mt[:, :], p5_mt[:, :], p5_xt[:, :])
                    ch_v.tick(ins)

                @block.sync
                def _(sync):
                    ch_v.wait(sync)
                    ins = sync.dma_start(
                        out=out_fm[k * 128:(k + 1) * 128, hsl], in_=p5_mt[:, :])
                    ch_st.tick(ins, 16)

        @block.sync
        def _(sync):
            ch_st.wait(sync)
            ch_g.wait(sync)
            ch_cc.wait(sync)

    return nc, dbg


def _prep_inputs(inputs):
    f32 = np.float32
    x = inputs["L0_attn_x"]
    w_in = inputs["L0_attn_w_in"]
    conv_w = inputs["L0_attn_conv_w"]
    conv_b = inputs["L0_attn_conv_b"]
    w_x = inputs["L0_attn_w_x"]
    w_dt = inputs["L0_attn_w_dt"]
    b_dt = inputs["L0_attn_b_dt"]
    a_log = inputs["L0_attn_a_log"]
    d_skip = inputs["L0_attn_d_skip"]
    w_out = inputs["L0_attn_w_out"]

    def chcol(v, q):
        return np.ascontiguousarray(
            v[q * CH:(q + 1) * CH].reshape(4, 128).T).astype(f32)

    def dcol(v):
        return np.ascontiguousarray(v.reshape(8, 128).T).astype(f32)

    rtw1 = np.ascontiguousarray(
        np.concatenate([inputs["L0_ffn_router_w"].T,
                        inputs["L0_ffn_actor_w1"].T], axis=1)).astype(f32)
    gate_wT = np.ascontiguousarray(
        inputs["L0_ffn_gate_w"].transpose(0, 2, 1)).astype(f32)
    up_wT = np.ascontiguousarray(
        inputs["L0_ffn_up_w"].transpose(0, 2, 1)).astype(f32)
    down_wT = np.ascontiguousarray(
        inputs["L0_ffn_down_w"].transpose(0, 2, 1)).astype(f32)

    shared = dict(
        ln1g=dcol(inputs["L0_attn_ln1_g"]),
        ln1b=dcol(inputs["L0_attn_ln1_b"]),
        ln2g=dcol(inputs["L0_attn_ln2_g"]),
        ln2b=dcol(inputs["L0_attn_ln2_b"]),
        w_ff1=np.ascontiguousarray(inputs["L0_attn_w_ff1"]).astype(f32),
        b_ff1=np.ascontiguousarray(
            inputs["L0_attn_b_ff1"].reshape(64, 128).T).astype(f32),
        w_ff2=np.ascontiguousarray(inputs["L0_attn_w_ff2"]).astype(f32),
        b_ff2=dcol(inputs["L0_attn_b_ff2"]),
        rtw1=rtw1,
        b1row=np.ascontiguousarray(
            inputs["L0_ffn_actor_b1"][None, :]).astype(f32),
        aw2=np.ascontiguousarray(inputs["L0_ffn_actor_w2"].T).astype(f32),
        b2row=np.ascontiguousarray(
            inputs["L0_ffn_actor_b2"][None, :]).astype(f32),
        gate_wT=gate_wT, up_wT=up_wT, down_wT=down_wT,
        ident=np.eye(128, dtype=f32),
        iota6=np.arange(6, dtype=f32)[None, :].copy(),
        rev8=np.arange(7, -1, -1, dtype=f32)[None, :].copy(),
        ones_r=np.ones((1, 128), dtype=f32),
        ones_c=np.ones((128, 1), dtype=f32),
    )

    in_maps = []
    for c in range(8):
        b, q = c // 4, c % 4
        m = dict(shared)
        m["x_sl"] = np.ascontiguousarray(
            x[b, q * TOK:(q + 1) * TOK, :].T).astype(f32)
        m["w_in_xc"] = np.ascontiguousarray(
            w_in[:, q * CH:(q + 1) * CH]).astype(f32)
        m["w_in_z"] = np.ascontiguousarray(
            w_in[:, DIN + q * CH:DIN + (q + 1) * CH]).astype(f32)
        m["conv_w"] = np.ascontiguousarray(
            conv_w[q * CH:(q + 1) * CH].reshape(4, 128, 4)
            .transpose(1, 0, 2).reshape(128, 16)).astype(f32)
        m["conv_b"] = chcol(conv_b, q)
        m["w_x_sl"] = np.ascontiguousarray(
            w_x[q * CH:(q + 1) * CH, :]).astype(f32)
        m["w_dt_sl"] = np.ascontiguousarray(
            w_dt[:, q * CH:(q + 1) * CH]).astype(f32)
        m["b_dt"] = chcol(b_dt, q)
        m["a_log"] = np.ascontiguousarray(
            a_log[q * CH:(q + 1) * CH].reshape(4, 128, 16)
            .transpose(1, 0, 2).reshape(128, 64)).astype(f32)
        m["d_skip"] = chcol(d_skip, q)
        m["w_out_sl"] = np.ascontiguousarray(
            w_out[q * CH:(q + 1) * CH, :]).astype(f32)
        in_maps.append(m)
    return in_maps


def kernel(**inputs):
    in_maps = _prep_inputs(inputs)
    nc, dbg = build_nc()
    res = run_bass_kernel_spmd(nc, in_maps, core_ids=list(range(8)))
    outs = []
    for b in range(B):
        parts = [res.results[b * 4 + q]["out_fm"] for q in range(4)]
        outs.append(np.concatenate(parts, axis=1).T)
    return np.stack(outs, axis=0).astype(np.float32)


# version 07
# speedup vs baseline: 1.8241x; 1.0043x over previous
import numpy as np
import ml_dtypes
import concourse.bass as bass
import concourse.mybir as mybir
from concourse.bass_utils import run_bass_kernel_spmd

F32 = mybir.dt.float32
BF16 = mybir.dt.bfloat16
OP = mybir.AluOpType
AF = mybir.ActivationFunctionType
AX = mybir.AxisListType

B, L, D = 2, 8192, 1024
DIN = 2048
NST = 16
DTR = 64
FFI = 4096
NE = 8
EI = 2688
TOK = 2048
CH = 512
EPS = 1e-5
GROUPS = [[0, 1, 2, 3], [4, 5, 6, 7]]

DEBUG = False


class Chan:

    def __init__(self, nc, name):
        self.h = nc.alloc_semaphore(name)
        self.n = 0

    def tick(self, instr, amt=1):
        instr.then_inc(self.h, amt)
        self.n += amt
        return self.n

    def wait(self, eng, n=None):
        eng.wait_ge(self.h, self.n if n is None else n)


def build_nc(debug=DEBUG):
    nc = bass.Bass("TRN2", target_bir_lowering=False)

    def din(name, shape, dt=F32):
        return nc.dram_tensor(name, shape, dt, kind="ExternalInput").ap()

    def dout(name, shape, dt=F32):
        return nc.dram_tensor(name, shape, dt, kind="ExternalOutput").ap()

    x_sl = din("x_sl", [D, TOK])
    ln1g = din("ln1g", [128, 8])
    ln1b = din("ln1b", [128, 8])
    ln2g = din("ln2g", [128, 8])
    ln2b = din("ln2b", [128, 8])
    w_in_xc = din("w_in_xc", [D, CH])
    w_in_z = din("w_in_z", [D, CH])
    conv_w = din("conv_w", [128, 16])
    conv_b = din("conv_b", [128, 4])
    w_x_sl = din("w_x_sl", [CH, 96])
    w_dt_sl = din("w_dt_sl", [DTR, CH])
    b_dt = din("b_dt", [128, 4])
    a_log = din("a_log", [128, 64])
    d_skip = din("d_skip", [128, 4])
    w_out_sl = din("w_out_sl", [CH, D])
    w_ff1 = din("w_ff1", [D, 2 * FFI])
    b_ff1 = din("b_ff1", [128, 64])
    w_ff2 = din("w_ff2", [FFI, D])
    b_ff2 = din("b_ff2", [128, 8])
    rtw1 = din("rtw1", [D, 58])
    b1row = din("b1row", [1, 50])
    aw2 = din("aw2", [50, 6])
    b2row = din("b2row", [1, 6])
    gate_wT = din("gate_wT", [NE, D, EI])
    up_wT = din("up_wT", [NE, D, EI])
    down_wT = din("down_wT", [NE, EI, D])
    ident = din("ident", [128, 128])
    iota6 = din("iota6", [1, 6])
    rev8 = din("rev8", [1, 8])
    ones_r = din("ones_r", [1, 128])
    ones_c = din("ones_c", [128, 1])

    out_fm = dout("out_fm", [D, TOK])

    dbg = {}
    if debug:
        dbg["xn"] = dout("dbg_xn", [D, TOK])
        dbg["h"] = dout("dbg_h", [D, TOK], BF16)
        dbg["probs"] = dout("dbg_probs", [TOK, 8])
        dbg["kidx"] = dout("dbg_kidx", [TOK, 1])
        dbg["scores"] = dout("dbg_scores", [TOK, 8])
        dbg["hfull"] = dout("dbg_hfull", [4 * D, TOK], BF16)
        dbg["rtraw"] = dout("dbg_rtraw", [TOK, 58])
        dbg["ahT"] = dout("dbg_ahT", [50, 128])
        dbg["praw"] = dout("dbg_praw", [TOK, 6])
        dbg["st1"] = dout("dbg_st1", [128, 6])
        dbg["st2"] = dout("dbg_st2", [128, 8])
        dbg["st3"] = dout("dbg_st3", [128, 8])
        dbg["st4"] = dout("dbg_st4", [128, 1])
        dbg["xcs"] = dout("dbg_xcs", [CH, L], BF16)
        dbg["sz"] = dout("dbg_sz", [CH, L], BF16)
        dbg["dbc"] = dout("dbg_dbc", [96, L])
        dbg["dt"] = dout("dbg_dt", [CH, L], BF16)
        dbg["ys"] = dout("dbg_ys", [CH, L])
        dbg["mamba"] = dout("dbg_mamba", [D, TOK])
        dbg["attn"] = dout("dbg_attn", [D, TOK])
        dbg["moe"] = dout("dbg_moe", [D, TOK])
        dbg["alog2"] = dout("dbg_alog2", [TOK, 6])

    ag_in = nc.dram_tensor("ag_in", [D, TOK], BF16).ap()
    ag_out = nc.dram_tensor("ag_out", [4 * D, TOK], BF16).ap()
    ar_in = nc.dram_tensor("ar_in", [96, L], F32).ap()
    ar_out = nc.dram_tensor("ar_out", [96, L], F32).ap()
    rs_in = nc.dram_tensor("rs_in", [4 * D, TOK], F32).ap()
    rs_out = nc.dram_tensor("rs_out", [D, TOK], F32).ap()
    xn_dram = nc.dram_tensor("xn_dram", [D, TOK], F32).ap()
    xcs_dram = nc.dram_tensor("xcs_dram", [CH, L], BF16).ap()
    sz_dram = nc.dram_tensor("sz_dram", [CH, L], BF16).ap()
    sc_dram = nc.dram_tensor("sc_dram", [NE, TOK], F32).ap()
    moe_dram = nc.dram_tensor("moe_dram", [D, TOK], F32).ap()
    ysd_dram = nc.dram_tensor("ysd_dram", [CH, L], F32).ap()
    dtd_dram = nc.dram_tensor("dtd_dram", [CH, L], BF16).ap()

    sb = nc.alloc_sbuf_tensor

    c_ln1g = sb("c_ln1g", [128, 8], F32)
    c_ln1b = sb("c_ln1b", [128, 8], F32)
    c_ln2g = sb("c_ln2g", [128, 8], F32)
    c_ln2b = sb("c_ln2b", [128, 8], F32)
    c_convw = sb("c_convw", [128, 16], F32)
    c_convb = sb("c_convb", [128, 4], F32)
    c_bdt = sb("c_bdt", [128, 4], F32)
    c_alog = sb("c_alog", [128, 64], F32)
    c_dskip = sb("c_dskip", [128, 4], F32)
    c_ident = sb("c_ident", [128, 128], F32)
    c_iota6 = sb("c_iota6", [128, 6], F32)
    c_rev8 = sb("c_rev8", [128, 8], F32)
    c_ones = sb("c_ones", [1, 128], F32)
    c_onec = sb("c_onec", [128, 1], F32)
    c_b1row = sb("c_b1row", [1, 50], F32)
    c_b2row = sb("c_b2row", [1, 6], F32)
    c_bff1 = sb("c_bff1", [128, 64], F32)
    c_bff2 = sb("c_bff2", [128, 8], F32)
    c_acol = sb("c_acol", [128, 64], F32)
    c_zero = sb("c_zero", [128, 512], F32)

    w_rtw1 = sb("w_rtw1", [128, 8, 58], F32)
    w_aw2 = sb("w_aw2", [50, 6], F32)
    wb_inxc = sb("wb_inxc", [128, 8, CH], BF16)
    wb_inz = sb("wb_inz", [128, 8, CH], BF16)
    wb_x = sb("wb_x", [128, 4, 96], BF16)
    wb_dt = sb("wb_dt", [DTR, CH], BF16)
    wb_out = sb("wb_out", [128, 4, D], BF16)

    t_xnb = sb("t_xnb", [128, 8, TOK], BF16)
    r_mu = sb("r_mu", [1, 512], F32)
    r_m2 = sb("r_m2", [1, 512], F32)
    r_t = sb("r_t", [1, 512], F32)
    r_sd = sb("r_sd", [1, 512], F32)
    rt_probs = sb("rt_probs", [128, 8], F32)
    rt_ah = sb("rt_ah", [128, 50], F32)
    rt_ahT = sb("rt_ahT", [50, 128], F32)
    rt_lg = sb("rt_lg", [128, 8], F32)
    rt_col = sb("rt_col", [128, 1], F32)
    rt_col2 = sb("rt_col2", [128, 1], F32)
    rt_eq6 = sb("rt_eq6", [128, 6], F32)
    rt_srt = sb("rt_srt", [128, 8], F32)
    rt_sc = sb("rt_sc", [128, 16, 8], F32)
    rt_scrow = sb("rt_scrow", [8, 128], F32)
    rt_dbg6 = sb("rt_dbg6", [128, 6], F32)
    dbg_s1t = sb("dbg_s1t", [128, 6], F32)
    dbg_s2t = sb("dbg_s2t", [128, 8], F32)
    dbg_s3t = sb("dbg_s3t", [128, 8], F32)
    dbg_s4t = sb("dbg_s4t", [128, 1], F32)

    _rem = nc.sbuf_bytes_remaining - 256
    base_ov, top_ov = nc.bump_sbuf(_rem)
    OVSZ = top_ov - base_ov
    assert OVSZ >= 131 * 1024, OVSZ

    def ov(name, shape, dt, off):
        h = nc.alloc_sbuf_tensor_at(name, shape, dt, offset=base_ov + off)
        return h

    t_xn = ov("t_xn", [128, 8, TOK], F32, 0)
    t_x = ov("t_x", [128, 8, 512], F32, 64 * 1024)
    t_sq = ov("t_sq", [128, 8, 512], F32, 80 * 1024)
    t_hb = ov("t_hb", [128, 8, TOK], BF16, 96 * 1024)
    t_tmp = ov("t_tmp", [128, 512], F32, 128 * 1024)
    p2_h = ov("p2_h", [128, 8, 2048], BF16, 0)
    p2_xc = ov("p2_xc", [128, 4, 2048], F32, 32 * 1024)
    p2_xcs = ov("p2_xcs", [128, 4, 2048], F32, 64 * 1024)
    p2_xcsb = ov("p2_xcsb", [128, 4, 2048], BF16, 96 * 1024)
    p2_tail = ov("p2_tail", [128, 4, 4], F32, 112 * 1024)
    p2_szb = ov("p2_szb", [128, 2, 512], BF16, 113 * 1024)
    p2_dbst = ov("p2_dbst", [96, 2, 512], F32, 116 * 1024)
    p3_lo = ov("p3_lo", [64, L], BF16, 0)
    p3_dt = ov("p3_dt", [128, 2048], BF16, 16 * 1024)
    p3_dtx = ov("p3_dtx", [128, 2048], BF16, 20 * 1024)
    p3_xcs = ov("p3_xcs", [128, 2048], BF16, 24 * 1024)
    p3_a = ov("p3_a", [128, 2, 2048], F32, 117 * 1024)
    p3_u = ov("p3_u", [128, 2048], BF16, 36 * 1024)
    p3_hh = ov("p3_hh", [128, 2048], BF16, 40 * 1024)
    p3_y = ov("p3_y", [128, 2048], F32, 44 * 1024)
    p3_Bbc = ov("p3_Bbc", [128, 2, 2048], BF16, 52 * 1024)
    p3_Cbc = ov("p3_Cbc", [128, 2, 2048], BF16, 60 * 1024)
    p3_sz = ov("p3_sz", [128, 2048], BF16, 68 * 1024)
    p3_yf = ov("p3_yf", [128, 4, 2048], BF16, 72 * 1024)
    p3_wo = ov("p3_wo", [128, 2, 512], F32, 88 * 1024)
    p3_t1 = ov("p3_t1", [128, 512], F32, 92 * 1024)
    p3_t2 = ov("p3_t2", [128, 512], F32, 94 * 1024)
    p3_carry = ov("p3_carry", [128, 64], F32, 96 * 1024)
    p5_xm = ov("p5_xm", [128, 8, 512], F32, 0)
    p5_h2 = ov("p5_h2", [128, 8, 512], BF16, 16 * 1024)
    p5_sq = ov("p5_sq", [128, 8, 512], F32, 24 * 1024)
    p5_ag = ov("p5_ag", [128, 32, 512], BF16, 24 * 1024)
    p5_w1 = ov("p5_w1", [128, 2, 8, 1024], BF16, 56 * 1024)
    p5_w2 = ov("p5_w2", [128, 32, 512], BF16, 88 * 1024)
    p5_g = ov("p5_g", [128, 512], BF16, 120 * 1024)
    p5_xt = ov("p5_xt", [128, 512], F32, 121 * 1024)
    p5_mt = ov("p5_mt", [128, 512], F32, 123 * 1024)
    p4_dw = ov("p4_dw", [128, 21, 1024], BF16, 0)
    p4_gw = ov("p4_gw", [128, 8, 1408], BF16, 42 * 1024)
    p4_uw = ov("p4_uw", [128, 8, 1408], BF16, 64 * 1024)
    p4_prod = ov("p4_prod", [128, 11, 512], BF16, 86 * 1024)
    p4_sb = ov("p4_sb", [128, 4, 512], BF16, 97 * 1024)
    p4_ds = ov("p4_ds", [128, 2, 512], F32, 101 * 1024)
    p4_scb = ov("p4_scb", [128, TOK], F32, 105 * 1024)

    pb = [nc.alloc_psum_tensor(f"pb{i}", [128, 512], F32) for i in range(8)]

    ch_ld = Chan(nc, "ld")
    ch_g = Chan(nc, "chg")
    ch_t = Chan(nc, "cht")
    ch_v = Chan(nc, "chv")
    ch_s = Chan(nc, "chs")
    ch_st = Chan(nc, "chst")
    ch_cc = Chan(nc, "chcc")

    ch_rh = Chan(nc, "chrh0")

    slot_tbl = {}
    ch_bar = [Chan(nc, f"bar{i}") for i in range(5)]

    def slot_need(key, eng):
        if key in slot_tbl:
            ch, n = slot_tbl[key]
            eng.wait_ge(ch.h, n)

    def slot_done(key, chan):
        slot_tbl[key] = (chan, chan.n)

    def vfence(v):
        ch_rh.tick(v.engine_nop())
        ch_rh.wait(v)

    with nc.Block() as block:
        def phase_barrier():
            @block.tensor
            def _(t):
                t.sem_inc(ch_bar[0].h, 1)
                ch_bar[0].n += 1

            @block.vector
            def _(v):
                v.sem_inc(ch_bar[1].h, 1)
                ch_bar[1].n += 1

            @block.scalar
            def _(s_):
                s_.sem_inc(ch_bar[2].h, 1)
                ch_bar[2].n += 1

            @block.sync
            def _(sy):
                sy.sem_inc(ch_bar[3].h, 1)
                ch_bar[3].n += 1

            @block.gpsimd
            def _(g):
                g.sem_inc(ch_bar[4].h, 1)
                ch_bar[4].n += 1

            vals = [c.n for c in ch_bar]

            @block.tensor
            def _(t):
                for c, nn in zip(ch_bar, vals):
                    t.wait_ge(c.h, nn)

            @block.vector
            def _(v):
                for c, nn in zip(ch_bar, vals):
                    v.wait_ge(c.h, nn)

            @block.scalar
            def _(s_):
                for c, nn in zip(ch_bar, vals):
                    s_.wait_ge(c.h, nn)

            @block.sync
            def _(sy):
                for c, nn in zip(ch_bar, vals):
                    sy.wait_ge(c.h, nn)

            @block.gpsimd
            def _(g):
                for c, nn in zip(ch_bar, vals):
                    g.wait_ge(c.h, nn)

        @block.sync
        def _(sync):
            pairs = [
                (c_ln1g[:], ln1g), (c_ln1b[:], ln1b), (c_ln2g[:], ln2g),
                (c_ln2b[:], ln2b), (c_convw[:], conv_w), (c_convb[:], conv_b),
                (c_bdt[:], b_dt), (c_alog[:], a_log), (c_dskip[:], d_skip),
                (c_ident[:], ident),
                (c_iota6[:], iota6.partition_broadcast(128)),
                (c_rev8[:], rev8.partition_broadcast(128)),
                (c_ones[:], ones_r), (c_onec[:], ones_c),
                (c_b1row[:], b1row), (c_b2row[:], b2row),
                (c_bff1[:], b_ff1), (c_bff2[:], b_ff2),
                (w_rtw1[:], rtw1.rearrange("(k p) m -> p k m", p=128)),
                (w_aw2[:], aw2),
            ]
            for o_, i_ in pairs:
                ch_ld.tick(sync.dma_start(out=o_, in_=i_), 16)

        @block.gpsimd
        def _(g):
            ch_g.tick(g.dma_start(out=wb_inxc[:], in_=w_in_xc.rearrange("(k p) m -> p k m", p=128)), 16)
            ch_g.tick(g.dma_start(out=wb_inz[:], in_=w_in_z.rearrange("(k p) m -> p k m", p=128)), 16)
            ch_g.tick(g.dma_start(out=wb_x[:], in_=w_x_sl.rearrange("(k p) m -> p k m", p=128)), 16)
            ch_g.tick(g.dma_start(out=wb_dt[:], in_=w_dt_sl), 16)
            ch_g.tick(g.dma_start(out=wb_out[:], in_=w_out_sl.rearrange("(k p) m -> p k m", p=128)), 16)

        @block.vector
        def _(v):
            ch_v.tick(v.memset(c_zero[:], 0.0))

        @block.sync
        def _(sync):
            ch_v.wait(sync)
            for m in range(8):
                for j in range(4):
                    ch_st.tick(sync.dma_start(
                        out=moe_dram[m * 128:(m + 1) * 128, j * 512:(j + 1) * 512],
                        in_=c_zero[:]), 16)

        @block.scalar
        def _(s):
            ch_ld.wait(s)
            ch_s.tick(s.activation(out=c_acol[:], in_=c_alog[:], func=AF.Exp))

        @block.vector
        def _(v):
            ch_s.wait(v)
            ch_v.tick(v.tensor_scalar(out=c_acol[:], in0=c_acol[:], scalar1=-1.0,
                                      scalar2=None, op0=OP.mult))

        def emit_ln(srcf, dst_write, pre=None):
            for tt in range(TOK // 512):
                sl = slice(tt * 512, (tt + 1) * 512)
                if pre is not None:
                    pre(tt)

                @block.scalar
                def _(s):
                    ch_st.wait(s)
                    for k in range(8):
                        ins = s.activation(out=t_sq[:, k, :], in_=srcf(k, tt),
                                           func=AF.Square)
                    ch_s.tick(ins)

                @block.tensor
                def _(t):
                    ch_s.wait(t)
                    ch_v.wait(t)
                    for k in range(8):
                        t.matmul(pb[6][0:1, :], c_onec[:, 0:1], srcf(k, tt),
                                 start=(k == 0), stop=(k == 7))
                    for k in range(8):
                        ins = t.matmul(pb[7][0:1, :], c_onec[:, 0:1], t_sq[:, k, :],
                                       start=(k == 0), stop=(k == 7))
                    ch_t.tick(ins)

                @block.vector
                def _(v):
                    ch_t.wait(v)
                    v.tensor_scalar(out=r_mu[:], in0=pb[6][0:1, :], scalar1=1.0 / D,
                                    scalar2=None, op0=OP.mult)
                    v.tensor_scalar(out=r_m2[:], in0=pb[7][0:1, :], scalar1=1.0 / D,
                                    scalar2=EPS, op0=OP.mult, op1=OP.add)
                    v.tensor_mul(r_t[:], r_mu[:], r_mu[:])
                    ch_v.tick(v.tensor_sub(r_m2[:], r_m2[:], r_t[:]))

                @block.scalar
                def _(s):
                    ch_v.wait(s)
                    ch_s.tick(s.activation(out=r_sd[:], in_=r_m2[:], func=AF.Sqrt))

                @block.vector
                def _(v):
                    ch_s.wait(v)
                    ch_v.tick(v.reciprocal(r_sd[:], r_sd[:]))

                @block.tensor
                def _(t):
                    ch_v.wait(t)
                    t.matmul(pb[4][:, :], c_ones[:, :], r_mu[:, :],
                             start=True, stop=True)
                    ins = t.matmul(pb[5][:, :], c_ones[:, :], r_sd[:, :],
                                   start=True, stop=True)
                    ch_t.tick(ins)

                @block.vector
                def _(v):
                    ch_t.wait(v)
                    for k in range(8):
                        ins = dst_write(v, k, sl, tt)
                    ch_v.tick(ins)

        def x_pre(tt):
            @block.sync
            def _(sync):
                if tt > 0:
                    ch_v.wait(sync)
                ins = sync.dma_start(
                    out=t_x[:],
                    in_=x_sl[:, tt * 512:(tt + 1) * 512]
                    .rearrange("(k p) t -> p k t", p=128))
                ch_st.tick(ins, 16)

        def xn_write(v, k, sl, tt):
            v.tensor_sub(t_xn[:, k, sl], t_x[:, k, :], pb[4][:, :])
            return v.tensor_mul(t_xn[:, k, sl], t_xn[:, k, sl], pb[5][:, :])

        emit_ln(lambda k, tt: t_x[:, k, :], xn_write, pre=x_pre)

        @block.sync
        def _(sync):
            ch_v.wait(sync)
            ins = sync.dma_start(out=xn_dram.rearrange("(k p) t -> p k t", p=128),
                                 in_=t_xn[:])
            ch_st.tick(ins, 16)

        @block.vector
        def _(v):
            for k in range(8):
                ins = v.tensor_copy(out=t_xnb[:, k, :], in_=t_xn[:, k, :])
            ch_v.tick(ins)

        def h_write(v, k, sl, tt):
            v.tensor_sub(t_tmp[:, :], t_xn[:, k, sl], pb[4][:, :])
            v.tensor_mul(t_tmp[:, :], t_tmp[:, :], pb[5][:, :])
            return v.tensor_scalar(out=t_hb[:, k, sl], in0=t_tmp[:, :],
                                   scalar1=c_ln1g[:, k:k + 1],
                                   scalar2=c_ln1b[:, k:k + 1],
                                   op0=OP.mult, op1=OP.add)

        emit_ln(lambda k, tt: t_xn[:, k, tt * 512:(tt + 1) * 512], h_write)

        @block.sync
        def _(sync):
            ch_v.wait(sync)
            ins = sync.dma_start(out=ag_in.rearrange("(k p) t -> p k t", p=128),
                                 in_=t_hb[:])
            ch_st.tick(ins, 16)

        @block.gpsimd
        def _(g):
            ch_st.wait(g)
            ins = g.collective_compute(
                "AllGather", OP.bypass, replica_groups=GROUPS,
                ins=[ag_in], outs=[ag_out])
            ch_cc.tick(ins)

        for tt in range(16):
            tsl = slice(tt * 128, (tt + 1) * 128)
            bank = pb[0] if tt % 2 == 0 else pb[1]

            @block.tensor
            def _(t):
                if tt >= 2:
                    ch_s.wait(t)
                    ch_v.wait(t)
                for k in range(8):
                    t.matmul(bank[:, 0:58], t_xn[:, k, tsl], w_rtw1[:, k, :],
                             start=(k == 0), stop=False)
                ins = t.matmul(bank[:, 8:58], c_ones[:, :], c_b1row[:, :],
                               start=False, stop=True)
                ch_t.tick(ins)

            if debug:
                @block.vector
                def _(v):
                    ch_t.wait(v)
                    ch_v.tick(v.tensor_copy(out=rt_ah[:, 0:58] if False else rt_sc[:, 0, :] if False else rt_ahT[0:128, 0:58] if False else rt_ah[:, :], in_=bank[:, 8:58]))

                @block.sync
                def _(sync):
                    ch_v.wait(sync)
                    ch_st.tick(sync.dma_start(out=dbg["rtraw"][tsl, 8:58],
                                              in_=rt_ah[:, :]), 16)

                @block.vector
                def _(v):
                    ch_st.wait(v)
                    ch_v.tick(v.tensor_copy(out=rt_lg[:, :], in_=bank[:, 0:8]))

                @block.sync
                def _(sync):
                    ch_v.wait(sync)
                    ch_st.tick(sync.dma_start(out=dbg["rtraw"][tsl, 0:8],
                                              in_=rt_lg[:, :]), 16)

            @block.vector
            def _(v):
                ch_t.wait(v)
                ch_s.wait(v)
                if debug:
                    ch_st.wait(v)
                v.reduce_max(out=rt_col[:], in_=bank[:, 0:8], axis=AX.X)
                vfence(v)
                ins = v.tensor_scalar(out=rt_col[:], in0=rt_col[:], scalar1=-1.0,
                                      scalar2=None, op0=OP.mult)
                ch_rh.tick(ins)

            @block.scalar
            def _(s):
                ch_rh.wait(s)
                if debug:
                    ch_st.wait(s)
                s.activation(out=rt_probs[:], in_=bank[:, 0:8], func=AF.Exp,
                             bias=rt_col[:, 0:1])
                ins = s.activation(out=rt_ah[:], in_=bank[:, 8:58],
                                   func=AF.Gelu_apprx_tanh)
                ch_s.tick(ins)

            @block.vector
            def _(v):
                ch_s.wait(v)
                v.tensor_reduce(out=rt_col[:], in_=rt_probs[:],
                                axis=AX.X, op=OP.add)
                vfence(v)
                v.reciprocal(rt_col[:], rt_col[:])
                vfence(v)
                ins = v.tensor_scalar(out=rt_probs[:], in0=rt_probs[:],
                                      scalar1=rt_col[:, 0:1], scalar2=None,
                                      op0=OP.mult)
                ch_v.tick(ins)

            @block.tensor
            def _(t):
                ch_s.wait(t)
                ch_t.tick(t.transpose(pb[2][0:50, 0:128], rt_ah[:, :],
                                      c_ident[:, :]))

            @block.vector
            def _(v):
                ch_t.wait(v)
                ins = v.tensor_copy(out=rt_ahT[:, :], in_=pb[2][0:50, 0:128])
                ch_v.tick(ins)

            if debug and tt == 0:
                @block.sync
                def _(sync):
                    ch_v.wait(sync)
                    ch_st.tick(sync.dma_start(out=dbg["ahT"], in_=rt_ahT[:, :]), 16)

            @block.tensor
            def _(t):
                ch_v.wait(t)
                if debug and tt == 0:
                    ch_st.wait(t)
                t.matmul(pb[3][:, 0:6], rt_ahT[:, :], w_aw2[:, :],
                         start=True, stop=False)
                ins = t.matmul(pb[3][:, 0:6], c_ones[:, :], c_b2row[:, :],
                               start=False, stop=True)
                ch_t.tick(ins)

            if debug:
                @block.vector
                def _(v):
                    ch_t.wait(v)
                    ch_st.wait(v)
                    ch_v.tick(v.tensor_copy(out=rt_dbg6[:, :], in_=pb[3][:, 0:6]))

                @block.sync
                def _(sync):
                    ch_v.wait(sync)
                    ch_st.tick(sync.dma_start(out=dbg["praw"][tsl, :],
                                              in_=rt_dbg6[:, :]), 16)

            @block.vector
            def _(v):
                ch_t.wait(v)
                if debug:
                    ch_st.wait(v)
                v.memset(rt_lg[:, 6:8], -1e30)
                v.tensor_copy(out=rt_eq6[:, :], in_=pb[3][:, 0:6])
                vfence(v)
                v.tensor_scalar(out=rt_lg[:, 0:6], in0=rt_eq6[:, :],
                                scalar1=-30.0, scalar2=30.0,
                                op0=OP.max, op1=OP.min)
                vfence(v)
                if debug:
                    v.tensor_copy(out=rt_dbg6[:, :], in_=rt_lg[:, 0:6])
                v.reduce_max(out=rt_col[:], in_=rt_lg[:, :], axis=AX.X)
                vfence(v)
                v.tensor_scalar(out=rt_srt[:, :], in0=rt_lg[:, :],
                                scalar1=rt_col[:, 0:1], scalar2=None,
                                op0=OP.is_ge)
                vfence(v)
                v.tensor_mul(rt_srt[:, :], rt_srt[:, :], c_rev8[:, :])
                vfence(v)
                v.tensor_reduce(out=rt_col2[:], in_=rt_srt[:, :],
                                axis=AX.X, op=OP.max)
                vfence(v)
                v.tensor_scalar(out=rt_col2[:], in0=rt_col2[:],
                                scalar1=-1.0, scalar2=7.0,
                                op0=OP.mult, op1=OP.add)
                vfence(v)
                if debug and tt == 0:
                    v.tensor_copy(out=dbg_s1t[:, :], in_=rt_eq6[:, :])
                    v.tensor_copy(out=dbg_s2t[:, :], in_=rt_lg[:, :])
                    v.tensor_copy(out=dbg_s3t[:, :], in_=rt_srt[:, :])
                    ch_v.tick(v.tensor_copy(out=dbg_s4t[:, :], in_=rt_col2[:, :]))
                v.max(out=rt_srt[:, :], in_=rt_probs[:, :])
                vfence(v)
                v.tensor_scalar(out=rt_eq6[:, :], in0=c_iota6[:, :],
                                scalar1=rt_col2[:, 0:1], scalar2=None,
                                op0=OP.is_equal)
                vfence(v)
                v.tensor_mul(rt_eq6[:, :], rt_eq6[:, :], rt_srt[:, 0:6])
                vfence(v)
                v.tensor_reduce(out=rt_col[:], in_=rt_eq6[:, :], axis=AX.X,
                                op=OP.add)
                vfence(v)
                v.tensor_scalar(out=rt_sc[:, tt, :], in0=rt_probs[:, :],
                                scalar1=rt_col[:, 0:1], scalar2=None,
                                op0=OP.is_ge)
                vfence(v)
                ins = v.tensor_mul(rt_sc[:, tt, :], rt_sc[:, tt, :],
                                   rt_probs[:, :])
                ch_v.tick(ins)

            if debug:
                @block.sync
                def _(sync):
                    ch_v.wait(sync)
                    ch_st.tick(sync.dma_start(out=dbg["probs"][tsl, :],
                                              in_=rt_probs[:, :]), 16)
                    ch_st.tick(sync.dma_start(out=dbg["kidx"][tsl, :],
                                              in_=rt_col2[:, :]), 16)
                    ch_st.tick(sync.dma_start(out=dbg["scores"][tsl, :],
                                              in_=rt_sc[:, tt, :]), 16)
                    ch_st.tick(sync.dma_start(out=dbg["alog2"][tsl, :],
                                              in_=rt_dbg6[:, :]), 16)
                    if tt == 0:
                        ch_st.tick(sync.dma_start(out=dbg["st1"], in_=dbg_s1t[:, :]), 16)
                        ch_st.tick(sync.dma_start(out=dbg["st2"], in_=dbg_s2t[:, :]), 16)
                        ch_st.tick(sync.dma_start(out=dbg["st3"], in_=dbg_s3t[:, :]), 16)
                        ch_st.tick(sync.dma_start(out=dbg["st4"], in_=dbg_s4t[:, :]), 16)

        for tt in range(16):
            bank = pb[2]

            @block.tensor
            def _(t):
                if tt > 0:
                    ch_st.wait(t)
                ins = t.transpose(bank[0:8, 0:128], rt_sc[:, tt, :], c_ident[:, :])
                ch_t.tick(ins)

            @block.vector
            def _(v):
                ch_t.wait(v)
                ins = v.tensor_copy(out=rt_scrow[:, :], in_=bank[0:8, 0:128])
                ch_v.tick(ins)

            @block.sync
            def _(sync):
                ch_v.wait(sync)
                ins = sync.dma_start(out=sc_dram[:, tt * 128:(tt + 1) * 128],
                                     in_=rt_scrow[:, :])
                ch_st.tick(ins, 16)

        if debug:
            @block.sync
            def _(sync):
                ch_v.wait(sync)
                ch_cc.wait(sync)
                ch_st.tick(sync.dma_start(
                    out=dbg["xn"].rearrange("(k p) t -> p k t", p=128),
                    in_=t_xn[:]), 16)
                ch_st.tick(sync.dma_start(
                    out=dbg["h"].rearrange("(k p) t -> p k t", p=128),
                    in_=t_hb[:]), 16)
                ch_st.tick(sync.dma_start(out=dbg["hfull"], in_=ag_out), 16)

        phase_barrier()
        for blk in range(4):
            @block.sync
            def _(sync):
                if blk == 0:
                    ch_cc.wait(sync)
                    ch_v.wait(sync)
                    ch_s.wait(sync)
                else:
                    ch_v.wait(sync)
                for k in range(8):
                    ch_st.tick(sync.dma_start(
                        out=p2_h[:, k, :],
                        in_=ag_out[blk * D + k * 128:blk * D + (k + 1) * 128, :]), 16)

            h_ready = ch_st.n
            for m in range(8):
                wsl = wb_inxc if m < 4 else wb_inz
                mm = m % 4

                for tt in range(4):
                    gi = (m * 4 + tt) % 2
                    bank = pb[gi]

                    @block.tensor
                    def _(t):
                        if blk == 0 and m == 0 and tt == 0:
                            ch_g.wait(t)
                        if m == 0 and tt == 0:
                            t.wait_ge(ch_st.h, h_ready)
                        slot_need(("bank", gi), t)
                        for k in range(8):
                            ins = t.matmul(
                                bank[:, :],
                                wsl[:, k, mm * 128:(mm + 1) * 128],
                                p2_h[:, k, tt * 512:(tt + 1) * 512],
                                start=(k == 0), stop=(k == 7))
                        ch_t.tick(ins)
                    xz_done = ch_t.n

                    if m < 4:
                        @block.vector
                        def _(v):
                            v.wait_ge(ch_t.h, xz_done)
                            ins = v.tensor_copy(
                                out=p2_xc[:, mm, tt * 512:(tt + 1) * 512],
                                in_=bank[:, :])
                            ch_v.tick(ins)
                        slot_done(("bank", gi), ch_v)
                    else:
                        @block.scalar
                        def _(s_):
                            s_.wait_ge(ch_t.h, xz_done)
                            slot_need(("p2sz", tt % 2), s_)
                            ins = s_.activation(out=p2_szb[:, tt % 2, :],
                                                in_=bank[:, :], func=AF.Silu)
                            ch_s.tick(ins)
                        slot_done(("bank", gi), ch_s)
                        sz_done = ch_s.n

                        @block.sync
                        def _(sync):
                            sync.wait_ge(ch_s.h, sz_done)
                            ins = sync.dma_start(
                                out=sz_dram[mm * 128:(mm + 1) * 128,
                                            blk * 2048 + tt * 512:
                                            blk * 2048 + (tt + 1) * 512],
                                in_=p2_szb[:, tt % 2, :])
                            ch_st.tick(ins, 16)
                        slot_done(("p2sz", tt % 2), ch_st)

            for d in range(4):
                @block.vector
                def _(v):
                    ch_v.wait(v)
                    v.tensor_scalar(out=p2_xcs[:, d, :], in0=p2_xc[:, d, :],
                                    scalar1=c_convw[:, d * 4 + 3:d * 4 + 4],
                                    scalar2=None, op0=OP.mult)
                    for kk in range(3):
                        sh = 3 - kk
                        v.scalar_tensor_tensor(
                            out=p2_xcs[:, d, sh:2048],
                            in0=p2_xc[:, d, 0:2048 - sh],
                            scalar=c_convw[:, d * 4 + kk:d * 4 + kk + 1],
                            in1=p2_xcs[:, d, sh:2048],
                            op0=OP.mult, op1=OP.add)
                        if blk > 0:
                            v.scalar_tensor_tensor(
                                out=p2_xcs[:, d, 0:sh],
                                in0=p2_tail[:, d, 3 - sh:3],
                                scalar=c_convw[:, d * 4 + kk:d * 4 + kk + 1],
                                in1=p2_xcs[:, d, 0:sh],
                                op0=OP.mult, op1=OP.add)
                    ins = v.tensor_copy(out=p2_tail[:, d, 0:3],
                                        in_=p2_xc[:, d, 2045:2048])
                    ch_v.tick(ins)

                @block.scalar
                def _(s_):
                    ch_v.wait(s_)
                    ins = s_.activation(out=p2_xcs[:, d, :], in_=p2_xcs[:, d, :],
                                        func=AF.Silu,
                                        bias=c_convb[:, d:d + 1])
                    ch_s.tick(ins)

                @block.vector
                def _(v):
                    ch_s.wait(v)
                    ins = v.tensor_copy(out=p2_xcsb[:, d, :], in_=p2_xcs[:, d, :])
                    ch_v.tick(ins)

                @block.sync
                def _(sync):
                    ch_v.wait(sync)
                    ins = sync.dma_start(
                        out=xcs_dram[d * 128:(d + 1) * 128,
                                     blk * 2048:(blk + 1) * 2048],
                        in_=p2_xcsb[:, d, :])
                    ch_st.tick(ins, 16)

            xcsb_ready = ch_v.n
            for tt in range(4):
                @block.tensor
                def _(t):
                    if tt == 0:
                        t.wait_ge(ch_v.h, xcsb_ready)
                    slot_need(("bank", 2), t)
                    for d in range(4):
                        ins = t.matmul(
                            pb[2][0:96, :],
                            wb_x[:, d, :],
                            p2_xcsb[:, d, tt * 512:(tt + 1) * 512],
                            start=(d == 0), stop=(d == 3))
                    ch_t.tick(ins)
                dbc_done = ch_t.n

                @block.vector
                def _(v):
                    v.wait_ge(ch_t.h, dbc_done)
                    slot_need(("p2db", tt % 2), v)
                    ins = v.tensor_copy(out=p2_dbst[:, tt % 2, :],
                                        in_=pb[2][0:96, :])
                    ch_v.tick(ins)
                slot_done(("bank", 2), ch_v)
                db_done = ch_v.n

                @block.sync
                def _(sync):
                    sync.wait_ge(ch_v.h, db_done)
                    ins = sync.dma_start(
                        out=ar_in[:, blk * 2048 + tt * 512:
                                  blk * 2048 + (tt + 1) * 512],
                        in_=p2_dbst[:, tt % 2, :])
                    ch_st.tick(ins, 16)
                slot_done(("p2db", tt % 2), ch_st)

        @block.gpsimd
        def _(g):
            ch_st.wait(g)
            ins = g.collective_compute(
                "AllReduce", OP.add, replica_groups=GROUPS,
                ins=[ar_in], outs=[ar_out])
            ch_cc.tick(ins)

        if debug:
            @block.gpsimd
            def _(g):
                ch_cc.wait(g)
                ch_g.tick(g.dma_start(out=dbg["xcs"], in_=xcs_dram), 16)
                ch_g.tick(g.dma_start(out=dbg["sz"], in_=sz_dram), 16)
                ch_g.tick(g.dma_start(out=dbg["dbc"], in_=ar_out), 16)

        phase_barrier()
        @block.gpsimd
        def _(g):
            ch_cc.wait(g)
            ch_g.tick(g.dma_start(out=p3_lo[:], in_=ar_out[0:64, :]), 16)

        @block.vector
        def _(v):
            ch_v.tick(v.memset(p3_carry[:], 0.0))

        bc_ready = {}
        for qt in range(4):
            for d in range(4):
                for tt in range(4):
                    csl = slice(qt * 2048 + tt * 512, qt * 2048 + (tt + 1) * 512)

                    @block.tensor
                    def _(t):
                        if qt == 0 and d == 0 and tt == 0:
                            ch_g.wait(t)
                        slot_need(("bank", 3), t)
                        ins = t.matmul(pb[3][:, :],
                                       wb_dt[:, d * 128:(d + 1) * 128],
                                       p3_lo[:, csl], start=True, stop=True)
                        ch_t.tick(ins)

                    @block.scalar
                    def _(s_):
                        ch_t.wait(s_)
                        ins = s_.activation(out=p3_t1[:, :], in_=pb[3][:, :],
                                            func=AF.Sigmoid,
                                            bias=c_bdt[:, d:d + 1])
                        ch_s.tick(ins)
                    slot_done(("bank", 3), ch_s)

                    @block.vector
                    def _(v):
                        ch_s.wait(v)
                        ins = v.tensor_scalar(out=p3_t2[:, :], in0=p3_t1[:, :],
                                              scalar1=-1.0, scalar2=1.0,
                                              op0=OP.mult, op1=OP.add)
                        ch_v.tick(ins)

                    @block.scalar
                    def _(s_):
                        ch_v.wait(s_)
                        ins = s_.activation(out=p3_t2[:, :], in_=p3_t2[:, :],
                                            func=AF.Ln)
                        ch_s.tick(ins)

                    @block.vector
                    def _(v):
                        ch_s.wait(v)
                        ins = v.tensor_scalar(
                            out=p3_dt[:, tt * 512:(tt + 1) * 512],
                            in0=p3_t2[:, :], scalar1=-1.0, scalar2=None,
                            op0=OP.mult)
                        ch_v.tick(ins)

                @block.sync
                def _(sync):
                    if qt > 0 or d > 0:
                        ch_v.wait(sync)
                    ins = sync.dma_start(
                        out=p3_xcs[:, :],
                        in_=xcs_dram[d * 128:(d + 1) * 128,
                                     qt * 2048:(qt + 1) * 2048])
                    ch_st.tick(ins, 16)

                @block.vector
                def _(v):
                    ch_st.wait(v)
                    ins = v.tensor_mul(p3_dtx[:, :], p3_dt[:, :], p3_xcs[:, :])
                    ch_v.tick(ins)
                dt_ready = ch_v.n
                a_ready = {}

                if debug:
                    @block.sync
                    def _(sync):
                        ch_v.wait(sync)
                        ins = sync.dma_start(
                            out=dtd_dram[d * 128:(d + 1) * 128,
                                         qt * 2048:(qt + 1) * 2048],
                            in_=p3_dt[:, :])
                        ch_st.tick(ins, 16)

                for n in range(16):
                    if (d, n) not in bc_ready or True:
                        pass
                    if n == 0:
                        @block.gpsimd
                        def _(g):
                            if qt > 0 or d > 0:
                                ch_v.wait(g)
                            g.dma_start(out=p3_Bbc[:, 0, :], in_=ar_out[
                                64 + 0:65 + 0, qt * 2048:(qt + 1) * 2048]
                                .partition_broadcast(128)).then_inc(ch_g.h, 16)
                            ch_g.n += 16
                            ins = g.dma_start(out=p3_Cbc[:, 0, :], in_=ar_out[
                                80 + 0:81 + 0, qt * 2048:(qt + 1) * 2048]
                                .partition_broadcast(128))
                            ch_g.tick(ins, 16)
                    bc_ready[n] = ch_g.n

                    if n + 1 < 16:
                        @block.gpsimd
                        def _(g):
                            if qt > 0 or d > 0 or n >= 1:
                                ch_v.wait(g)
                            g.dma_start(out=p3_Bbc[:, (n + 1) % 2, :],
                                        in_=ar_out[64 + n + 1:65 + n + 1,
                                                   qt * 2048:(qt + 1) * 2048]
                                        .partition_broadcast(128)).then_inc(ch_g.h, 16)
                            ch_g.n += 16
                            ins = g.dma_start(out=p3_Cbc[:, (n + 1) % 2, :],
                                              in_=ar_out[80 + n + 1:81 + n + 1,
                                                         qt * 2048:(qt + 1) * 2048]
                                              .partition_broadcast(128))
                            ch_g.tick(ins, 16)

                    @block.scalar
                    def _(s_):
                        s_.wait_ge(ch_v.h, dt_ready)
                        slot_need(("p3a", n % 2), s_)
                        ins = s_.activation(out=p3_a[:, n % 2, :],
                                            in_=p3_dt[:, :],
                                            func=AF.Exp,
                                            scale=c_acol[:, d * 16 + n:
                                                         d * 16 + n + 1])
                        ch_s.tick(ins)
                    a_ready[n] = ch_s.n

                    @block.vector
                    def _(v):
                        ch_g.wait(v, bc_ready[n])
                        v.wait_ge(ch_s.h, a_ready[n])
                        v.tensor_mul(p3_u[:, :], p3_dtx[:, :],
                                     p3_Bbc[:, n % 2, :])
                        vfence(v)
                        ini = 0.0 if qt == 0 else p3_carry[:, d * 16 + n:
                                                           d * 16 + n + 1]
                        v.tensor_tensor_scan(out=p3_hh[:, :],
                                             data0=p3_a[:, n % 2, :],
                                             data1=p3_u[:, :], initial=ini,
                                             op0=OP.mult, op1=OP.add)
                        vfence(v)
                        v.tensor_copy(out=p3_carry[:, d * 16 + n:d * 16 + n + 1],
                                      in_=p3_hh[:, 2047:2048])
                        if n == 0:
                            ins = v.tensor_mul(p3_y[:, :], p3_hh[:, :],
                                               p3_Cbc[:, n % 2, :])
                        else:
                            v.tensor_mul(p3_u[:, :], p3_hh[:, :],
                                         p3_Cbc[:, n % 2, :])
                            vfence(v)
                            ins = v.tensor_add(p3_y[:, :], p3_y[:, :], p3_u[:, :])
                        ch_v.tick(ins)
                    slot_done(("p3a", n % 2), ch_v)

                if debug:
                    @block.sync
                    def _(sync):
                        ch_v.wait(sync)
                        ins = sync.dma_start(
                            out=ysd_dram[d * 128:(d + 1) * 128,
                                         qt * 2048:(qt + 1) * 2048],
                            in_=p3_y[:, :])
                        ch_st.tick(ins, 16)

                @block.sync
                def _(sync):
                    if qt > 0 or d > 0:
                        ch_v.wait(sync)
                    ins = sync.dma_start(
                        out=p3_sz[:, :],
                        in_=sz_dram[d * 128:(d + 1) * 128,
                                    qt * 2048:(qt + 1) * 2048])
                    ch_st.tick(ins, 16)

                @block.vector
                def _(v):
                    ch_st.wait(v)
                    v.scalar_tensor_tensor(out=p3_y[:, :], in0=p3_xcs[:, :],
                                           scalar=c_dskip[:, d:d + 1],
                                           in1=p3_y[:, :], op0=OP.mult,
                                           op1=OP.add)
                    vfence(v)
                    ins = v.tensor_mul(p3_yf[:, d, :], p3_y[:, :], p3_sz[:, :])
                    ch_v.tick(ins)

            yf_ready = ch_v.n
            for m in range(8):
                for tt in range(4):
                    gi = (m * 4 + tt) % 2
                    bank = pb[3 + gi]

                    @block.tensor
                    def _(t):
                        if m == 0 and tt == 0:
                            t.wait_ge(ch_v.h, yf_ready)
                        slot_need(("bank", 3 + gi), t)
                        for d in range(4):
                            ins = t.matmul(
                                bank[:, :],
                                wb_out[:, d, m * 128:(m + 1) * 128],
                                p3_yf[:, d, tt * 512:(tt + 1) * 512],
                                start=(d == 0), stop=(d == 3))
                        ch_t.tick(ins)
                    mm_done = ch_t.n

                    @block.vector
                    def _(v):
                        v.wait_ge(ch_t.h, mm_done)
                        slot_need(("p3wo", gi), v)
                        ins = v.tensor_copy(out=p3_wo[:, gi, :],
                                            in_=bank[:, :])
                        ch_v.tick(ins)
                    slot_done(("bank", 3 + gi), ch_v)
                    ev_done = ch_v.n

                    @block.sync
                    def _(sync):
                        sync.wait_ge(ch_v.h, ev_done)
                        ins = sync.dma_start(
                            out=rs_in[qt * D + m * 128:qt * D + (m + 1) * 128,
                                      tt * 512:(tt + 1) * 512],
                            in_=p3_wo[:, gi, :])
                        ch_st.tick(ins, 16)
                    slot_done(("p3wo", gi), ch_st)

        @block.gpsimd
        def _(g):
            ch_st.wait(g)
            ins = g.collective_compute(
                "ReduceScatter", OP.add, replica_groups=GROUPS,
                ins=[rs_in], outs=[rs_out])
            ch_cc.tick(ins)

        if debug:
            @block.gpsimd
            def _(g):
                ch_cc.wait(g)
                ch_g.tick(g.dma_start(out=dbg["dt"], in_=dtd_dram), 16)
                ch_g.tick(g.dma_start(out=dbg["ys"], in_=ysd_dram), 16)
                ch_g.tick(g.dma_start(out=dbg["mamba"], in_=rs_out), 16)

        phase_barrier()
        IH = [(0, 10), (10, 11)]
        for e in range(NE):
            @block.gpsimd
            def _(g):
                if e == 0:
                    ch_v.wait(g)
                    ch_st.wait(g)
                    ch_s.wait(g)
                else:
                    ch_t.wait(g)
                    ch_v.wait(g)
                g.dma_start(out=p4_scb[:, :],
                            in_=sc_dram[e:e + 1, :].partition_broadcast(128)
                            ).then_inc(ch_g.h, 16)
                ch_g.n += 16
                ins = g.dma_start(
                    out=p4_dw[:, :, :],
                    in_=down_wT[e].rearrange("(c p) m -> p c m", p=128))
                ch_g.tick(ins, 16)
            scb_ready = ch_g.n

            for ih, (ib, ihc) in enumerate(IH):
                @block.gpsimd
                def _(g):
                    if ih > 0:
                        ch_t.wait(g)
                    g.dma_start(
                        out=p4_gw[:, :, 0:ihc * 128],
                        in_=gate_wT[e][:, ib * 128:(ib + ihc) * 128]
                        .rearrange("(k p) m -> p k m", p=128)).then_inc(ch_g.h, 16)
                    ch_g.n += 16
                    ins = g.dma_start(
                        out=p4_uw[:, :, 0:ihc * 128],
                        in_=up_wT[e][:, ib * 128:(ib + ihc) * 128]
                        .rearrange("(k p) m -> p k m", p=128))
                    ch_g.tick(ins, 16)

                w_ready = ch_g.n
                for tt in range(4):
                    tsl = slice(tt * 512, (tt + 1) * 512)
                    for ic in range(ihc):
                        gb = ic % 2
                        sb0, sb1 = 2 * gb, 2 * gb + 1

                        @block.tensor
                        def _(t):
                            t.wait_ge(ch_g.h, w_ready)
                            slot_need(("bank", 0 + 2 * gb), t)
                            slot_need(("bank", 1 + 2 * gb), t)
                            slot_need(("prod", ic), t)
                            for k in range(8):
                                t.matmul(pb[0 + 2 * gb][:, :],
                                         p4_gw[:, k, ic * 128:(ic + 1) * 128],
                                         t_xnb[:, k, tsl],
                                         start=(k == 0), stop=(k == 7))
                            for k in range(8):
                                ins = t.matmul(
                                    pb[1 + 2 * gb][:, :],
                                    p4_uw[:, k, ic * 128:(ic + 1) * 128],
                                    t_xnb[:, k, tsl],
                                    start=(k == 0), stop=(k == 7))
                            ch_t.tick(ins)
                        gu_done = ch_t.n

                        @block.scalar
                        def _(s_):
                            s_.wait_ge(ch_t.h, gu_done)
                            slot_need(("p4sb", sb0), s_)
                            ins = s_.activation(out=p4_sb[:, sb0, :],
                                                in_=pb[0 + 2 * gb][:, :],
                                                func=AF.Silu)
                            ch_s.tick(ins)
                        slot_done(("bank", 0 + 2 * gb), ch_s)
                        g_done = ch_s.n

                        @block.vector
                        def _(v):
                            v.wait_ge(ch_s.h, g_done)
                            if e == 0 and ih == 0 and tt == 0 and ic == 0:
                                ch_st.wait(v)
                            slot_need(("p4sb", sb1), v)
                            v.tensor_copy(out=p4_sb[:, sb1, :],
                                          in_=pb[1 + 2 * gb][:, :])
                            vfence(v)
                            ins = v.tensor_mul(p4_prod[:, ic, :],
                                               p4_sb[:, sb0, :],
                                               p4_sb[:, sb1, :])
                            ch_v.tick(ins)
                        slot_done(("bank", 1 + 2 * gb), ch_v)
                        slot_done(("p4sb", sb0), ch_v)
                        slot_done(("p4sb", sb1), ch_v)
                    prod_ready = ch_v.n

                    for m in range(8):
                        db = 4 + m % 2

                        @block.tensor
                        def _(t):
                            if m == 0:
                                t.wait_ge(ch_v.h, prod_ready)
                            slot_need(("bank", db), t)
                            for ic in range(ihc):
                                ins = t.matmul(
                                    pb[db][:, :],
                                    p4_dw[:, ib + ic, m * 128:(m + 1) * 128],
                                    p4_prod[:, ic, :],
                                    start=(ic == 0), stop=(ic == ihc - 1))
                            ch_t.tick(ins)
                        dn_done = ch_t.n
                        if m == 7:
                            for icx in range(ihc):
                                slot_done(("prod", icx), ch_t)

                        @block.vector
                        def _(v):
                            v.wait_ge(ch_t.h, dn_done)
                            v.wait_ge(ch_g.h, scb_ready)
                            slot_need(("p4ds", m % 2), v)
                            ins = v.tensor_mul(p4_ds[:, m % 2, :],
                                               pb[db][:, :],
                                               p4_scb[:, tsl])
                            ch_v.tick(ins)
                        slot_done(("bank", db), ch_v)
                        ds_done = ch_v.n

                        @block.gpsimd
                        def _(g):
                            g.wait_ge(ch_v.h, ds_done)
                            ins = g.dma_start(
                                out=moe_dram[m * 128:(m + 1) * 128, tsl],
                                in_=p4_ds[:, m % 2, :],
                                accum_op=OP.add)
                            ch_g.tick(ins, 16)
                        slot_done(("p4ds", m % 2), ch_g)

        if debug:
            @block.gpsimd
            def _(g):
                ch_g.wait(g)
                ch_g.tick(g.dma_start(out=dbg["moe"], in_=moe_dram), 16)

        phase_barrier()
        for tq in range(4):
            hsl = slice(tq * 512, (tq + 1) * 512)

            for k in range(8):
                @block.sync
                def _(sync):
                    if tq == 0 and k == 0:
                        ch_cc.wait(sync)
                    ch_v.wait(sync)
                    ch_st.tick(sync.dma_start(
                        out=p5_xm[:, k, :],
                        in_=xn_dram[k * 128:(k + 1) * 128, hsl]), 16)
                    ins = sync.dma_start(out=p5_xt[:, :],
                                         in_=rs_out[k * 128:(k + 1) * 128, hsl])
                    ch_st.tick(ins, 16)

                @block.vector
                def _(v):
                    ch_st.wait(v)
                    ins = v.tensor_add(p5_xm[:, k, :], p5_xm[:, k, :],
                                       p5_xt[:, :])
                    ch_v.tick(ins)

            @block.scalar
            def _(s_):
                ch_v.wait(s_)
                for k in range(8):
                    ins = s_.activation(out=p5_sq[:, k, :], in_=p5_xm[:, k, :],
                                        func=AF.Square)
                ch_s.tick(ins)

            @block.tensor
            def _(t):
                ch_s.wait(t)
                ch_v.wait(t)
                for k in range(8):
                    t.matmul(pb[6][0:1, :], c_onec[:, 0:1], p5_xm[:, k, :],
                             start=(k == 0), stop=(k == 7))
                for k in range(8):
                    ins = t.matmul(pb[7][0:1, :], c_onec[:, 0:1], p5_sq[:, k, :],
                                   start=(k == 0), stop=(k == 7))
                ch_t.tick(ins)

            @block.vector
            def _(v):
                ch_t.wait(v)
                v.tensor_scalar(out=r_mu[:], in0=pb[6][0:1, :],
                                scalar1=1.0 / D, scalar2=None, op0=OP.mult)
                v.tensor_scalar(out=r_m2[:], in0=pb[7][0:1, :],
                                scalar1=1.0 / D, scalar2=EPS,
                                op0=OP.mult, op1=OP.add)
                vfence(v)
                v.tensor_mul(r_t[:], r_mu[:], r_mu[:])
                vfence(v)
                ch_v.tick(v.tensor_sub(r_m2[:], r_m2[:], r_t[:]))

            @block.scalar
            def _(s_):
                ch_v.wait(s_)
                ch_s.tick(s_.activation(out=r_sd[:], in_=r_m2[:], func=AF.Sqrt))

            @block.vector
            def _(v):
                ch_s.wait(v)
                ch_v.tick(v.reciprocal(r_sd[:], r_sd[:]))

            @block.tensor
            def _(t):
                ch_v.wait(t)
                t.matmul(pb[4][:, :], c_ones[:, :], r_mu[:, :],
                         start=True, stop=True)
                ins = t.matmul(pb[5][:, :], c_ones[:, :], r_sd[:, :],
                               start=True, stop=True)
                ch_t.tick(ins)

            @block.vector
            def _(v):
                ch_t.wait(v)
                for k in range(8):
                    v.tensor_sub(t_tmp[:, :], p5_xm[:, k, :], pb[4][:, :])
                    v.tensor_mul(t_tmp[:, :], t_tmp[:, :], pb[5][:, :])
                    ins = v.tensor_scalar(out=p5_h2[:, k, :], in0=t_tmp[:, :],
                                          scalar1=c_ln2g[:, k:k + 1],
                                          scalar2=c_ln2b[:, k:k + 1],
                                          op0=OP.mult, op1=OP.add)
                ch_v.tick(ins)

            h2_ready = ch_v.n
            w1_ready = {}
            for mg in range(8):
                wslot = mg % 2

                @block.gpsimd
                def _(g):
                    if tq == 0 and mg == 0:
                        ch_t.wait(g)
                        ch_v.wait(g)
                    slot_need(("p5w1", wslot), g)
                    g.dma_start(
                        out=p5_w1[:, wslot, :, 0:512],
                        in_=w_ff1[:, mg * 512:(mg + 1) * 512]
                        .rearrange("(k p) m -> p k m", p=128)).then_inc(ch_g.h, 16)
                    ch_g.n += 16
                    ins = g.dma_start(
                        out=p5_w1[:, wslot, :, 512:1024],
                        in_=w_ff1[:, FFI + mg * 512:FFI + (mg + 1) * 512]
                        .rearrange("(k p) m -> p k m", p=128))
                    ch_g.tick(ins, 16)
                w1_ready[mg] = ch_g.n

                for sm in range(4):
                    ab = sm % 2

                    @block.tensor
                    def _(t):
                        t.wait_ge(ch_g.h, w1_ready[mg])
                        if mg == 0 and sm == 0:
                            t.wait_ge(ch_v.h, h2_ready)
                        slot_need(("bank", 0 + 2 * ab), t)
                        slot_need(("bank", 1 + 2 * ab), t)
                        for k in range(8):
                            t.matmul(pb[0 + 2 * ab][:, :],
                                     p5_w1[:, wslot, k, sm * 128:(sm + 1) * 128],
                                     p5_h2[:, k, :],
                                     start=(k == 0), stop=(k == 7))
                        for k in range(8):
                            ins = t.matmul(
                                pb[1 + 2 * ab][:, :],
                                p5_w1[:, wslot, k,
                                      512 + sm * 128:512 + (sm + 1) * 128],
                                p5_h2[:, k, :],
                                start=(k == 0), stop=(k == 7))
                        ch_t.tick(ins)
                    if sm == 3:
                        slot_done(("p5w1", wslot), ch_t)
                    mm_done = ch_t.n

                    mi = mg * 4 + sm

                    @block.scalar
                    def _(s_):
                        s_.wait_ge(ch_t.h, mm_done)
                        slot_need(("p5g",), s_)
                        s_.activation(out=p5_ag[:, mi, :], in_=pb[0 + 2 * ab][:, :],
                                      func=AF.Identity,
                                      bias=c_bff1[:, mi:mi + 1])
                        ins = s_.activation(out=p5_g[:, :], in_=pb[1 + 2 * ab][:, :],
                                            func=AF.Gelu_apprx_tanh,
                                            bias=c_bff1[:, 32 + mi:33 + mi])
                        ch_s.tick(ins)
                    slot_done(("bank", 0 + 2 * ab), ch_s)
                    slot_done(("bank", 1 + 2 * ab), ch_s)
                    sg_done = ch_s.n

                    @block.vector
                    def _(v):
                        v.wait_ge(ch_s.h, sg_done)
                        ins = v.tensor_mul(p5_ag[:, mi, :], p5_ag[:, mi, :],
                                           p5_g[:, :])
                        ch_v.tick(ins)
                    slot_done(("p5g",), ch_v)

            ag_ready = ch_v.n
            for mp in range(2):
                @block.gpsimd
                def _(g):
                    if tq == 0 and mp == 0:
                        ch_t.wait(g)
                        ch_v.wait(g)
                    slot_need(("p5w2full",), g)
                    ins = g.dma_start(
                        out=p5_w2[:, :, :],
                        in_=w_ff2[:, mp * 512:(mp + 1) * 512]
                        .rearrange("(c p) m -> p c m", p=128))
                    ch_g.tick(ins, 16)
                w2r = ch_g.n

                for kc in range(32):
                    @block.tensor
                    def _(t):
                        if kc == 0:
                            t.wait_ge(ch_g.h, w2r)
                            t.wait_ge(ch_v.h, ag_ready)
                            for m4 in range(4):
                                slot_need(("bank", m4), t)
                        for m4 in range(4):
                            ins = t.matmul(
                                pb[m4][:, :],
                                p5_w2[:, kc, m4 * 128:(m4 + 1) * 128],
                                p5_ag[:, kc, :],
                                start=(kc == 0), stop=(kc == 31))
                        if kc == 31:
                            ch_t.tick(ins)

                ff2_done = ch_t.n
                slot_done(("p5w2full",), ch_t)

                @block.vector
                def _(v):
                    v.wait_ge(ch_t.h, ff2_done)
                    for m4 in range(4):
                        mi = mp * 4 + m4
                        ins = v.scalar_tensor_tensor(
                            out=p5_xm[:, mi, :], in0=pb[m4][:, :],
                            scalar=c_bff2[:, mi:mi + 1],
                            in1=p5_xm[:, mi, :], op0=OP.add, op1=OP.add)
                    ch_v.tick(ins)
                for m4 in range(4):
                    slot_done(("bank", m4), ch_v)

            if debug:
                @block.sync
                def _(sync):
                    ch_v.wait(sync)
                    for k in range(8):
                        ch_st.tick(sync.dma_start(
                            out=dbg["attn"][k * 128:(k + 1) * 128, hsl],
                            in_=p5_xm[:, k, :]), 16)

            for k in range(8):
                @block.sync
                def _(sync):
                    if tq == 0 and k == 0:
                        ch_g.wait(sync)
                    ch_v.wait(sync)
                    if debug:
                        ch_st.wait(sync)
                    ch_st.tick(sync.dma_start(
                        out=p5_xt[:, :], in_=x_sl[k * 128:(k + 1) * 128, hsl]),
                        16)
                    ins = sync.dma_start(
                        out=p5_mt[:, :], in_=moe_dram[k * 128:(k + 1) * 128, hsl])
                    ch_st.tick(ins, 16)

                @block.vector
                def _(v):
                    ch_st.wait(v)
                    v.tensor_add(p5_xt[:, :], p5_xm[:, k, :], p5_xt[:, :])
                    vfence(v)
                    ins = v.tensor_add(p5_mt[:, :], p5_mt[:, :], p5_xt[:, :])
                    ch_v.tick(ins)

                @block.sync
                def _(sync):
                    ch_v.wait(sync)
                    ins = sync.dma_start(
                        out=out_fm[k * 128:(k + 1) * 128, hsl], in_=p5_mt[:, :])
                    ch_st.tick(ins, 16)

        @block.sync
        def _(sync):
            ch_st.wait(sync)
            ch_g.wait(sync)
            ch_cc.wait(sync)

    return nc, dbg


def _prep_inputs(inputs):
    f32 = np.float32
    x = inputs["L0_attn_x"]
    w_in = inputs["L0_attn_w_in"]
    conv_w = inputs["L0_attn_conv_w"]
    conv_b = inputs["L0_attn_conv_b"]
    w_x = inputs["L0_attn_w_x"]
    w_dt = inputs["L0_attn_w_dt"]
    b_dt = inputs["L0_attn_b_dt"]
    a_log = inputs["L0_attn_a_log"]
    d_skip = inputs["L0_attn_d_skip"]
    w_out = inputs["L0_attn_w_out"]

    def chcol(v, q):
        return np.ascontiguousarray(
            v[q * CH:(q + 1) * CH].reshape(4, 128).T).astype(f32)

    def dcol(v):
        return np.ascontiguousarray(v.reshape(8, 128).T).astype(f32)

    rtw1 = np.ascontiguousarray(
        np.concatenate([inputs["L0_ffn_router_w"].T,
                        inputs["L0_ffn_actor_w1"].T], axis=1)).astype(f32)
    gate_wT = np.ascontiguousarray(
        inputs["L0_ffn_gate_w"].transpose(0, 2, 1)).astype(f32)
    up_wT = np.ascontiguousarray(
        inputs["L0_ffn_up_w"].transpose(0, 2, 1)).astype(f32)
    down_wT = np.ascontiguousarray(
        inputs["L0_ffn_down_w"].transpose(0, 2, 1)).astype(f32)

    shared = dict(
        ln1g=dcol(inputs["L0_attn_ln1_g"]),
        ln1b=dcol(inputs["L0_attn_ln1_b"]),
        ln2g=dcol(inputs["L0_attn_ln2_g"]),
        ln2b=dcol(inputs["L0_attn_ln2_b"]),
        w_ff1=np.ascontiguousarray(inputs["L0_attn_w_ff1"]).astype(f32),
        b_ff1=np.ascontiguousarray(
            inputs["L0_attn_b_ff1"].reshape(64, 128).T).astype(f32),
        w_ff2=np.ascontiguousarray(inputs["L0_attn_w_ff2"]).astype(f32),
        b_ff2=dcol(inputs["L0_attn_b_ff2"]),
        rtw1=rtw1,
        b1row=np.ascontiguousarray(
            inputs["L0_ffn_actor_b1"][None, :]).astype(f32),
        aw2=np.ascontiguousarray(inputs["L0_ffn_actor_w2"].T).astype(f32),
        b2row=np.ascontiguousarray(
            inputs["L0_ffn_actor_b2"][None, :]).astype(f32),
        gate_wT=gate_wT, up_wT=up_wT, down_wT=down_wT,
        ident=np.eye(128, dtype=f32),
        iota6=np.arange(6, dtype=f32)[None, :].copy(),
        rev8=np.arange(7, -1, -1, dtype=f32)[None, :].copy(),
        ones_r=np.ones((1, 128), dtype=f32),
        ones_c=np.ones((128, 1), dtype=f32),
    )

    in_maps = []
    for c in range(8):
        b, q = c // 4, c % 4
        m = dict(shared)
        m["x_sl"] = np.ascontiguousarray(
            x[b, q * TOK:(q + 1) * TOK, :].T).astype(f32)
        m["w_in_xc"] = np.ascontiguousarray(
            w_in[:, q * CH:(q + 1) * CH]).astype(f32)
        m["w_in_z"] = np.ascontiguousarray(
            w_in[:, DIN + q * CH:DIN + (q + 1) * CH]).astype(f32)
        m["conv_w"] = np.ascontiguousarray(
            conv_w[q * CH:(q + 1) * CH].reshape(4, 128, 4)
            .transpose(1, 0, 2).reshape(128, 16)).astype(f32)
        m["conv_b"] = chcol(conv_b, q)
        m["w_x_sl"] = np.ascontiguousarray(
            w_x[q * CH:(q + 1) * CH, :]).astype(f32)
        m["w_dt_sl"] = np.ascontiguousarray(
            w_dt[:, q * CH:(q + 1) * CH]).astype(f32)
        m["b_dt"] = chcol(b_dt, q)
        m["a_log"] = np.ascontiguousarray(
            a_log[q * CH:(q + 1) * CH].reshape(4, 128, 16)
            .transpose(1, 0, 2).reshape(128, 64)).astype(f32)
        m["d_skip"] = chcol(d_skip, q)
        m["w_out_sl"] = np.ascontiguousarray(
            w_out[q * CH:(q + 1) * CH, :]).astype(f32)
        in_maps.append(m)
    return in_maps


def kernel(**inputs):
    in_maps = _prep_inputs(inputs)
    nc, dbg = build_nc()
    res = run_bass_kernel_spmd(nc, in_maps, core_ids=list(range(8)))
    outs = []
    for b in range(B):
        parts = [res.results[b * 4 + q]["out_fm"] for q in range(4)]
        outs.append(np.concatenate(parts, axis=1).T)
    return np.stack(outs, axis=0).astype(np.float32)
